# version 14
# baseline (speedup 1.0000x reference)
import numpy as np
import ml_dtypes
from contextlib import ExitStack

import concourse.bass as bass
import concourse.mybir as mybir
from concourse.bass_utils import run_bass_kernel_spmd

dt = mybir.dt
OP = mybir.AluOpType
AF = mybir.ActivationFunctionType
AX = mybir.AxisListType
F32, BF16, F16, U16 = dt.float32, dt.bfloat16, dt.float16, dt.uint16
bf16 = ml_dtypes.bfloat16

D, S, NB = 2048, 4096, 4
NH, NKV, HD = 32, 8, 64
II = 5632
NCH = 16
TOWN = 2048
NSLOT, QS, KCH = 4, 512, 128
SPANS = (1024, 2048, 3072, 4096)
NMASK = 8
LNEPS, RMSEPS = 1e-5, 1e-6
S14, S28 = float(2.0 ** 14), float(2.0 ** 28)
NIC, NOB = 44, 22
VTW = 520
HB = 1024

ENGS = ("sync", "tensor", "vector", "scalar", "gpsimd")


class Sched:

    def __init__(self, nc, stack):
        self.nc = nc
        self.q = {e: [] for e in ENGS}
        self.sem = {}
        self.stack = stack
        for e in ENGS:
            h = stack.enter_context(nc.semaphore("sem_" + e))
            self.sem[e] = [h, 0]
        h = stack.enter_context(nc.semaphore("sem_dma"))
        self.sem["dma"] = [h, 0]
        h = stack.enter_context(nc.semaphore("sem_dmg"))
        self.sem["dmg"] = [h, 0]

    def step(self, eng, emit, waits=(), inc=None):
        wl = []
        for w in waits:
            if w is None:
                continue
            s, v = w
            if v > 0:
                wl.append((self.sem[s][0], v))
        tok = None
        ih = None
        if inc is not None:
            rec = self.sem[inc]
            rec[1] += 1
            ih, tok = rec[0], (inc, rec[1])
        self.q[eng].append((wl, emit, ih, 1))
        return tok

    def dma(self, eng, out, in_, waits=()):
        semn = "dmg" if eng == "gpsimd" else "dma"
        wl = []
        for w in waits:
            if w is None:
                continue
            s, v = w
            if v > 0:
                wl.append((self.sem[s][0], v))
        rec = self.sem[semn]
        rec[1] += 16
        self.q[eng].append(
            (wl, lambda E, o=out, i=in_: E.dma_start(out=o, in_=i), rec[0], 16))
        return (semn, rec[1])

    def flush(self):
        qs, self.q = self.q, {e: [] for e in ENGS}
        with self.nc.Block() as blk:
            for ename in ENGS:
                items = qs[ename]
                if not items:
                    continue

                def body(E, items=items):
                    for wl, emit, ih, iv in items:
                        for (h, v) in wl:
                            E.wait_ge(h, v)
                        inst = emit(E)
                        if ih is not None:
                            inst.then_inc(ih, iv)
                getattr(blk, ename)(body)


def _perm256(n):
    p = []
    for s0 in range(0, n, 256):
        p += list(range(s0, s0 + 256, 2)) + list(range(s0 + 1, s0 + 256, 2))
    return np.asarray(p, dtype=np.int64)


def _qcol_perm():
    e, o = [], []
    for j in range(8):
        for h in (j, j + 8, j + 16, j + 24):
            e += [64 * h + 2 * f for f in range(32)]
            o += [64 * h + 2 * f + 1 for f in range(32)]
    return np.asarray(e + o, dtype=np.int64)


def _kcol_perm():
    e, o = [], []
    for tile in ((0, 2, 4, 6), (1, 3, 5, 7)):
        for kv in tile:
            e += [64 * kv + 2 * f for f in range(32)]
            o += [64 * kv + 2 * f + 1 for f in range(32)]
    return np.asarray(e + o, dtype=np.int64)


def _wo_row_perm():
    rows = []
    for j in range(8):
        for h in (j, j + 8):
            rows += list(range(64 * h, 64 * h + 64))
        for h in (j + 16, j + 24):
            rows += list(range(64 * h, 64 * h + 64))
    return np.asarray(rows, dtype=np.int64)


def _kv_of(j, a):
    return (j // 4) + 2 * a


def build_program(upto=99, fparts=19):
    nc = bass.Bass("TRN2", target_bir_lowering=False)
    top = ExitStack()
    sc = Sched(nc, top)

    ap = {}
    for name, shape, dty, kind in [
        ("xT", [D, S], F32, "ExternalInput"),
        ("xqT", [D, TOWN], F32, "ExternalInput"),
        ("wq", [D, D], F32, "ExternalInput"),
        ("wk", [D, 512], F32, "ExternalInput"),
        ("wv", [D, 512], F32, "ExternalInput"),
        ("wo", [D, D], F32, "ExternalInput"),
        ("bgu", [1024, 2 * II], U16, "ExternalInput"),
        ("bdn", [2816, D], U16, "ExternalInput"),
        ("sgu", [1024, 2 * II], F16, "ExternalInput"),
        ("sdn", [2816, D], F16, "ExternalInput"),
        ("cosT", [128, S], BF16, "ExternalInput"),
        ("sinT", [128, S], BF16, "ExternalInput"),
        ("cosQ", [128, TOWN], BF16, "ExternalInput"),
        ("sinQ", [128, TOWN], BF16, "ExternalInput"),
        ("maskd", [NSLOT * NMASK * 128, QS], BF16, "ExternalInput"),
        ("identb", [128, 128], BF16, "ExternalInput"),
        ("onesr", [1, 128], F32, "ExternalInput"),
        ("onesc", [128, 1], BF16, "ExternalInput"),
        ("ind4", [128, 4], BF16, "ExternalInput"),
        ("outT", [D, TOWN], F32, "ExternalOutput"),
        ("x1d", [D, TOWN], F32, "Internal"),
    ]:
        ap[name] = nc.dram_tensor(name, shape, dty, kind=kind).ap()

    def sb(name, shape, dty, stk):
        return stk.enter_context(nc.sbuf_tensor(name, shape, dty))

    identb_t = sb("identb_t", [128, 128], BF16, top)
    epst = sb("epst", [128, 2], F32, top)
    onesr_t = sb("onesr_t", [1, 128], F32, top)
    onesc_t = sb("onesc_t", [128, 1], BF16, top)
    ind4_t = sb("ind4_t", [128, 4], BF16, top)
    negB = sb("negB", [128, 1], F32, top)
    kn2 = sb("kn2", [1, NKV], F32, top)
    ROWS = sb("ROWS", [128, S], F32, top)

    V = sc.step

    for nm, dst in (("identb", identb_t), ("onesr", onesr_t),
                    ("onesc", onesc_t), ("ind4", ind4_t)):
        tok = sc.dma("sync", dst[:], ap[nm][:])
    V("vector", lambda E: E.memset(kn2[:], 0.0), waits=[tok], inc="vector")
    V("vector", lambda E: E.memset(epst[:, 0:1], LNEPS))
    V("vector", lambda E: E.memset(epst[:, 1:2], RMSEPS))
    nc.const_aps.register(F32, LNEPS, epst[:, 0:1])
    nc.const_aps.register(F32, RMSEPS, epst[:, 1:2])
    sc.flush()

    with ExitStack() as ph:
        p1x = sb("p1x", [128, 2 * 2048], F32, ph)
        p1b = sb("p1b", [128, 2 * 2048], BF16, ph)
        p1s = sb("p1s", [128, 2 * 2048], BF16, ph)
        p1ps = ph.enter_context(nc.psum_tensor("p1ps", [1, 4096], F32))
        rounds = [(ap["xT"], 0, 0, 0), (ap["xT"], 2048, 0, 2048),
                  (ap["xqT"], 0, 64, 0)]
        w = 2048
        cp_t = [None] * 48
        sq_t = [None] * 48
        mm_t = [None] * 48
        ev_t = None
        i = 0
        for r, (src, c0, rbase, rcol) in enumerate(rounds):
            for ch in range(NCH):
                sl = i % 2
                xr = p1x[:, sl * 2048:(sl + 1) * 2048]
                xb = p1b[:, sl * 2048:(sl + 1) * 2048]
                xq2 = p1s[:, sl * 2048:(sl + 1) * 2048]
                d_t = sc.dma("sync", xr, src[128 * ch:128 * (ch + 1), c0:c0 + w],
                             waits=[cp_t[i - 2] if i >= 2 else None])
                cp_t[i] = V("vector", lambda E, o=xb, a=xr: E.tensor_copy(o, a),
                            waits=[d_t, sq_t[i - 2] if i >= 2 else None,
                                   mm_t[i - 2] if i >= 2 else None],
                            inc="vector")
                sq_t[i] = V("scalar", lambda E, o=xq2, a=xb:
                            E.activation(o, a, AF.Square),
                            waits=[cp_t[i], mm_t[i - 2] if i >= 2 else None],
                            inc="scalar")
                for g in range(4):
                    V("tensor", lambda E, o=p1ps[0:1, 1024 * g:1024 * g + 512],
                      rr=xb[:, 512 * g:512 * (g + 1)], st_=(ch == 0):
                      E.matmul(o, onesc_t[:], rr, start=st_, stop=(ch == 15)),
                      waits=([sq_t[i], ev_t] if g == 0 else []))
                    lastg = (g == 3)
                    mm_t[i] = V("tensor",
                                lambda E, o=p1ps[0:1, 1024 * g + 512:1024 * (g + 1)],
                                rr=xq2[:, 512 * g:512 * (g + 1)], st_=(ch == 0):
                                E.matmul(o, onesc_t[:], rr, start=st_,
                                         stop=(ch == 15)),
                                inc=("tensor" if lastg else None)) or mm_t[i]
                i += 1
            for g in range(4):
                V("vector", lambda E,
                  o=ROWS[rbase:rbase + 1, rcol + 512 * g:rcol + 512 * (g + 1)],
                  a=p1ps[0:1, 1024 * g:1024 * g + 512]:
                  E.tensor_copy(o, a), waits=([mm_t[i - 1]] if g == 0 else []))
                ev_t = V("vector", lambda E,
                         o=ROWS[rbase + 32:rbase + 33,
                                rcol + 512 * g:rcol + 512 * (g + 1)],
                         a=p1ps[0:1, 1024 * g + 512:1024 * (g + 1)]:
                         E.tensor_copy(o, a), inc="vector")
        sc.flush()
    if upto <= 1:
        top.close(); return nc

    att = ExitStack()
    RBq = sb("RBq", [128, TOWN], BF16, att)
    MBq = sb("MBq", [128, TOWN], BF16, att)
    Kt = sb("Kt", [128, 4 * S], BF16, att)
    VT = sb("VT", [128, (S // KCH) * VTW], BF16, att)
    ln1 = ExitStack()
    RB1 = sb("RB1", [128, S], BF16, ln1)
    MB1 = sb("MB1", [128, S], BF16, ln1)

    def ln_rowmath(base, w, tmp, eps, w0=None):
        last = None
        for c0 in range(0, w, 2048):
            cw = min(2048, w - c0)
            su = ROWS[base:base + 1, c0:c0 + cw]
            sq = ROWS[base + 32:base + 33, c0:c0 + cw]
            A = tmp[0:1, 0:cw]
            Bv = tmp[0:1, 2048:2048 + cw]
            C = tmp[0:1, 4096:4096 + cw]
            tk = V("vector", lambda E, a=A, s_=sq: E.tensor_copy(a, s_),
                   waits=[last, w0 if last is None else None], inc="vector")
            tk = V("vector", lambda E, b=Bv, s_=su: E.tensor_copy(b, s_),
                   waits=[tk], inc="vector")
            tk = V("vector", lambda E, a=A: E.tensor_scalar(a, a, 1.0 / D, None,
                                                            OP.mult),
                   waits=[tk], inc="vector")
            tk = V("vector", lambda E, b=Bv: E.tensor_scalar(b, b, 1.0 / D, None,
                                                             OP.mult),
                   waits=[tk], inc="vector")
            tk = V("vector", lambda E, c=C, b=Bv: E.tensor_tensor(c, b, b,
                                                                  OP.mult),
                   waits=[tk], inc="vector")
            t1 = V("vector", lambda E, a=A, c=C:
                   E.tensor_tensor(a, a, c, OP.subtract), waits=[tk],
                   inc="vector")
            t2 = V("scalar", lambda E, c=C, a=A:
                   E.activation(c, a, AF.Sqrt, bias=eps), waits=[t1],
                   inc="scalar")
            tk = V("vector", lambda E, a=A, c=C: E.reciprocal(a, c),
                   waits=[t2], inc="vector")
            tk = V("vector", lambda E, s_=su, a=A: E.tensor_copy(s_, a),
                   waits=[tk], inc="vector")
            tk = V("vector", lambda E, c=C, b=Bv: E.tensor_scalar(c, b, -1.0,
                                                                  None,
                                                                  OP.mult),
                   waits=[tk], inc="vector")
            last = V("vector", lambda E, s_=sq, c=C: E.tensor_copy(s_, c),
                     waits=[tk], inc="vector")
        return last

    def mm_bcast(rsl, w, dest, wt, slots, brow, free0=None):
        n = w // 512
        ev = [free0, free0, free0]
        mm = [None] * (n + 2)
        last = None
        for i in range(n + 2):
            sl = i % 3
            bsl = i % 2
            if i < n:
                cp = V("vector", lambda E,
                       o=brow[0:1, 512 * bsl:512 * (bsl + 1)],
                       a=rsl(512 * i): E.tensor_copy(o, a),
                       waits=[wt if i == 0 else None,
                              free0 if i == 0 else None,
                              mm[i - 2] if i >= 2 else None, last],
                       inc="vector")
            else:
                cp = None
            mm[i] = V("tensor", lambda E, o=slots[sl],
                      r=brow[0:1, 512 * bsl:512 * (bsl + 1)]:
                      E.matmul(o, onesr_t[:], r, start=True, stop=True),
                      waits=[cp, ev[sl]], inc="tensor")
            if i >= 2:
                j = i - 2
                jsl = j % 3
                last = V("vector", lambda E, o=dest[:, 512 * j:512 * (j + 1)],
                         a=slots[jsl]:
                         E.tensor_copy(o, a), waits=[mm[i], last],
                         inc="vector")
                ev[jsl] = last
        return last

    bc_t = []
    with ExitStack() as ph:
        tmpr = sb("tmpr", [1, 3 * 2048], F32, ph)
        brow = sb("brow", [1, 1024], F32, ph)
        psBC = ph.enter_context(nc.psum_tensor("psBC", [128, 1536], F32))
        wt = ln_rowmath(0, S, tmpr, LNEPS, w0=ev_t)
        bsl3 = [psBC[:, 0:512], psBC[:, 512:1024], psBC[:, 1024:1536]]
        bc_t.append(mm_bcast(lambda c0: ROWS[0:1, c0:c0 + 512], S, RB1, wt,
                             bsl3, brow))
        bc_t.append(mm_bcast(lambda c0: ROWS[32:33, c0:c0 + 512], S, MB1,
                             bc_t[-1], bsl3, brow, free0=bc_t[-1]))
        wt = ln_rowmath(64, TOWN, tmpr, LNEPS, w0=ev_t)
        bc_t.append(mm_bcast(lambda c0: ROWS[64:65, c0:c0 + 512], TOWN, RBq,
                             wt, bsl3, brow, free0=bc_t[-1]))
        bc_t.append(mm_bcast(lambda c0: ROWS[96:97, c0:c0 + 512], TOWN, MBq,
                             bc_t[-1], bsl3, brow, free0=bc_t[-1]))
        sc.flush()
    if upto <= 2:
        ln1.close(); att.close(); top.close(); return nc

    with ExitStack() as ph:
        cosK = sb("cosK", [128, S], BF16, ph)
        sinK = sb("sinK", [128, S], BF16, ph)
        wkv_f = sb("wkv_f", [128, 4 * 512], F32, ph)
        wk_b = sb("wk_b", [128, 16 * 512], BF16, ph)
        wv_b = sb("wv_b", [128, 16 * 512], BF16, ph)
        xs_f = sb("xs_f", [128, 2 * 512], F32, ph)
        xns = sb("xns", [128, 2 * 16 * 512], BF16, ph)
        rsc = sb("rsc", [128, 2 * 512], F32, ph)
        vsb = sb("vsb", [128, 512], BF16, ph)
        ksq = sb("ksq", [128, 2 * 512], BF16, ph)
        psE = ph.enter_context(nc.psum_tensor("psE", [128, 512], F32))
        psO = ph.enter_context(nc.psum_tensor("psO", [128, 512], F32))
        psV = ph.enter_context(nc.psum_tensor("psV", [128, 512], F32))
        psT = ph.enter_context(nc.psum_tensor("psT", [128, 128], BF16))
        psN2 = ph.enter_context(nc.psum_tensor("psN2", [1, 2048], F32))

        vt0 = V("vector", lambda E: E.memset(VT[:], 1.0), inc="vector")
        sc.dma("sync", cosK[:], ap["cosT"][:])
        sc.dma("sync", sinK[:], ap["sinT"][:])
        cw2 = None
        for wsrc, wdst_b in ((ap["wk"], wk_b), (ap["wv"], wv_b)):
            wr = wsrc.rearrange("(c p) o -> p c o", p=128)
            for rd in range(4):
                d_t = sc.dma("sync",
                             wkv_f[:].rearrange("p (c o) -> p c o", o=512),
                             wr[:, 4 * rd:4 * (rd + 1), :], waits=[cw2])
                cw2 = V("scalar", lambda E,
                        o=wdst_b[:, 2048 * rd:2048 * (rd + 1)], a=wkv_f[:]:
                        E.copy(o, a), waits=[d_t], inc="scalar")

        xn_cons = [None, None]
        rp_prev = None
        tr_last = None
        mmK_t = None
        rope_t = [None, None]
        vtr_t = None
        dma_x = [None] * (8 * NCH)
        for slp in range(8):
            sl = slp
            xsl = slp % 2
            xn = xns[:, xsl * 8192:(xsl + 1) * 8192]
            c0 = 512 * sl
            last_apply = None
            for ch in range(NCH):
                xr = xs_f[:, (ch % 2) * 512:(ch % 2 + 1) * 512]
                ii = slp * NCH + ch
                d_t = sc.dma("sync", xr,
                             ap["xT"][128 * ch:128 * (ch + 1), c0:c0 + 512],
                             waits=[dma_x[ii - 2] if ii >= 2 else bc_t[1]])
                tt1 = V("vector", lambda E, o=rsc[:, 0:512], a=xr,
                        b=MB1[:, c0:c0 + 512]: E.tensor_tensor(o, a, b, OP.add),
                        waits=[d_t, xn_cons[xsl] if ch == 0 else None,
                               bc_t[1] if slp == 0 and ch == 0 else None,
                               rp_prev if ch == 0 else None,
                               last_apply],
                        inc="vector")
                dma_x[ii] = tt1
                last_apply = V("vector",
                               lambda E, o=xn[:, 512 * ch:512 * (ch + 1)],
                               a=rsc[:, 0:512], b=RB1[:, c0:c0 + 512]:
                               E.tensor_tensor(o, a, b, OP.mult),
                               waits=[tt1], inc="vector")
            rp = None
            for pair in range(2):
                for part, ps in ((0, psE), (1, psO)):
                    for ch in range(NCH):
                        co = 512 * ch + 128 * (pair + 2 * part)
                        tok = V("tensor", lambda E, o=ps[:],
                                l=wk_b[:, co:co + 128],
                                r=xn[:, 512 * ch:512 * (ch + 1)], st_=(ch == 0):
                                E.matmul(o, l, r, start=st_, stop=(ch == 15)),
                                waits=([last_apply, cw2, rp] if ch == 0 else []),
                                inc=("tensor" if ch == 15 else None))
                        if ch == 15:
                            mmK_t = tok
                csl = cosK[:, c0:c0 + 512]
                ssl = sinK[:, c0:c0 + 512]
                ke = Kt[:, pair * S + c0:pair * S + c0 + 512]
                ko = Kt[:, (pair + 2) * S + c0:(pair + 2) * S + c0 + 512]
                k1 = V("vector", lambda E, o=rsc[:, 0:512], a=psE[:], b=csl:
                       E.tensor_tensor(o, a, b, OP.mult), waits=[mmK_t, last_apply],
                       inc="vector")
                k2 = V("vector", lambda E, o=rsc[:, 512:1024], a=psO[:], b=ssl:
                       E.tensor_tensor(o, a, b, OP.mult), inc="vector")
                k3 = V("vector", lambda E, o=ke, a=rsc[:, 0:512], b=rsc[:, 512:1024]:
                       E.tensor_tensor(o, a, b, OP.subtract), waits=[k1, k2],
                       inc="vector")
                k4 = V("vector", lambda E, o=rsc[:, 0:512], a=psE[:], b=ssl:
                       E.tensor_tensor(o, a, b, OP.mult), waits=[k3], inc="vector")
                k5 = V("vector", lambda E, o=rsc[:, 512:1024], a=psO[:], b=csl:
                       E.tensor_tensor(o, a, b, OP.mult), inc="vector")
                rp = V("vector", lambda E, o=ko, a=rsc[:, 0:512],
                       b=rsc[:, 512:1024]: E.tensor_tensor(o, a, b, OP.add),
                       waits=[k4, k5], inc="vector")
            rope_t[xsl] = rp
            rp_prev = rp
            for vt in range(4):
                vslot = vt % 2
                pv = psV[:, 0:512]
                for ch in range(NCH):
                    co = 512 * ch + 128 * vt
                    tok = V("tensor", lambda E, o=pv, l=wv_b[:, co:co + 128],
                            r=xn[:, 512 * ch:512 * (ch + 1)], st_=(ch == 0):
                            E.matmul(o, l, r, start=st_, stop=(ch == 15)),
                            waits=([cw2, rp, vtr_t] if ch == 0 else []),
                            inc=("tensor" if ch == 15 else None))
                    if ch == 15:
                        mmV_t = tok
                        xn_cons[xsl] = tok
                vs = vsb[:, 0:512]
                ev = V("scalar", lambda E, o=vs, a=pv: E.copy(o, a),
                       waits=[mmV_t, tr_last], inc="scalar")
                for kc in range(4):
                    cglob = 4 * sl + kc
                    tr = V("tensor", lambda E, o=psT[:],
                           a=vs[:, 128 * kc:128 * (kc + 1)]:
                           E.transpose(o, a, identb_t[:]),
                           waits=[ev, vt0, vtr_t], inc="tensor")
                    tr_last = tr
                    for half in range(2):
                        kv = 2 * vt + half
                        vtr_t = V("scalar", lambda E,
                                  o=VT[:, cglob * VTW + kv * 65:
                                       cglob * VTW + kv * 65 + 64],
                                  a=psT[:, 64 * half:64 * (half + 1)]:
                                  E.copy(o, a), waits=[tr],
                                  inc="scalar")
        kq_t = None
        for pair in range(2):
            sq_ts = []
            mmn = None
            for g in range(8):
                s1 = V("scalar", lambda E, o=ksq[:, 0:512],
                       a=Kt[:, pair * S + 512 * g:pair * S + 512 * (g + 1)]:
                       E.activation(o, a, AF.Square),
                       waits=[rope_t[0], rope_t[1], mmn] if g == 0 else [mmn],
                       inc="scalar")
                s2 = V("scalar", lambda E, o=ksq[:, 512:1024],
                       a=Kt[:, (pair + 2) * S + 512 * g:(pair + 2) * S + 512 * (g + 1)]:
                       E.activation(o, a, AF.Square), inc="scalar")
                for a_ in range(4):
                    V("tensor", lambda E, o=psN2[0:1, 512 * a_:512 * (a_ + 1)],
                      rr=ksq[:, 0:512], ia=a_, st_=(g == 0):
                      E.matmul(o, ind4_t[:, ia:ia + 1], rr, start=st_,
                               stop=False),
                      waits=([s2, kq_t] if (g == 0 and a_ == 0) else []))
                    mmn = V("tensor", lambda E, o=psN2[0:1, 512 * a_:512 * (a_ + 1)],
                            rr=ksq[:, 512:1024], ia=a_:
                            E.matmul(o, ind4_t[:, ia:ia + 1], rr, start=False,
                                     stop=(g == 7)),
                            inc=("tensor" if (g == 7 and a_ == 3) else None)) or mmn
            for a_ in range(4):
                kv = pair + 2 * a_
                kq_t = V("vector", lambda E, o=kn2[0:1, kv:kv + 1],
                         a=psN2[0:1, 512 * a_:512 * (a_ + 1)]:
                         E.reduce_max(o, a, AX.X),
                         waits=([mmn] if a_ == 0 else []),
                         inc=("vector" if a_ == 3 else None)) or kq_t
        sc.flush()
    ln1.close()
    if upto <= 3:
        att.close(); top.close(); return nc

    Qt = sb("Qt", [128, 16 * QS], BF16, att)
    aot = sb("aot", [128, 16 * QS], BF16, att)
    cosQ_t = sb("cosQ_t", [128, TOWN], BF16, att)
    sinQ_t = sb("sinQ_t", [128, TOWN], BF16, att)
    sc.dma("sync", cosQ_t[:], ap["cosQ"][:])
    sc.dma("sync", sinQ_t[:], ap["sinQ"][:])

    with ExitStack() as ph:
        dscr4 = sb("dscr4", [128, 16], F32, ph)
        xnq = sb("xnq", [128, 16 * 512], BF16, ph)
        wpf = sb("wpf", [128, 16 * 128], F32, ph)
        wpb = sb("wpb", [128, 2 * 2048], BF16, ph)
        xqf = sb("xqf", [128, 2 * 512], F32, ph)
        qsq = sb("qsq", [128, 2 * 512], BF16, ph)
        PT = sb("PT", [128, 2 * 2048], BF16, ph)
        SCR = sb("SCR", [128, 2048], F32, ph)
        mskt = sb("mskt", [128, NMASK * 512], BF16, ph)
        x1t = sb("x1t", [128, 2 * 512], F32, ph)
        x1b = sb("x1b", [128, 512], BF16, ph)
        x1s = sb("x1s", [128, 512], BF16, ph)
        rtmp = sb("rtmp", [1, 2048], F32, ph)
        psA = ph.enter_context(nc.psum_tensor("psA", [128, 2048], F32))
        psB = ph.enter_context(nc.psum_tensor("psB", [128, 2048], F32))

        kq_wait = kq_t
        psA_free = None
        psB_free = None
        wpb_free = [None, None]
        wpf_free = None
        xq_free = [None, None]
        scr_free = None
        x1_free = [None, None]
        pw_free = [None, None]
        msk_free = None
        x1_store_t = None
        for slot in range(NSLOT):
            q0 = QS * slot
            nkc = SPANS[slot] // KCH
            lastq = None
            dq_tok = [None] * (NCH + 1)
            dq_tok[0] = sc.dma("sync", xqf[:, 0:512],
                               ap["xqT"][0:128, q0:q0 + 512],
                               waits=[xq_free[0], bc_t[3]])
            for ch in range(NCH):
                xr = xqf[:, (ch % 2) * 512:(ch % 2 + 1) * 512]
                if ch + 1 < NCH:
                    dq_tok[ch + 1] = sc.dma(
                        "sync",
                        xqf[:, ((ch + 1) % 2) * 512:((ch + 1) % 2 + 1) * 512],
                        ap["xqT"][128 * (ch + 1):128 * (ch + 2), q0:q0 + 512],
                        waits=[xq_free[(ch + 1) % 2]])
                else:
                    dq_tok[ch + 1] = sc.dma("sync", dscr4[:],
                                            ap["xqT"][0:128, 0:16])
                tt1 = V("vector", lambda E, o=SCR[:, 0:512], a=xr,
                        b=MBq[:, q0:q0 + 512]: E.tensor_tensor(o, a, b, OP.add),
                        waits=[dq_tok[ch], dq_tok[ch + 1],
                               scr_free if ch == 0 else None, lastq],
                        inc="vector")
                xq_free[ch % 2] = tt1
                lastq = V("vector", lambda E, o=xnq[:, 512 * ch:512 * (ch + 1)],
                          a=SCR[:, 0:512], b=RBq[:, q0:q0 + 512]:
                          E.tensor_tensor(o, a, b, OP.mult), waits=[tt1],
                          inc="vector")
            rope_done = None
            for j in range(8):
                ws = j % 2
                mmq = [None, None]
                for part, coff, ps in ((0, 128 * j, psA), (1, 1024 + 128 * j, psB)):
                    d_t = sc.dma("sync",
                                 wpf[:].rearrange("p (c o) -> p c o", o=128),
                                 ap["wq"][:, coff:coff + 128]
                                 .rearrange("(c p) o -> p c o", p=128),
                                 waits=[wpb_free[ws], wpf_free])
                    cv = V("scalar", lambda E,
                           o=wpb[:, ws * 2048:(ws + 1) * 2048], a=wpf[:]:
                           E.copy(o, a), waits=[d_t], inc="scalar")
                    wpf_free = cv
                    for ch in range(NCH):
                        mmq[part] = V("tensor", lambda E, o=ps[:, 0:512],
                                      l=wpb[:, ws * 2048 + 128 * ch:
                                            ws * 2048 + 128 * (ch + 1)],
                                      r=xnq[:, 512 * ch:512 * (ch + 1)],
                                      st_=(ch == 0):
                                      E.matmul(o, l, r, start=st_, stop=(ch == 15)),
                                      waits=([cv, lastq,
                                              psA_free if part == 0 else psB_free]
                                             if ch == 0 else []),
                                      inc=("tensor" if ch == 15 else None)) \
                            or mmq[part]
                    wpb_free[ws] = mmq[part]
                csl = cosQ_t[:, q0:q0 + 512]
                ssl = sinQ_t[:, q0:q0 + 512]
                qe = Qt[:, 512 * j:512 * (j + 1)]
                qo = Qt[:, 512 * (j + 8):512 * (j + 9)]
                q1 = V("vector", lambda E, o=SCR[:, 0:512], a=psA[:, 0:512], b=csl:
                       E.tensor_tensor(o, a, b, OP.mult), waits=[mmq[1], lastq],
                       inc="vector")
                q2 = V("vector", lambda E, o=SCR[:, 512:1024], a=psB[:, 0:512], b=ssl:
                       E.tensor_tensor(o, a, b, OP.mult), inc="vector")
                q3 = V("vector", lambda E, o=qe, a=SCR[:, 0:512], b=SCR[:, 512:1024]:
                       E.tensor_tensor(o, a, b, OP.subtract), waits=[q1, q2],
                       inc="vector")
                q4 = V("vector", lambda E, o=SCR[:, 0:512], a=psA[:, 0:512], b=ssl:
                       E.tensor_tensor(o, a, b, OP.mult), waits=[q3], inc="vector")
                q5 = V("vector", lambda E, o=SCR[:, 512:1024], a=psB[:, 0:512], b=csl:
                       E.tensor_tensor(o, a, b, OP.mult), inc="vector")
                rope_done = V("vector", lambda E, o=qo, a=SCR[:, 0:512],
                              b=SCR[:, 512:1024]:
                              E.tensor_tensor(o, a, b, OP.add), waits=[q4, q5],
                              inc="vector")
                psA_free = rope_done
                psB_free = rope_done
            scr_free = rope_done
            qmx = None
            mmn = None
            for j in range(8):
                s1 = V("scalar", lambda E, o=qsq[:, 0:512],
                       a=Qt[:, 512 * j:512 * (j + 1)]:
                       E.activation(o, a, AF.Square),
                       waits=[rope_done, mmn] if j == 0 else [mmn],
                       inc="scalar")
                s2 = V("scalar", lambda E, o=qsq[:, 512:1024],
                       a=Qt[:, 512 * (j + 8):512 * (j + 9)]:
                       E.activation(o, a, AF.Square), inc="scalar")
                for a_ in range(4):
                    V("tensor", lambda E, o=psB[0:1, 512 * a_:512 * (a_ + 1)],
                      rr=qsq[:, 0:512], ia=a_:
                      E.matmul(o, ind4_t[:, ia:ia + 1], rr, start=True,
                               stop=False),
                      waits=([s2, psB_free] if a_ == 0 else []))
                    mmn = V("tensor", lambda E,
                            o=psB[0:1, 512 * a_:512 * (a_ + 1)],
                            rr=qsq[:, 512:1024], ia=a_:
                            E.matmul(o, ind4_t[:, ia:ia + 1], rr, start=False,
                                     stop=True),
                            inc=("tensor" if a_ == 3 else None)) or mmn
                for a_ in range(4):
                    h = j + 8 * a_
                    qmx = V("vector", lambda E, o=rtmp[0:1, 32 + h:33 + h],
                            a=psB[0:1, 512 * a_:512 * (a_ + 1)]:
                            E.reduce_max(o, a, AX.X),
                            waits=([mmn] if a_ == 0 else []),
                            inc=("vector" if a_ == 3 else None)) or qmx
                psB_free = qmx
            tr1 = V("vector", lambda E: E.reduce_max(rtmp[0:1, 64:65],
                                                     rtmp[0:1, 32:64], AX.X),
                    waits=[qmx, kq_wait], inc="vector")
            tr2 = V("vector", lambda E: E.reduce_max(rtmp[0:1, 65:66],
                                                     kn2[0:1, :], AX.X),
                    inc="vector")
            t1 = V("vector", lambda E: E.tensor_tensor(
                rtmp[0:1, 66:67], rtmp[0:1, 64:65], rtmp[0:1, 65:66], OP.mult),
                waits=[tr1, tr2], inc="vector")
            t2 = V("scalar", lambda E: E.activation(rtmp[0:1, 67:68],
                                                    rtmp[0:1, 66:67], AF.Sqrt),
                   waits=[t1], inc="scalar")
            t3 = V("scalar", lambda E: E.mul(rtmp[0:1, 68:69], rtmp[0:1, 67:68],
                                             -0.125), waits=[t2], inc="scalar")
            t4 = V("tensor", lambda E: E.matmul(psB[:, 0:1], onesr_t[:],
                                                rtmp[0:1, 68:69], start=True,
                                                stop=True),
                   waits=[t3, psB_free], inc="tensor")
            t5 = V("tensor", lambda E: E.matmul(psB[0:1, 512:1024],
                                                onesr_t[0:1, 0:1],
                                                rtmp[0:1, 0:512], start=True,
                                                stop=True),
                   waits=[t4], inc="tensor")
            t5b = V("tensor", lambda E: E.matmul(psB[0:1, 512:1024],
                                                 onesr_t[0:1, 0:1],
                                                 rtmp[0:1, 0:512], start=True,
                                                 stop=True),
                    waits=[t5], inc="tensor")
            nb_t = V("vector", lambda E: E.tensor_copy(negB[:], psB[:, 0:1]),
                     waits=[t5b], inc="vector")
            psB_free = nb_t
            mt = sc.dma("sync",
                        mskt[:].rearrange("p (m q) -> p m q", q=QS),
                        ap["maskd"][slot * NMASK * 128:(slot + 1) * NMASK * 128, :]
                        .rearrange("(m p) q -> p m q", p=128),
                        waits=[msk_free])
            ao_t = None
            exp_b = [psA_free] * 4
            pv_b = [[None] * 4 for _ in range(2)]
            pdep_b = [None] * 4
            for j in range(8):
                et = j // 4
                pvmm = None
                for c in range(nkc):
                    pslot = c % 2
                    pt = PT[:, pslot * 2048:(pslot + 1) * 2048]
                    qk_b = [None] * 4
                    for a_ in range(4):
                        V("tensor", lambda E,
                          l=Kt[32 * a_:32 * (a_ + 1),
                               et * S + KCH * c:et * S + KCH * (c + 1)],
                          r=Qt[32 * a_:32 * (a_ + 1), 512 * j:512 * (j + 1)],
                          o=psA[:, 512 * a_:512 * (a_ + 1)], tp=(32 * a_, 0):
                          E.matmul(o, l, r, start=True, stop=False,
                                   tile_position=tp),
                          waits=[exp_b[a_]])
                        qk_b[a_] = V("tensor", lambda E,
                                     l=Kt[32 * a_:32 * (a_ + 1),
                                          (et + 2) * S + KCH * c:
                                          (et + 2) * S + KCH * (c + 1)],
                                     r=Qt[32 * a_:32 * (a_ + 1),
                                          512 * (j + 8):512 * (j + 9)],
                                     o=psA[:, 512 * a_:512 * (a_ + 1)],
                                     tp=(32 * a_, 0):
                                     E.matmul(o, l, r, start=False, stop=True,
                                              tile_position=tp), inc="tensor")
                    for a_ in range(4):
                        exp_b[a_] = V("scalar", lambda E,
                                      o=pt[:, 512 * a_:512 * (a_ + 1)],
                                      a=psA[:, 512 * a_:512 * (a_ + 1)]:
                                      E.activation(o, a, AF.Exp, bias=negB[:],
                                                   scale=0.125),
                                      waits=[qk_b[a_],
                                             nb_t if (c == 0 and a_ == 0)
                                             else None,
                                             pv_b[pslot][a_]], inc="scalar")
                    if c >= nkc - NMASK:
                        mi = c - (nkc - NMASK)
                        for a_ in range(4):
                            pdep_b[a_] = V("vector", lambda E,
                                           o=pt[:, 512 * a_:512 * (a_ + 1)],
                                           m=mskt[:, 512 * mi:512 * (mi + 1)]:
                                           E.tensor_tensor(o, o, m, OP.mult),
                                           waits=[exp_b[a_],
                                                  mt if (mi == 0 and a_ == 0)
                                                  else None],
                                           inc="vector")
                    else:
                        for a_ in range(4):
                            pdep_b[a_] = exp_b[a_]
                    for a_ in range(4):
                        kv = _kv_of(j, a_)
                        pvmm = V("tensor", lambda E,
                                 l=VT[:, c * VTW + kv * 65:c * VTW + kv * 65 + 65],
                                 r=pt[:, 512 * a_:512 * (a_ + 1)],
                                 o=psB[0:65, 512 * a_:512 * (a_ + 1)],
                                 st_=(c == 0):
                                 E.matmul(o, l, r, start=st_, stop=(c == nkc - 1)),
                                 waits=[pdep_b[a_],
                                        psB_free if (c == 0 and a_ == 0)
                                        else None],
                                 inc="tensor")
                        pv_b[pslot][a_] = pvmm
                dn = V("scalar", lambda E: E.copy(rtmp[0:1, :],
                                              psB[64:65, :]),
                       waits=[pvmm], inc="scalar")
                rc = V("vector", lambda E: E.reciprocal(rtmp[0:1, :],
                                                        rtmp[0:1, :]),
                       waits=[dn], inc="vector")
                bcm = None
                for g in range(4):
                    bcm = V("tensor", lambda E,
                            o=psA[:, 512 * g:512 * (g + 1)],
                            rr=rtmp[0:1, 512 * g:512 * (g + 1)]:
                            E.matmul(o, onesr_t[:], rr, start=True, stop=True),
                            waits=[rc if g == 0 else None, exp_b[g]],
                            inc=("tensor" if g == 3 else None)) or bcm
                ev = V("vector", lambda E: E.tensor_copy(SCR[:], psA[:]),
                       waits=[bcm, scr_free], inc="vector")
                psA_free = ev
                exp_b = [ev] * 4
                for a_ in range(4):
                    t = 2 * j + (1 if a_ >= 2 else 0)
                    rh = 64 * (a_ % 2)
                    ao_t = V("vector", lambda E,
                             o=aot[rh:rh + 64, 512 * t:512 * t + 512],
                             a=psB[0:64, 512 * a_:512 * (a_ + 1)],
                             b=SCR[0:64, 512 * a_:512 * (a_ + 1)]:
                             E.tensor_tensor(o, a, b, OP.mult),
                             waits=([ev] if a_ == 0 else []),
                             inc=("vector" if a_ == 3 else None)) or ao_t
                psB_free = ao_t
                scr_free = ao_t
            msk_free = ao_t
            st_done = None
            pend_wo = None

            def emit_wo(oc, ws, pw, mmw_tok, d2, cover, d2n):
                nonlocal st_done, x1_store_t
                xr = xqf[:, ws * 512:(ws + 1) * 512]
                xo = x1t[:, ws * 512:(ws + 1) * 512]
                ad = V("vector", lambda E, o=xo, a=pw, b=xr:
                       E.tensor_tensor(o, a, b, OP.add),
                       waits=[mmw_tok, cover, d2, d2n, x1_free[ws]],
                       inc="vector")
                xq_free[ws] = ad
                pw_free[ws] = ad
                s_t = sc.dma("gpsimd",
                             ap["x1d"][128 * oc:128 * (oc + 1), q0:q0 + 512],
                             xo, waits=[ad])
                x1_store_t = s_t
                cb = V("vector", lambda E, o=x1b[:], a=xo: E.tensor_copy(o, a),
                       waits=[st_done, ad], inc="vector")
                x1_free[ws] = s_t
                sq2 = V("scalar", lambda E, o=x1s[:], a=x1b[:]:
                        E.activation(o, a, AF.Square), waits=[cb],
                        inc="scalar")
                V("tensor", lambda E, o=psB[0:1, 1024:1536], rr=x1b[:],
                  st_=(oc == 0): E.matmul(o, onesc_t[:], rr, start=st_,
                                          stop=(oc == 15)),
                  waits=[sq2])
                st_done = V("tensor", lambda E, o=psB[0:1, 1536:2048],
                            rr=x1s[:], st_=(oc == 0):
                            E.matmul(o, onesc_t[:], rr, start=st_,
                                     stop=(oc == 15)),
                            inc=("tensor" if oc == 15 else None)) or st_done

            pend_ad = None
            for oc in range(NCH):
                ws = oc % 2
                d_t = sc.dma("sync",
                             wpf[:].rearrange("p (c o) -> p c o", o=128),
                             ap["wo"][:, 128 * oc:128 * (oc + 1)]
                             .rearrange("(c p) o -> p c o", p=128),
                             waits=[wpb_free[ws], wpf_free])
                cv = V("scalar", lambda E, o=wpb[:, ws * 2048:(ws + 1) * 2048],
                       a=wpf[:]: E.copy(o, a), waits=[d_t], inc="scalar")
                wpf_free = cv
                pw = psA[:, 512 * ws:512 * (ws + 1)]
                mmw = None
                for t in range(16):
                    mmw = V("tensor", lambda E, o=pw,
                            l=wpb[:, ws * 2048 + 128 * t:ws * 2048 + 128 * (t + 1)],
                            r=aot[:, 512 * t:512 * (t + 1)], st_=(t == 0):
                            E.matmul(o, l, r, start=st_, stop=(t == 15)),
                            waits=([cv, ao_t,
                                    psA_free if oc < 2 else pw_free[ws]]
                                   if t == 0 else []),
                            inc=("tensor" if t == 15 else None)) or mmw
                wpb_free[ws] = mmw
                d2 = sc.dma("sync", xqf[:, ws * 512:(ws + 1) * 512],
                            ap["xqT"][128 * oc:128 * (oc + 1), q0:q0 + 512],
                            waits=[xq_free[ws]])
                if pend_wo is not None:
                    pend_wo(mmw, d2)
                pend_wo = (lambda cover, d2n, oc=oc, ws=ws, pw=pw, mmw=mmw,
                           d2=d2: emit_wo(oc, ws, pw, mmw, d2, cover, d2n))
            dumW = V("tensor", lambda E: E.matmul(
                psA[0:1, 1024:1536], onesr_t[0:1, 0:1], rtmp[0:1, 0:512],
                start=True, stop=True),
                waits=[wpb_free[1]], inc="tensor")
            dumWb = V("tensor", lambda E: E.matmul(
                psA[0:1, 1024:1536], onesr_t[0:1, 0:1], rtmp[0:1, 0:512],
                start=True, stop=True),
                waits=[dumW], inc="tensor")
            d2echo = sc.dma("sync", dscr4[:], ap["xqT"][0:128, 0:16])
            pend_wo(dumWb, d2echo)
            pend_wo = None
            dum3 = V("tensor", lambda E: E.matmul(
                psA[0:1, 1024:1536], onesr_t[0:1, 0:1], rtmp[0:1, 0:512],
                start=True, stop=True),
                waits=[st_done, psA_free], inc="tensor")
            dum3b = V("tensor", lambda E: E.matmul(
                psA[0:1, 1024:1536], onesr_t[0:1, 0:1], rtmp[0:1, 0:512],
                start=True, stop=True),
                waits=[dum3], inc="tensor")
            e1 = V("vector", lambda E, o=ROWS[0:1, q0:q0 + 512],
                   a=psB[0:1, 1024:1536]: E.tensor_copy(o, a),
                   waits=[st_done, dum3b])
            e2 = V("vector", lambda E, o=ROWS[32:33, q0:q0 + 512],
                   a=psB[0:1, 1536:2048]: E.tensor_copy(o, a), inc="vector")
            psB_free = e2
            psA_free = e2
        x1_stores = x1_store_t
        sc.flush()
    att.close()
    if upto <= 4:
        top.close(); return nc

    ffn = ExitStack()
    RB2 = sb("RB2", [128, TOWN], BF16, ffn)
    MB2 = sb("MB2", [128, TOWN], BF16, ffn)
    with ExitStack() as ph:
        tmpr = sb("tmpr2", [1, 3 * 2048], F32, ph)
        brow2 = sb("brow2", [1, 1024], F32, ph)
        psBC2 = ph.enter_context(nc.psum_tensor("psBC2", [128, 1536], F32))
        wt = ln_rowmath(0, TOWN, tmpr, LNEPS, w0=psA_free)
        bsl3b = [psBC2[:, 0:512], psBC2[:, 512:1024], psBC2[:, 1024:1536]]
        b1 = mm_bcast(lambda c0: ROWS[0:1, c0:c0 + 512], TOWN, RB2, wt, bsl3b,
                      brow2)
        b2 = mm_bcast(lambda c0: ROWS[32:33, c0:c0 + 512], TOWN, MB2, b1,
                      bsl3b, brow2, free0=b1)
        sc.flush()
    ln2bc = b2
    if upto <= 5:
        ffn.close(); top.close(); return nc

    out_done = None
    with ExitStack() as ph:
        xn2 = sb("xn2", [128, 16 * HB], BF16, ph)
        hbuf = sb("hbuf", [128, NIC * HB], BF16, ph)
        wstu = sb("wstu", [128, NIC * 256], BF16, ph)
        wst = wstu
        wdst = wstu
        byt = sb("byt", [128, 4 * 256], U16, ph)
        sgb = sb("sgb", [128, 4 * 256], F16, ph)
        dqA = sb("dqA", [128, 256], U16, ph)
        dqB = sb("dqB", [128, 256], U16, ph)
        dqU = sb("dqU", [128, 2 * 256], U16, ph)
        sg = sb("sg", [128, 2 * HB], BF16, ph)
        ysq = sb("ysq", [128, 512], BF16, ph)
        rb3 = sb("rb3", [128, HB], F32, ph)
        xff = sb("xff", [128, 2 * HB], F32, ph)
        oscr = sb("oscr", [128, HB], F32, ph)
        rtmp = sb("rtmp3", [1, HB], F32, ph)
        brow3 = sb("brow3", [1, 1024], F32, ph)
        dscr = sb("dscr", [128, 16], F32, ph)
        psG = ph.enter_context(nc.psum_tensor("psG", [128, 2 * 512], F32))
        psD = ph.enter_context(nc.psum_tensor("psD", [128, 4 * 512], F32))
        psR = ph.enter_context(nc.psum_tensor("psR", [1, 1024], F32))

        rb3_ev = None
        byt_free = [None] * 4
        dq_free = None
        down_prev = None
        xf_free = [None, None]
        os_free = [None, None]
        y_free = None
        hb_free = None

        dqp = {"u_lo": None, "u_hi": None}

        def dequant(bt_ap, scb_ap, out_lo, out_hi, w8, wfree):
            a1 = V("vector", lambda E, b=bt_ap: E.tensor_scalar(
                dqA[:], b, 0x07, 9, OP.bitwise_and, OP.logical_shift_left),
                waits=list(w8) + [dq_free, wfree], inc="vector")
            a2 = V("vector", lambda E, b=bt_ap: E.tensor_scalar(
                dqB[:], b, 0x08, 12, OP.bitwise_and, OP.logical_shift_left),
                inc="vector")
            v3 = V("vector", lambda E: E.tensor_tensor(
                dqU[:, 0:256], dqA[:], dqB[:], OP.bitwise_or),
                waits=[a1, a2, dqp["u_lo"]], inc="vector")
            a4 = V("vector", lambda E, b=bt_ap: E.tensor_scalar(
                dqA[:], b, 0x70, 5, OP.bitwise_and, OP.logical_shift_left),
                waits=[v3], inc="vector")
            a5 = V("vector", lambda E, b=bt_ap: E.tensor_scalar(
                dqB[:], b, 0x80, 8, OP.bitwise_and, OP.logical_shift_left),
                inc="vector")
            t = V("vector", lambda E: E.tensor_tensor(
                dqU[:, 256:512], dqA[:], dqB[:], OP.bitwise_or),
                waits=[a4, a5, dqp["u_hi"]], inc="vector")
            v7 = V("vector", lambda E, o=out_lo, s_=scb_ap: E.tensor_tensor(
                o, dqU[:, 0:256].bitcast(F16), s_, OP.mult),
                waits=[v3], inc="vector")
            t2 = V("vector", lambda E, o=out_hi, s_=scb_ap: E.tensor_tensor(
                o, dqU[:, 256:512].bitcast(F16), s_, OP.mult),
                waits=[t], inc="vector")
            dqp["u_lo"] = v7
            dqp["u_hi"] = t2
            return t, t2

        for hf in range(2 if fparts >= 19 else 1):
            t0 = HB * hf
            lastx = None
            d_tok = [None] * (NCH + 1)
            d_tok[0] = sc.dma("sync", xff[:, 0:HB],
                              ap["x1d"][0:128, t0:t0 + HB],
                              waits=[xf_free[0], x1_stores,
                                     ln2bc if hf == 0 else None,
                                     y_free if hf == 1 else None])
            for ch in range(NCH):
                xr = xff[:, (ch % 2) * HB:(ch % 2 + 1) * HB]
                osl = oscr[:, 0:HB]
                if ch + 1 < NCH:
                    d_tok[ch + 1] = sc.dma(
                        "sync", xff[:, ((ch + 1) % 2) * HB:((ch + 1) % 2 + 1) * HB],
                        ap["x1d"][128 * (ch + 1):128 * (ch + 2), t0:t0 + HB],
                        waits=[xf_free[(ch + 1) % 2]])
                else:
                    d_tok[ch + 1] = sc.dma("sync", dscr[:],
                                           ap["x1d"][0:128, 0:16])
                tt1 = V("vector", lambda E, o=osl, a=xr, b=MB2[:, t0:t0 + HB]:
                        E.tensor_tensor(o, a, b, OP.add),
                        waits=[d_tok[ch], d_tok[ch + 1],
                               os_free[0] if ch == 0 else None,
                               lastx],
                        inc="vector")
                xf_free[ch % 2] = tt1
                lastx = V("vector", lambda E, o=xn2[:, HB * ch:HB * (ch + 1)],
                          a=osl, b=RB2[:, t0:t0 + HB]:
                          E.tensor_tensor(o, a, b, OP.mult),
                          waits=[y_free if (hf == 1 and ch == 0) else None,
                                 tt1],
                          inc="vector")
            os_free[0] = lastx
            if fparts <= 1:
                break
            sg_t = [None, None]
            hb_t = None
            psG_free = [rb3_ev, rb3_ev]
            wst_free = [down_prev, down_prev]
            hb_free = down_prev
            pend_ev = None

            def emit_sg(oc, tt_, gsl, pg, grp, nxt):
                sg_t[oc] = V("scalar", lambda E,
                             o=sg[:, HB * oc + 512 * tt_:
                                  HB * oc + 512 * (tt_ + 1)], a=pg:
                             E.activation(o, a, AF.Silu, scale=S14),
                             waits=[grp, nxt], inc="scalar")
                psG_free[gsl] = sg_t[oc]

            def emit_hb(obi, oc, tt_, gsl, pg, grp, nxt):
                nonlocal hb_t
                ic = 2 * obi + oc
                hb_t = V("vector", lambda E,
                         o=hbuf[:, HB * ic + 512 * tt_:
                               HB * ic + 512 * (tt_ + 1)], a=pg,
                         b=sg[:, HB * oc + 512 * tt_:
                              HB * oc + 512 * (tt_ + 1)]:
                         E.tensor_tensor(o, a, b, OP.mult),
                         waits=[grp, nxt, sg_t[oc], hb_free], inc="vector")
                psG_free[gsl] = hb_t
            for obi in range(NOB):
                for gu in range(2):
                    wsl = gu
                    wsp = wst[:, wsl * 4096:(wsl + 1) * 4096]
                    col0 = II * gu + 256 * obi
                    dq_t = None
                    for jj in range(8):
                        bsl = jj % 4
                        bt_ap = byt[:, 256 * bsl:256 * (bsl + 1)]
                        scb = sgb[:, 256 * bsl:256 * (bsl + 1)]
                        d1 = sc.dma("sync", bt_ap,
                                    ap["bgu"][128 * jj:128 * (jj + 1),
                                              col0:col0 + 256],
                                    waits=[byt_free[bsl]])
                        d2 = sc.dma("sync", scb[:, :],
                                    ap["sgu"][128 * jj:128 * (jj + 1),
                                              col0:col0 + 256])
                        _, dq_t = dequant(
                            bt_ap, scb,
                            wsp[:, 256 * (2 * jj):256 * (2 * jj + 1)],
                            wsp[:, 256 * (2 * jj + 1):256 * (2 * jj + 2)],
                            [d1, d2], wst_free[wsl] if jj == 0 else None)
                        byt_free[bsl] = dq_t
                    dq_free = dq_t
                    mmg = [None, None]
                    for oc in range(2):
                        for tt_ in range(2):
                            gsl = tt_
                            pg = psG[:, 512 * gsl:512 * (gsl + 1)]
                            for chh in range(16):
                                w0 = []
                                if chh == 0:
                                    w0 = [dq_t, lastx, psG_free[gsl]]
                                mmg[gsl] = V("tensor", lambda E, o=pg,
                                             l=wsp[:, 256 * chh + 128 * oc:
                                                   256 * chh + 128 * oc + 128],
                                             r=xn2[:, HB * chh + 512 * tt_:
                                                   HB * chh + 512 * (tt_ + 1)],
                                             st_=(chh == 0):
                                             E.matmul(o, l, r, start=st_,
                                                      stop=(chh == 15)),
                                             waits=w0,
                                             inc=("tensor" if chh == 15
                                                  else None)) or mmg[gsl]
                            if pend_ev is not None:
                                pend_ev(mmg[gsl])
                            grp = mmg[gsl]
                            if gu == 0:
                                pend_ev = (lambda nxt, oc=oc, tt_=tt_,
                                           gsl=gsl, pg=pg, grp=grp:
                                           emit_sg(oc, tt_, gsl, pg, grp,
                                                   nxt))
                            else:
                                pend_ev = (lambda nxt, obi=obi, oc=oc,
                                           tt_=tt_, gsl=gsl, pg=pg, grp=grp:
                                           emit_hb(obi, oc, tt_, gsl, pg,
                                                   grp, nxt))
                    wst_free[wsl] = mmg[0] if mmg[1] is None else mmg[1]
            dum = V("tensor", lambda E: E.matmul(
                psG[0:1, 0:512], onesr_t[0:1, 0:1], brow3[0:1, 0:512],
                start=True, stop=True),
                waits=[wst_free[1], psG_free[0]], inc="tensor")
            dum_b = V("tensor", lambda E: E.matmul(
                psG[0:1, 0:512], onesr_t[0:1, 0:1], brow3[0:1, 0:512],
                start=True, stop=True),
                waits=[dum], inc="tensor")
            pend_ev(dum_b)
            pend_ev = None
            if fparts <= 2:
                break
            n_hg = (fparts - 10) if 10 < fparts < 19 else (1 if fparts >= 30 else 8)
            psD_free = [None] * 4
            rms_t = None
            y_t = None
            wdst_free = [hb_t, hb_t]
            for hg in range(n_hg):
                hc0 = 256 * hg
                wd = wdst[:, 0:NIC * 256]
                dq_t = None
                for ip in range(NOB):
                    bsl = ip % 4
                    bt_ap = byt[:, 256 * bsl:256 * (bsl + 1)]
                    scb = sgb[:, 256 * bsl:256 * (bsl + 1)]
                    d1 = sc.dma("sync", bt_ap,
                           ap["bdn"][128 * ip:128 * (ip + 1), hc0:hc0 + 256],
                           waits=[byt_free[bsl]])
                    d2 = sc.dma("sync", scb[:, :],
                                ap["sdn"][128 * ip:128 * (ip + 1),
                                          hc0:hc0 + 256])
                    _, dq_t = dequant(
                        bt_ap, scb,
                        wd[:, 256 * (2 * ip):256 * (2 * ip + 1)],
                        wd[:, 256 * (2 * ip + 1):256 * (2 * ip + 2)],
                        [d1, d2], wdst_free[0] if ip == 0 else None)
                    byt_free[bsl] = dq_t
                dq_free = dq_t
                mmd = None
                for ic in range(NIC):
                    for hhc in range(2):
                        for tt_ in range(2):
                            bidx = 2 * hhc + tt_
                            pd = psD[:, 512 * bidx:512 * (bidx + 1)]
                            w0 = []
                            if ic == 0:
                                w0 = [dq_t, hb_t]
                                if psD_free[bidx] is not None:
                                    w0 += psD_free[bidx]
                            mmd = V("tensor", lambda E, o=pd,
                                    l=wd[:, 256 * ic + 128 * hhc:
                                         256 * ic + 128 * hhc + 128],
                                    r=hbuf[:, HB * ic + 512 * tt_:
                                           HB * ic + 512 * (tt_ + 1)],
                                    st_=(ic == 0):
                                    E.matmul(o, l, r, start=st_,
                                             stop=(ic == NIC - 1)),
                                    waits=w0,
                                    inc=("tensor" if (ic == NIC - 1)
                                         else None)) or mmd
                wdst_free[0] = mmd
                for hhc in range(2):
                    hh = 2 * hg + hhc
                    for tt_ in range(2):
                        bidx = 2 * hhc + tt_
                        pd = psD[:, 512 * bidx:512 * (bidx + 1)]
                        y_t = V("vector", lambda E,
                                o=xn2[:, HB * hh + 512 * tt_:
                                      HB * hh + 512 * (tt_ + 1)],
                                a=pd: E.tensor_copy(o, a),
                                waits=[mmd, lastx if hg == 0 else None]
                                if (hhc == 0 and tt_ == 0) else [mmd],
                                inc="vector")
                        sqy = V("scalar", lambda E, o=ysq[:],
                                a=xn2[:, HB * hh + 512 * tt_:
                                      HB * hh + 512 * (tt_ + 1)]:
                                E.activation(o, a, AF.Square, scale=S28),
                                waits=[y_t, rms_t], inc="scalar")
                        rms_t = V("tensor", lambda E,
                                  o=psR[0:1, 512 * tt_:512 * (tt_ + 1)],
                                  st_=(hg == 0 and hhc == 0):
                                  E.matmul(o, onesc_t[:], ysq[:], start=st_,
                                           stop=(hg == n_hg - 1 and hhc == 1)),
                                  waits=[sqy], inc="tensor")
                        psD_free[bidx] = [y_t, y_t]
            if fparts <= 3:
                break
            dum2 = V("tensor", lambda E: E.matmul(
                psG[0:1, 0:512], onesr_t[0:1, 0:1], brow3[0:1, 0:512],
                start=True, stop=True),
                waits=[rms_t], inc="tensor")
            dum2b = V("tensor", lambda E: E.matmul(
                psG[0:1, 0:512], onesr_t[0:1, 0:1], brow3[0:1, 0:512],
                start=True, stop=True),
                waits=[dum2], inc="tensor")
            ev = V("vector", lambda E: E.tensor_copy(rtmp[0:1, :], psR[0:1, :]),
                   waits=[rms_t, dum2b], inc="vector")
            t2 = V("scalar", lambda E: E.activation(rtmp[0:1, :], rtmp[0:1, :],
                                                    AF.Sqrt, scale=1.0 / D,
                                                    bias=RMSEPS),
                   waits=[ev], inc="scalar")
            t3 = V("vector", lambda E: E.reciprocal(rtmp[0:1, :], rtmp[0:1, :]),
                   waits=[t2], inc="vector")
            t4 = V("vector", lambda E: E.tensor_scalar(rtmp[0:1, :],
                                                       rtmp[0:1, :], S28, None,
                                                       OP.mult), waits=[t3],
                   inc="vector")
            b4 = mm_bcast(lambda c0: rtmp[0:1, c0:c0 + 512], HB, rb3, t4,
                          [psG[:, 0:512], psG[:, 512:1024], psD[:, 0:512]],
                          brow3, free0=hb_t)
            rb3_ev = b4
            fin = None
            d_tok = [None] * (NCH + 1)
            d_tok[0] = sc.dma("sync", xff[:, 0:HB],
                              ap["x1d"][0:128, t0:t0 + HB],
                              waits=[xf_free[0]])
            for ch in range(NCH):
                xr = xff[:, (ch % 2) * HB:(ch % 2 + 1) * HB]
                if ch + 1 < NCH:
                    d_tok[ch + 1] = sc.dma(
                        "sync", xff[:, ((ch + 1) % 2) * HB:((ch + 1) % 2 + 1) * HB],
                        ap["x1d"][128 * (ch + 1):128 * (ch + 2), t0:t0 + HB],
                        waits=[xf_free[(ch + 1) % 2]])
                else:
                    d_tok[ch + 1] = sc.dma("sync", dscr[:],
                                           ap["x1d"][0:128, 0:16])
                osl = oscr[:, 0:HB]
                m1 = V("vector", lambda E, o=osl,
                       a=xn2[:, HB * ch:HB * (ch + 1)], b=rb3[:]:
                       E.tensor_tensor(o, a, b, OP.mult),
                       waits=[b4, y_t if ch == 0 else None,
                              os_free[0]], inc="vector")
                a1 = V("vector", lambda E, o=osl, a=osl, b=xr:
                       E.tensor_tensor(o, a, b, OP.add),
                       waits=[d_tok[ch], d_tok[ch + 1], m1], inc="vector")
                xf_free[ch % 2] = a1
                s_t = sc.dma("gpsimd",
                             ap["outT"][128 * ch:128 * (ch + 1), t0:t0 + HB],
                             osl, waits=[a1])
                os_free[0] = s_t
                fin = s_t
            y_free = fin
            down_prev = wdst_free[0]
            out_done = fin
        if out_done is not None:
            V("gpsimd", lambda E: E.memset(dqA[0:1, 0:1], 0),
              waits=[out_done])
        sc.flush()
    ffn.close()
    top.close()
    return nc


_PROG = None


def _get_prog():
    global _PROG
    if _PROG is None:
        _PROG = build_program()
    return _PROG


def kernel(L0_attn_x, L0_attn_Wq, L0_attn_Wk, L0_attn_Wv, L0_attn_Wo,
           L0_ffn_gate_up_packed, L0_ffn_gate_up_scales,
           L0_ffn_down_packed, L0_ffn_down_scales):
    x = np.ascontiguousarray(np.asarray(L0_attn_x, dtype=np.float32))
    Wq = np.asarray(L0_attn_Wq, dtype=np.float32)
    Wk = np.asarray(L0_attn_Wk, dtype=np.float32)
    Wv = np.asarray(L0_attn_Wv, dtype=np.float32)
    Wo = np.asarray(L0_attn_Wo, dtype=np.float32)
    gup = np.asarray(L0_ffn_gate_up_packed)
    gus = np.asarray(L0_ffn_gate_up_scales)
    dnp_ = np.asarray(L0_ffn_down_packed)
    dns = np.asarray(L0_ffn_down_scales)

    dperm = _perm256(D)
    iperm = _perm256(II)
    qcp = _qcol_perm()
    kcp = _kcol_perm()
    worp = _wo_row_perm()

    wq_h = np.ascontiguousarray(Wq[dperm][:, qcp])
    wk_h = np.ascontiguousarray(Wk[dperm][:, kcp])
    wv_h = np.ascontiguousarray(Wv[dperm])
    wo_h = np.ascontiguousarray(Wo[worp][:, dperm])
    gub = np.ascontiguousarray(
        gup.view(np.uint8).reshape(2 * II, 1024).T).astype(np.uint16)
    icolp = np.concatenate([iperm, II + iperm])
    gub = np.ascontiguousarray(gub[:, icolp])
    sgu_h = np.ascontiguousarray(
        np.repeat(np.asarray(gus, dtype=np.float16)[:, icolp], 64, axis=0))
    dnb = np.ascontiguousarray(
        dnp_.view(np.uint8).reshape(D, 2816).T).astype(np.uint16)
    dnb = np.ascontiguousarray(dnb[:, dperm])
    sdn_h = np.ascontiguousarray(
        np.repeat(np.asarray(dns, dtype=np.float16)[:, dperm], 64, axis=0))

    half = HD // 2
    invf = 1.0 / 10000.0 ** (np.arange(half, dtype=np.float64) / half)
    pos = np.arange(S, dtype=np.float64)
    ang = np.outer(invf, pos)
    cosf = np.cos(ang).astype(np.float32)
    sinf = np.sin(ang).astype(np.float32)
    cosT_h = np.ascontiguousarray(np.tile(cosf, (4, 1)).astype(bf16))
    sinT_h = np.ascontiguousarray(np.tile(sinf, (4, 1)).astype(bf16))

    identb_h = np.eye(128, dtype=bf16)
    onesr_h = np.ones((1, 128), dtype=np.float32)
    onesc_h = np.ones((128, 1), dtype=bf16)
    ind4_h = np.zeros((128, 4), dtype=np.float32)
    for a in range(4):
        ind4_h[32 * a:32 * (a + 1), a] = 1.0
    ind4_h = ind4_h.astype(bf16)

    in_maps = []
    blocksets = [(0, 2, 4, 6), (1, 3, 5, 7)]
    for core in range(8):
        b = core // 2
        blocks = blocksets[core % 2]
        ownpos = np.concatenate([np.arange(QS * blk, QS * (blk + 1))
                                 for blk in blocks])
        xTb = np.ascontiguousarray(x[b].T[dperm])
        xq = np.ascontiguousarray(xTb[:, ownpos])
        cosQ_h = np.ascontiguousarray(cosT_h[:, ownpos])
        sinQ_h = np.ascontiguousarray(sinT_h[:, ownpos])
        mk = np.zeros((NSLOT, NMASK, 128, QS), dtype=np.float32)
        for s_i, blk in enumerate(blocks):
            nkc = SPANS[s_i] // KCH
            qpos = np.arange(QS * blk, QS * (blk + 1))
            for mi in range(NMASK):
                c = nkc - NMASK + mi
                kpos = np.arange(KCH * c, KCH * (c + 1))
                mk[s_i, mi] = (kpos[:, None] <= qpos[None, :])
        maskd_h = np.ascontiguousarray(
            mk.reshape(NSLOT * NMASK * 128, QS).astype(bf16))
        in_maps.append({
            "xT": xTb, "xqT": xq, "wq": wq_h, "wk": wk_h, "wv": wv_h,
            "wo": wo_h, "bgu": gub, "bdn": dnb, "sgu": sgu_h, "sdn": sdn_h,
            "cosT": cosT_h, "sinT": sinT_h, "cosQ": cosQ_h, "sinQ": sinQ_h,
            "maskd": maskd_h, "identb": identb_h, "onesr": onesr_h,
            "onesc": onesc_h, "ind4": ind4_h,
        })

    nc = _get_prog()
    res = run_bass_kernel_spmd(nc, in_maps, core_ids=list(range(8)))

    out = np.empty((NB, S, D), dtype=np.float32)
    inv = np.empty_like(dperm)
    inv[dperm] = np.arange(D)
    for core in range(8):
        b = core // 2
        blocks = blocksets[core % 2]
        ownpos = np.concatenate([np.arange(QS * blk, QS * (blk + 1))
                                 for blk in blocks])
        oT = res.results[core]["outT"]
        out[b, ownpos, :] = oT[inv].T
    return out


# version 18
# speedup vs baseline: 1.1208x; 1.1208x over previous
import numpy as np
import ml_dtypes
from contextlib import ExitStack

import concourse.bass as bass
import concourse.mybir as mybir
from concourse.bass_utils import run_bass_kernel_spmd

dt = mybir.dt
OP = mybir.AluOpType
AF = mybir.ActivationFunctionType
AX = mybir.AxisListType
F32, BF16, F16, U16 = dt.float32, dt.bfloat16, dt.float16, dt.uint16
bf16 = ml_dtypes.bfloat16

D, S, NB = 2048, 4096, 4
NH, NKV, HD = 32, 8, 64
II = 5632
NCH = 16
TOWN = 2048
NSLOT, QS, KCH = 4, 512, 128
SPANS = (1024, 2048, 3072, 4096)
NMASK = 8
LNEPS, RMSEPS = 1e-5, 1e-6
S14, S28 = float(2.0 ** 14), float(2.0 ** 28)
NIC, NOB = 44, 22
VTW = 520
HB = 1024

ENGS = ("sync", "tensor", "vector", "scalar", "gpsimd")


class Sched:

    def __init__(self, nc, stack):
        self.nc = nc
        self.q = {e: [] for e in ENGS}
        self.sem = {}
        self.stack = stack
        for e in ENGS:
            h = stack.enter_context(nc.semaphore("sem_" + e))
            self.sem[e] = [h, 0]
        h = stack.enter_context(nc.semaphore("sem_dma"))
        self.sem["dma"] = [h, 0]
        h = stack.enter_context(nc.semaphore("sem_dmg"))
        self.sem["dmg"] = [h, 0]

    def step(self, eng, emit, waits=(), inc=None):
        wl = []
        for w in waits:
            if w is None:
                continue
            s, v = w
            if v > 0:
                wl.append((self.sem[s][0], v))
        tok = None
        ih = None
        if inc is not None:
            rec = self.sem[inc]
            rec[1] += 1
            ih, tok = rec[0], (inc, rec[1])
        self.q[eng].append((wl, emit, ih, 1))
        return tok

    def dma(self, eng, out, in_, waits=()):
        semn = "dmg" if eng == "gpsimd" else "dma"
        wl = []
        for w in waits:
            if w is None:
                continue
            s, v = w
            if v > 0:
                wl.append((self.sem[s][0], v))
        rec = self.sem[semn]
        rec[1] += 16
        self.q[eng].append(
            (wl, lambda E, o=out, i=in_: E.dma_start(out=o, in_=i), rec[0], 16))
        return (semn, rec[1])

    def flush(self):
        qs, self.q = self.q, {e: [] for e in ENGS}
        with self.nc.Block() as blk:
            for ename in ENGS:
                items = qs[ename]
                if not items:
                    continue

                def body(E, items=items):
                    for wl, emit, ih, iv in items:
                        for (h, v) in wl:
                            E.wait_ge(h, v)
                        inst = emit(E)
                        if ih is not None:
                            inst.then_inc(ih, iv)
                getattr(blk, ename)(body)


def _perm256(n):
    p = []
    for s0 in range(0, n, 256):
        p += list(range(s0, s0 + 256, 2)) + list(range(s0 + 1, s0 + 256, 2))
    return np.asarray(p, dtype=np.int64)


def _qcol_perm():
    e, o = [], []
    for j in range(8):
        for h in (j, j + 8, j + 16, j + 24):
            e += [64 * h + 2 * f for f in range(32)]
            o += [64 * h + 2 * f + 1 for f in range(32)]
    return np.asarray(e + o, dtype=np.int64)


def _kcol_perm():
    e, o = [], []
    for tile in ((0, 2, 4, 6), (1, 3, 5, 7)):
        for kv in tile:
            e += [64 * kv + 2 * f for f in range(32)]
            o += [64 * kv + 2 * f + 1 for f in range(32)]
    return np.asarray(e + o, dtype=np.int64)


def _wo_row_perm():
    rows = []
    for j in range(8):
        for h in (j, j + 8):
            rows += list(range(64 * h, 64 * h + 64))
        for h in (j + 16, j + 24):
            rows += list(range(64 * h, 64 * h + 64))
    return np.asarray(rows, dtype=np.int64)


def _kv_of(j, a):
    return (j // 4) + 2 * a


def build_program(upto=99, fparts=19):
    nc = bass.Bass("TRN2", target_bir_lowering=False)
    top = ExitStack()
    sc = Sched(nc, top)

    ap = {}
    for name, shape, dty, kind in [
        ("xT", [D, S], F32, "ExternalInput"),
        ("xqT", [D, TOWN], F32, "ExternalInput"),
        ("wq", [D, D], F32, "ExternalInput"),
        ("wk", [512, 2048], F32, "ExternalInput"),
        ("wv", [512, 2048], F32, "ExternalInput"),
        ("wo", [D, D], F32, "ExternalInput"),
        ("bgu", [1024, 2 * II], U16, "ExternalInput"),
        ("bdn", [2816, D], U16, "ExternalInput"),
        ("sgu", [1024, 2 * II], F16, "ExternalInput"),
        ("sdn", [2816, D], F16, "ExternalInput"),
        ("cosT", [128, S], BF16, "ExternalInput"),
        ("sinT", [128, S], BF16, "ExternalInput"),
        ("cosQ", [128, TOWN], BF16, "ExternalInput"),
        ("sinQ", [128, TOWN], BF16, "ExternalInput"),
        ("maskd", [NSLOT * 128, NMASK * QS], BF16, "ExternalInput"),
        ("identb", [128, 128], BF16, "ExternalInput"),
        ("onesr", [1, 128], F32, "ExternalInput"),
        ("onesc", [128, 1], BF16, "ExternalInput"),
        ("ind4", [128, 4], BF16, "ExternalInput"),
        ("outT", [D, TOWN], F32, "ExternalOutput"),
        ("x1d", [D, TOWN], F32, "Internal"),
    ]:
        ap[name] = nc.dram_tensor(name, shape, dty, kind=kind).ap()

    def sb(name, shape, dty, stk):
        return stk.enter_context(nc.sbuf_tensor(name, shape, dty))

    identb_t = sb("identb_t", [128, 128], BF16, top)
    epst = sb("epst", [128, 2], F32, top)
    onesr_t = sb("onesr_t", [1, 128], F32, top)
    onesc_t = sb("onesc_t", [128, 1], BF16, top)
    ind4_t = sb("ind4_t", [128, 4], BF16, top)
    negB = sb("negB", [128, 1], F32, top)
    kn2 = sb("kn2", [1, NKV], F32, top)
    ROWS = sb("ROWS", [128, S], F32, top)

    V = sc.step

    for nm, dst in (("identb", identb_t), ("onesr", onesr_t),
                    ("onesc", onesc_t), ("ind4", ind4_t)):
        tok = sc.dma("sync", dst[:], ap[nm][:])
    V("vector", lambda E: E.memset(kn2[:], 0.0), waits=[tok], inc="vector")
    V("vector", lambda E: E.memset(epst[:, 0:1], LNEPS))
    V("vector", lambda E: E.memset(epst[:, 1:2], RMSEPS))
    nc.const_aps.register(F32, LNEPS, epst[:, 0:1])
    nc.const_aps.register(F32, RMSEPS, epst[:, 1:2])
    sc.flush()

    with ExitStack() as ph:
        p1x = sb("p1x", [128, 2 * 2048], F32, ph)
        p1b = sb("p1b", [128, 2 * 2048], BF16, ph)
        p1s = sb("p1s", [128, 2 * 2048], BF16, ph)
        p1ps = ph.enter_context(nc.psum_tensor("p1ps", [1, 4096], F32))
        rounds = [(ap["xT"], 0, 0, 0), (ap["xT"], 2048, 0, 2048),
                  (ap["xqT"], 0, 64, 0)]
        w = 2048
        cp_t = [None] * 48
        sq_t = [None] * 48
        mm_t = [None] * 48
        ev_t = None
        i = 0
        for r, (src, c0, rbase, rcol) in enumerate(rounds):
            for ch in range(NCH):
                sl = i % 2
                xr = p1x[:, sl * 2048:(sl + 1) * 2048]
                xb = p1b[:, sl * 2048:(sl + 1) * 2048]
                xq2 = p1s[:, sl * 2048:(sl + 1) * 2048]
                d_t = sc.dma("sync", xr, src[128 * ch:128 * (ch + 1), c0:c0 + w],
                             waits=[cp_t[i - 2] if i >= 2 else None])
                cp_t[i] = V("vector", lambda E, o=xb, a=xr: E.tensor_copy(o, a),
                            waits=[d_t, sq_t[i - 2] if i >= 2 else None,
                                   mm_t[i - 2] if i >= 2 else None],
                            inc="vector")
                sq_t[i] = V("scalar", lambda E, o=xq2, a=xb:
                            E.activation(o, a, AF.Square),
                            waits=[cp_t[i], mm_t[i - 2] if i >= 2 else None],
                            inc="scalar")
                for g in range(4):
                    V("tensor", lambda E, o=p1ps[0:1, 1024 * g:1024 * g + 512],
                      rr=xb[:, 512 * g:512 * (g + 1)], st_=(ch == 0):
                      E.matmul(o, onesc_t[:], rr, start=st_, stop=(ch == 15)),
                      waits=([sq_t[i], ev_t] if g == 0 else []))
                    lastg = (g == 3)
                    mm_t[i] = V("tensor",
                                lambda E, o=p1ps[0:1, 1024 * g + 512:1024 * (g + 1)],
                                rr=xq2[:, 512 * g:512 * (g + 1)], st_=(ch == 0):
                                E.matmul(o, onesc_t[:], rr, start=st_,
                                         stop=(ch == 15)),
                                inc=("tensor" if lastg else None)) or mm_t[i]
                i += 1
            for g in range(4):
                V("scalar", lambda E,
                  o=ROWS[rbase:rbase + 1, rcol + 512 * g:rcol + 512 * (g + 1)],
                  a=p1ps[0:1, 1024 * g:1024 * g + 512]:
                  E.copy(o, a), waits=([mm_t[i - 1]] if g == 0 else []))
                ev_t = V("scalar", lambda E,
                         o=ROWS[rbase + 32:rbase + 33,
                                rcol + 512 * g:rcol + 512 * (g + 1)],
                         a=p1ps[0:1, 1024 * g + 512:1024 * (g + 1)]:
                         E.copy(o, a), inc="scalar")
        sc.flush()
    if upto <= 1:
        top.close(); return nc

    att = ExitStack()
    RBq = sb("RBq", [128, TOWN], BF16, att)
    MBq = sb("MBq", [128, TOWN], BF16, att)
    Kt = sb("Kt", [128, 4 * S], BF16, att)
    VT = sb("VT", [128, (S // KCH) * VTW], BF16, att)
    ln1 = ExitStack()
    RB1 = sb("RB1", [128, S], BF16, ln1)
    MB1 = sb("MB1", [128, S], BF16, ln1)

    def ln_rowmath(base, w, tmp, eps, w0=None):
        last = None
        for c0 in range(0, w, 2048):
            cw = min(2048, w - c0)
            su = ROWS[base:base + 1, c0:c0 + cw]
            sq = ROWS[base + 32:base + 33, c0:c0 + cw]
            A = tmp[0:1, 0:cw]
            Bv = tmp[0:1, 2048:2048 + cw]
            C = tmp[0:1, 4096:4096 + cw]
            tk = V("vector", lambda E, a=A, s_=sq: E.tensor_copy(a, s_),
                   waits=[last, w0 if last is None else None], inc="vector")
            tk = V("vector", lambda E, b=Bv, s_=su: E.tensor_copy(b, s_),
                   waits=[tk], inc="vector")
            tk = V("vector", lambda E, a=A: E.tensor_scalar(a, a, 1.0 / D, None,
                                                            OP.mult),
                   waits=[tk], inc="vector")
            tk = V("vector", lambda E, b=Bv: E.tensor_scalar(b, b, 1.0 / D, None,
                                                             OP.mult),
                   waits=[tk], inc="vector")
            tk = V("vector", lambda E, c=C, b=Bv: E.tensor_tensor(c, b, b,
                                                                  OP.mult),
                   waits=[tk], inc="vector")
            t1 = V("vector", lambda E, a=A, c=C:
                   E.tensor_tensor(a, a, c, OP.subtract), waits=[tk],
                   inc="vector")
            t2 = V("scalar", lambda E, c=C, a=A:
                   E.activation(c, a, AF.Sqrt, bias=eps), waits=[t1],
                   inc="scalar")
            tk = V("vector", lambda E, a=A, c=C: E.reciprocal(a, c),
                   waits=[t2], inc="vector")
            tk = V("vector", lambda E, s_=su, a=A: E.tensor_copy(s_, a),
                   waits=[tk], inc="vector")
            tk = V("vector", lambda E, c=C, b=Bv: E.tensor_scalar(c, b, -1.0,
                                                                  None,
                                                                  OP.mult),
                   waits=[tk], inc="vector")
            last = V("vector", lambda E, s_=sq, c=C: E.tensor_copy(s_, c),
                     waits=[tk], inc="vector")
        return last

    def ln_bcast(base, w, RB, MB, wt, slots, brow, tmp, eps, free0=None):
        n = w // 512
        mm = [None] * (2 * n + 1)
        ev3 = [free0] * 3
        last = None
        tA, tM, tC, tB = (tmp[:, 0:512], tmp[:, 512:1024],
                          tmp[:, 1024:1536], tmp[:, 1536:2048])
        for i in range(2 * n + 1):
            sl = i % 3
            bsl = i % 2
            if i < 2 * n:
                row = base if (i % 2 == 0) else base + 32
                c0 = 512 * (i // 2)
                cp = V("vector", lambda E,
                       o=brow[0:1, 512 * bsl:512 * (bsl + 1)],
                       a=ROWS[row:row + 1, c0:c0 + 512]: E.tensor_copy(o, a),
                       waits=[wt if i == 0 else None,
                              free0 if i == 0 else None,
                              mm[i - 2] if i >= 2 else None], inc="vector")
            else:
                cp = None
            mm[i] = V("tensor", lambda E, o=slots[sl],
                      r=brow[0:1, 512 * bsl:512 * (bsl + 1)]:
                      E.matmul(o, onesr_t[:], r, start=True, stop=True),
                      waits=[cp, ev3[sl]], inc="tensor")
            if i >= 2 and i % 2 == 0:
                c = (i - 2) // 2
                c0 = 512 * c
                s_sl = (2 * c) % 3
                q_sl = (2 * c + 1) % 3
                t1 = V("vector", lambda E, o=tM, a=slots[s_sl]:
                       E.tensor_scalar(o, a, 1.0 / D, None, OP.mult),
                       waits=[mm[i], last], inc="vector")
                t2 = V("vector", lambda E, o=MB[:, c0:c0 + 512], a=slots[s_sl]:
                       E.tensor_scalar(o, a, -1.0 / D, None, OP.mult),
                       waits=[t1], inc="vector")
                ev3[s_sl] = t2
                t3 = V("vector", lambda E, o=tA, a=slots[q_sl]:
                       E.tensor_scalar(o, a, 1.0 / D, None, OP.mult),
                       waits=[t2], inc="vector")
                ev3[q_sl] = t3
                t4 = V("vector", lambda E, o=tC, a=tM, b=tM:
                       E.tensor_tensor(o, a, b, OP.mult), waits=[t3],
                       inc="vector")
                t5 = V("vector", lambda E, o=tA, a=tA, b=tC:
                       E.tensor_tensor(o, a, b, OP.subtract), waits=[t4],
                       inc="vector")
                t6 = V("scalar", lambda E, o=tB, a=tA:
                       E.activation(o, a, AF.Sqrt, bias=eps), waits=[t5],
                       inc="scalar")
                t7 = V("vector", lambda E, o=tM, a=tB:
                       E.reciprocal(o, a), waits=[t6], inc="vector")
                last = V("vector", lambda E, o=RB[:, c0:c0 + 512], a=tM:
                         E.tensor_copy(o, a), waits=[t7], inc="vector")
        return last

    def mm_bcast(rsl, w, dest, wt, slots, brow, free0=None):
        n = w // 512
        ev = [free0, free0, free0]
        mm = [None] * (n + 2)
        last = None
        for i in range(n + 2):
            sl = i % 3
            bsl = i % 2
            if i < n:
                cp = V("vector", lambda E,
                       o=brow[0:1, 512 * bsl:512 * (bsl + 1)],
                       a=rsl(512 * i): E.tensor_copy(o, a),
                       waits=[wt if i == 0 else None,
                              free0 if i == 0 else None,
                              mm[i - 2] if i >= 2 else None, last],
                       inc="vector")
            else:
                cp = None
            mm[i] = V("tensor", lambda E, o=slots[sl],
                      r=brow[0:1, 512 * bsl:512 * (bsl + 1)]:
                      E.matmul(o, onesr_t[:], r, start=True, stop=True),
                      waits=[cp, ev[sl]], inc="tensor")
            if i >= 2:
                j = i - 2
                jsl = j % 3
                last = V("vector", lambda E, o=dest[:, 512 * j:512 * (j + 1)],
                         a=slots[jsl]:
                         E.tensor_copy(o, a), waits=[mm[i], last],
                         inc="vector")
                ev[jsl] = last
        return last

    bc_t = []
    with ExitStack() as ph:
        brow = sb("brow", [1, 1024], F32, ph)
        tmpf = sb("tmpf", [128, 2048], F32, ph)
        psBC = ph.enter_context(nc.psum_tensor("psBC", [128, 1536], F32))
        bsl3 = [psBC[:, 0:512], psBC[:, 512:1024], psBC[:, 1024:1536]]
        b = ln_bcast(0, S, RB1, MB1, ev_t, bsl3, brow, tmpf, LNEPS)
        bc_t.append(b)
        bc_t.append(b)
        b = ln_bcast(64, TOWN, RBq, MBq, ev_t, bsl3, brow, tmpf, LNEPS,
                     free0=b)
        bc_t.append(b)
        bc_t.append(b)
        sc.flush()
    if upto <= 2:
        ln1.close(); att.close(); top.close(); return nc

    with ExitStack() as ph:
        cosK = sb("cosK", [128, S], BF16, ph)
        sinK = sb("sinK", [128, S], BF16, ph)
        wkv_f = sb("wkv_f", [128, 4 * 512], F32, ph)
        wk_b = sb("wk_b", [128, 16 * 512], BF16, ph)
        wv_b = sb("wv_b", [128, 16 * 512], BF16, ph)
        xs_f = sb("xs_f", [128, 2 * 512], F32, ph)
        xns = sb("xns", [128, 2 * 16 * 512], BF16, ph)
        rsc = sb("rsc", [128, 2 * 512], F32, ph)
        vsb = sb("vsb", [128, 512], BF16, ph)
        ksq = sb("ksq", [128, 2 * 512], BF16, ph)
        psE = ph.enter_context(nc.psum_tensor("psE", [128, 512], F32))
        psO = ph.enter_context(nc.psum_tensor("psO", [128, 512], F32))
        psV = ph.enter_context(nc.psum_tensor("psV", [128, 512], F32))
        psT = ph.enter_context(nc.psum_tensor("psT", [128, 128], BF16))
        psN2 = ph.enter_context(nc.psum_tensor("psN2", [1, 2048], F32))

        vt0 = V("vector", lambda E: E.memset(VT[:], 1.0), inc="vector")
        sc.dma("sync", cosK[:], ap["cosT"][:])
        sc.dma("sync", sinK[:], ap["sinT"][:])
        cw2 = None
        for wsrc, wdst_b in ((ap["wk"], wk_b), (ap["wv"], wv_b)):
            for rd in range(4):
                d_t = sc.dma("sync", wkv_f[:],
                             wsrc[128 * rd:128 * (rd + 1), :], waits=[cw2])
                cw2 = V("scalar", lambda E,
                        o=wdst_b[:, 2048 * rd:2048 * (rd + 1)], a=wkv_f[:]:
                        E.copy(o, a), waits=[d_t], inc="scalar")

        xn_cons = [None, None]
        rp_prev = None
        tr_last = None
        mmK_t = None
        rope_t = [None, None]
        vtr_t = None
        dma_x = [None] * (8 * NCH)
        for slp in range(8):
            sl = slp
            xsl = slp % 2
            xn = xns[:, xsl * 8192:(xsl + 1) * 8192]
            c0 = 512 * sl
            last_apply = None
            for ch in range(NCH):
                xr = xs_f[:, (ch % 2) * 512:(ch % 2 + 1) * 512]
                ii = slp * NCH + ch
                d_t = sc.dma("sync", xr,
                             ap["xT"][128 * ch:128 * (ch + 1), c0:c0 + 512],
                             waits=[dma_x[ii - 2] if ii >= 2 else bc_t[1]])
                tt1 = V("vector", lambda E, o=rsc[:, 0:512], a=xr,
                        b=MB1[:, c0:c0 + 512]: E.tensor_tensor(o, a, b, OP.add),
                        waits=[d_t, xn_cons[xsl] if ch == 0 else None,
                               bc_t[1] if slp == 0 and ch == 0 else None,
                               rp_prev if ch == 0 else None,
                               last_apply],
                        inc="vector")
                dma_x[ii] = tt1
                last_apply = V("vector",
                               lambda E, o=xn[:, 512 * ch:512 * (ch + 1)],
                               a=rsc[:, 0:512], b=RB1[:, c0:c0 + 512]:
                               E.tensor_tensor(o, a, b, OP.mult),
                               waits=[tt1], inc="vector")
            rp = None
            for pair in range(2):
                for part, ps in ((0, psE), (1, psO)):
                    for ch in range(NCH):
                        co = 512 * ch + 128 * (pair + 2 * part)
                        tok = V("tensor", lambda E, o=ps[:],
                                l=wk_b[:, co:co + 128],
                                r=xn[:, 512 * ch:512 * (ch + 1)], st_=(ch == 0):
                                E.matmul(o, l, r, start=st_, stop=(ch == 15)),
                                waits=([last_apply, cw2, rp] if ch == 0 else []),
                                inc=("tensor" if ch == 15 else None))
                        if ch == 15:
                            mmK_t = tok
                csl = cosK[:, c0:c0 + 512]
                ssl = sinK[:, c0:c0 + 512]
                ke = Kt[:, pair * S + c0:pair * S + c0 + 512]
                ko = Kt[:, (pair + 2) * S + c0:(pair + 2) * S + c0 + 512]
                k1 = V("vector", lambda E, o=rsc[:, 0:512], a=psE[:], b=csl:
                       E.tensor_tensor(o, a, b, OP.mult), waits=[mmK_t, last_apply],
                       inc="vector")
                k2 = V("vector", lambda E, o=rsc[:, 512:1024], a=psO[:], b=ssl:
                       E.tensor_tensor(o, a, b, OP.mult), inc="vector")
                k3 = V("vector", lambda E, o=ke, a=rsc[:, 0:512], b=rsc[:, 512:1024]:
                       E.tensor_tensor(o, a, b, OP.subtract), waits=[k1, k2],
                       inc="vector")
                k4 = V("vector", lambda E, o=rsc[:, 0:512], a=psE[:], b=ssl:
                       E.tensor_tensor(o, a, b, OP.mult), waits=[k3], inc="vector")
                k5 = V("vector", lambda E, o=rsc[:, 512:1024], a=psO[:], b=csl:
                       E.tensor_tensor(o, a, b, OP.mult), inc="vector")
                rp = V("vector", lambda E, o=ko, a=rsc[:, 0:512],
                       b=rsc[:, 512:1024]: E.tensor_tensor(o, a, b, OP.add),
                       waits=[k4, k5], inc="vector")
            rope_t[xsl] = rp
            rp_prev = rp
            for vt in range(4):
                vslot = vt % 2
                pv = psV[:, 0:512]
                for ch in range(NCH):
                    co = 512 * ch + 128 * vt
                    tok = V("tensor", lambda E, o=pv, l=wv_b[:, co:co + 128],
                            r=xn[:, 512 * ch:512 * (ch + 1)], st_=(ch == 0):
                            E.matmul(o, l, r, start=st_, stop=(ch == 15)),
                            waits=([cw2, rp, vtr_t] if ch == 0 else []),
                            inc=("tensor" if ch == 15 else None))
                    if ch == 15:
                        mmV_t = tok
                        xn_cons[xsl] = tok
                vs = vsb[:, 0:512]
                ev = V("scalar", lambda E, o=vs, a=pv: E.copy(o, a),
                       waits=[mmV_t, tr_last], inc="scalar")
                for kc in range(4):
                    cglob = 4 * sl + kc
                    tr = V("tensor", lambda E, o=psT[:],
                           a=vs[:, 128 * kc:128 * (kc + 1)]:
                           E.transpose(o, a, identb_t[:]),
                           waits=[ev, vt0, vtr_t], inc="tensor")
                    tr_last = tr
                    for half in range(2):
                        kv = 2 * vt + half
                        vtr_t = V("scalar", lambda E,
                                  o=VT[:, cglob * VTW + kv * 65:
                                       cglob * VTW + kv * 65 + 64],
                                  a=psT[:, 64 * half:64 * (half + 1)]:
                                  E.copy(o, a), waits=[tr],
                                  inc="scalar")
        kq_t = None
        for pair in range(2):
            sq_ts = []
            mmn = None
            for g in range(8):
                s1 = V("scalar", lambda E, o=ksq[:, 0:512],
                       a=Kt[:, pair * S + 512 * g:pair * S + 512 * (g + 1)]:
                       E.activation(o, a, AF.Square),
                       waits=[rope_t[0], rope_t[1], mmn] if g == 0 else [mmn],
                       inc="scalar")
                s2 = V("scalar", lambda E, o=ksq[:, 512:1024],
                       a=Kt[:, (pair + 2) * S + 512 * g:(pair + 2) * S + 512 * (g + 1)]:
                       E.activation(o, a, AF.Square), inc="scalar")
                for a_ in range(4):
                    V("tensor", lambda E, o=psN2[0:1, 512 * a_:512 * (a_ + 1)],
                      rr=ksq[:, 0:512], ia=a_, st_=(g == 0):
                      E.matmul(o, ind4_t[:, ia:ia + 1], rr, start=st_,
                               stop=False),
                      waits=([s2, kq_t] if (g == 0 and a_ == 0) else []))
                    mmn = V("tensor", lambda E, o=psN2[0:1, 512 * a_:512 * (a_ + 1)],
                            rr=ksq[:, 512:1024], ia=a_:
                            E.matmul(o, ind4_t[:, ia:ia + 1], rr, start=False,
                                     stop=(g == 7)),
                            inc=("tensor" if (g == 7 and a_ == 3) else None)) or mmn
            for a_ in range(4):
                kv = pair + 2 * a_
                kq_t = V("vector", lambda E, o=kn2[0:1, kv:kv + 1],
                         a=psN2[0:1, 512 * a_:512 * (a_ + 1)]:
                         E.reduce_max(o, a, AX.X),
                         waits=([mmn] if a_ == 0 else []),
                         inc=("vector" if a_ == 3 else None)) or kq_t
        sc.flush()
    ln1.close()
    if upto <= 3:
        att.close(); top.close(); return nc

    Qt = sb("Qt", [128, 16 * QS], BF16, att)
    aot = sb("aot", [128, 16 * QS], BF16, att)
    cosQ_t = sb("cosQ_t", [128, TOWN], BF16, att)
    sinQ_t = sb("sinQ_t", [128, TOWN], BF16, att)
    sc.dma("sync", cosQ_t[:], ap["cosQ"][:])
    sc.dma("sync", sinQ_t[:], ap["sinQ"][:])

    with ExitStack() as ph:
        dscr4 = sb("dscr4", [128, 16], F32, ph)
        xnq = sb("xnq", [128, 16 * 512], BF16, ph)
        wpf = sb("wpf", [128, 16 * 128], F32, ph)
        wpb = sb("wpb", [128, 2 * 2048], BF16, ph)
        xqf = sb("xqf", [128, 2 * 512], F32, ph)
        qsq = sb("qsq", [128, 2 * 512], BF16, ph)
        PT = sb("PT", [128, 2 * 2048], BF16, ph)
        SCR = sb("SCR", [128, 2048], F32, ph)
        mskt = sb("mskt", [128, NMASK * 512], BF16, ph)
        x1t = sb("x1t", [128, 2 * 512], F32, ph)
        x1b = sb("x1b", [128, 512], BF16, ph)
        x1s = sb("x1s", [128, 512], BF16, ph)
        rtmp = sb("rtmp", [1, 2048], F32, ph)
        psA = ph.enter_context(nc.psum_tensor("psA", [128, 2048], F32))
        psB = ph.enter_context(nc.psum_tensor("psB", [128, 2048], F32))

        kq_wait = kq_t
        psA_free = None
        psB_free = None
        wpb_free = [None, None]
        wpf_free = None
        xq_free = [None, None]
        scr_free = None
        x1_free = [None, None]
        pw_free = [None, None]
        msk_free = None
        x1_store_t = None
        for slot in range(NSLOT):
            q0 = QS * slot
            nkc = SPANS[slot] // KCH
            lastq = None
            dq_tok = [None] * (NCH + 1)
            dq_tok[0] = sc.dma("sync", xqf[:, 0:512],
                               ap["xqT"][0:128, q0:q0 + 512],
                               waits=[xq_free[0], bc_t[3]])
            for ch in range(NCH):
                xr = xqf[:, (ch % 2) * 512:(ch % 2 + 1) * 512]
                if ch + 1 < NCH:
                    dq_tok[ch + 1] = sc.dma(
                        "sync",
                        xqf[:, ((ch + 1) % 2) * 512:((ch + 1) % 2 + 1) * 512],
                        ap["xqT"][128 * (ch + 1):128 * (ch + 2), q0:q0 + 512],
                        waits=[xq_free[(ch + 1) % 2]])
                else:
                    dq_tok[ch + 1] = sc.dma("sync", dscr4[:],
                                            ap["xqT"][0:128, 0:16])
                tt1 = V("vector", lambda E, o=SCR[:, 0:512], a=xr,
                        b=MBq[:, q0:q0 + 512]: E.tensor_tensor(o, a, b, OP.add),
                        waits=[dq_tok[ch], dq_tok[ch + 1],
                               scr_free if ch == 0 else None, lastq],
                        inc="vector")
                xq_free[ch % 2] = tt1
                lastq = V("vector", lambda E, o=xnq[:, 512 * ch:512 * (ch + 1)],
                          a=SCR[:, 0:512], b=RBq[:, q0:q0 + 512]:
                          E.tensor_tensor(o, a, b, OP.mult), waits=[tt1],
                          inc="vector")
            rope_done = None
            for j in range(8):
                ws = j % 2
                mmq = [None, None]
                for part, coff, ps in ((0, 128 * j, psA), (1, 1024 + 128 * j, psB)):
                    pid = coff // 128
                    d_t = sc.dma("sync", wpf[:],
                                 ap["wq"][128 * pid:128 * (pid + 1), :],
                                 waits=[wpb_free[ws], wpf_free])
                    cv = V("scalar", lambda E,
                           o=wpb[:, ws * 2048:(ws + 1) * 2048], a=wpf[:]:
                           E.copy(o, a), waits=[d_t], inc="scalar")
                    wpf_free = cv
                    for ch in range(NCH):
                        mmq[part] = V("tensor", lambda E, o=ps[:, 0:512],
                                      l=wpb[:, ws * 2048 + 128 * ch:
                                            ws * 2048 + 128 * (ch + 1)],
                                      r=xnq[:, 512 * ch:512 * (ch + 1)],
                                      st_=(ch == 0):
                                      E.matmul(o, l, r, start=st_, stop=(ch == 15)),
                                      waits=([cv, lastq,
                                              psA_free if part == 0 else psB_free]
                                             if ch == 0 else []),
                                      inc=("tensor" if ch == 15 else None)) \
                            or mmq[part]
                    wpb_free[ws] = mmq[part]
                csl = cosQ_t[:, q0:q0 + 512]
                ssl = sinQ_t[:, q0:q0 + 512]
                qe = Qt[:, 512 * j:512 * (j + 1)]
                qo = Qt[:, 512 * (j + 8):512 * (j + 9)]
                q1 = V("vector", lambda E, o=SCR[:, 0:512], a=psA[:, 0:512], b=csl:
                       E.tensor_tensor(o, a, b, OP.mult), waits=[mmq[1], lastq],
                       inc="vector")
                q2 = V("vector", lambda E, o=SCR[:, 512:1024], a=psB[:, 0:512], b=ssl:
                       E.tensor_tensor(o, a, b, OP.mult), inc="vector")
                q3 = V("vector", lambda E, o=qe, a=SCR[:, 0:512], b=SCR[:, 512:1024]:
                       E.tensor_tensor(o, a, b, OP.subtract), waits=[q1, q2],
                       inc="vector")
                q4 = V("vector", lambda E, o=SCR[:, 0:512], a=psA[:, 0:512], b=ssl:
                       E.tensor_tensor(o, a, b, OP.mult), waits=[q3], inc="vector")
                q5 = V("vector", lambda E, o=SCR[:, 512:1024], a=psB[:, 0:512], b=csl:
                       E.tensor_tensor(o, a, b, OP.mult), inc="vector")
                rope_done = V("vector", lambda E, o=qo, a=SCR[:, 0:512],
                              b=SCR[:, 512:1024]:
                              E.tensor_tensor(o, a, b, OP.add), waits=[q4, q5],
                              inc="vector")
                psA_free = rope_done
                psB_free = rope_done
            scr_free = rope_done
            qmx = None
            mmn = None
            for j in range(8):
                s1 = V("scalar", lambda E, o=qsq[:, 0:512],
                       a=Qt[:, 512 * j:512 * (j + 1)]:
                       E.activation(o, a, AF.Square),
                       waits=[rope_done, mmn] if j == 0 else [mmn],
                       inc="scalar")
                s2 = V("scalar", lambda E, o=qsq[:, 512:1024],
                       a=Qt[:, 512 * (j + 8):512 * (j + 9)]:
                       E.activation(o, a, AF.Square), inc="scalar")
                for a_ in range(4):
                    V("tensor", lambda E, o=psB[0:1, 512 * a_:512 * (a_ + 1)],
                      rr=qsq[:, 0:512], ia=a_:
                      E.matmul(o, ind4_t[:, ia:ia + 1], rr, start=True,
                               stop=False),
                      waits=([s2, psB_free] if a_ == 0 else []))
                    mmn = V("tensor", lambda E,
                            o=psB[0:1, 512 * a_:512 * (a_ + 1)],
                            rr=qsq[:, 512:1024], ia=a_:
                            E.matmul(o, ind4_t[:, ia:ia + 1], rr, start=False,
                                     stop=True),
                            inc=("tensor" if a_ == 3 else None)) or mmn
                for a_ in range(4):
                    h = j + 8 * a_
                    qmx = V("vector", lambda E, o=rtmp[0:1, 32 + h:33 + h],
                            a=psB[0:1, 512 * a_:512 * (a_ + 1)]:
                            E.reduce_max(o, a, AX.X),
                            waits=([mmn] if a_ == 0 else []),
                            inc=("vector" if a_ == 3 else None)) or qmx
                psB_free = qmx
            tr1 = V("vector", lambda E: E.reduce_max(rtmp[0:1, 64:65],
                                                     rtmp[0:1, 32:64], AX.X),
                    waits=[qmx, kq_wait], inc="vector")
            tr2 = V("vector", lambda E: E.reduce_max(rtmp[0:1, 65:66],
                                                     kn2[0:1, :], AX.X),
                    inc="vector")
            t1 = V("vector", lambda E: E.tensor_tensor(
                rtmp[0:1, 66:67], rtmp[0:1, 64:65], rtmp[0:1, 65:66], OP.mult),
                waits=[tr1, tr2], inc="vector")
            t2 = V("scalar", lambda E: E.activation(rtmp[0:1, 67:68],
                                                    rtmp[0:1, 66:67], AF.Sqrt),
                   waits=[t1], inc="scalar")
            t3 = V("scalar", lambda E: E.mul(rtmp[0:1, 68:69], rtmp[0:1, 67:68],
                                             -0.125), waits=[t2], inc="scalar")
            t4 = V("tensor", lambda E: E.matmul(psB[:, 0:1], onesr_t[:],
                                                rtmp[0:1, 68:69], start=True,
                                                stop=True),
                   waits=[t3, psB_free], inc="tensor")
            t5 = V("tensor", lambda E: E.matmul(psB[0:1, 512:1024],
                                                onesr_t[0:1, 0:1],
                                                rtmp[0:1, 0:512], start=True,
                                                stop=True),
                   waits=[t4], inc="tensor")
            t5b = V("tensor", lambda E: E.matmul(psB[0:1, 512:1024],
                                                 onesr_t[0:1, 0:1],
                                                 rtmp[0:1, 0:512], start=True,
                                                 stop=True),
                    waits=[t5], inc="tensor")
            nb_t = V("vector", lambda E: E.tensor_copy(negB[:], psB[:, 0:1]),
                     waits=[t5b], inc="vector")
            psB_free = nb_t
            mt = sc.dma("sync", mskt[:],
                        ap["maskd"][slot * 128:(slot + 1) * 128, :],
                        waits=[msk_free])
            ao_t = None
            exp_b = [psA_free] * 4
            pv_b = [[None] * 4 for _ in range(2)]
            pdep_b = [None] * 4
            psb_b = [nb_t] * 4
            for j in range(8):
                et = j // 4
                pvmm = None
                for c in range(nkc):
                    pslot = c % 2
                    pt = PT[:, pslot * 2048:(pslot + 1) * 2048]
                    qk_b = [None] * 4
                    for a_ in range(4):
                        V("tensor", lambda E,
                          l=Kt[32 * a_:32 * (a_ + 1),
                               et * S + KCH * c:et * S + KCH * (c + 1)],
                          r=Qt[32 * a_:32 * (a_ + 1), 512 * j:512 * (j + 1)],
                          o=psA[:, 512 * a_:512 * (a_ + 1)], tp=(32 * a_, 0):
                          E.matmul(o, l, r, start=True, stop=False,
                                   tile_position=tp),
                          waits=[exp_b[a_]])
                        qk_b[a_] = V("tensor", lambda E,
                                     l=Kt[32 * a_:32 * (a_ + 1),
                                          (et + 2) * S + KCH * c:
                                          (et + 2) * S + KCH * (c + 1)],
                                     r=Qt[32 * a_:32 * (a_ + 1),
                                          512 * (j + 8):512 * (j + 9)],
                                     o=psA[:, 512 * a_:512 * (a_ + 1)],
                                     tp=(32 * a_, 0):
                                     E.matmul(o, l, r, start=False, stop=True,
                                              tile_position=tp), inc="tensor")
                    for a_ in range(4):
                        exp_b[a_] = V("scalar", lambda E,
                                      o=pt[:, 512 * a_:512 * (a_ + 1)],
                                      a=psA[:, 512 * a_:512 * (a_ + 1)]:
                                      E.activation(o, a, AF.Exp, bias=negB[:],
                                                   scale=0.125),
                                      waits=[qk_b[a_],
                                             nb_t if (c == 0 and a_ == 0)
                                             else None,
                                             pv_b[pslot][a_]], inc="scalar")
                    if c >= nkc - NMASK:
                        mi = c - (nkc - NMASK)
                        for a_ in range(4):
                            pdep_b[a_] = V("vector", lambda E,
                                           o=pt[:, 512 * a_:512 * (a_ + 1)],
                                           m=mskt[:, 512 * mi:512 * (mi + 1)]:
                                           E.tensor_tensor(o, o, m, OP.mult),
                                           waits=[exp_b[a_],
                                                  mt if (mi == 0 and a_ == 0)
                                                  else None],
                                           inc="vector")
                    else:
                        for a_ in range(4):
                            pdep_b[a_] = exp_b[a_]
                    for a_ in range(4):
                        kv = _kv_of(j, a_)
                        pvmm = V("tensor", lambda E,
                                 l=VT[:, c * VTW + kv * 65:c * VTW + kv * 65 + 65],
                                 r=pt[:, 512 * a_:512 * (a_ + 1)],
                                 o=psB[0:65, 512 * a_:512 * (a_ + 1)],
                                 st_=(c == 0):
                                 E.matmul(o, l, r, start=st_, stop=(c == nkc - 1)),
                                 waits=[pdep_b[a_],
                                        psb_b[a_] if c == 0 else None],
                                 inc="tensor")
                        pv_b[pslot][a_] = pvmm
                dn = V("scalar", lambda E: E.copy(rtmp[0:1, :],
                                              psB[64:65, :]),
                       waits=[pvmm], inc="scalar")
                bcm_b = [None] * 4
                for g in range(4):
                    bcm_b[g] = V("tensor", lambda E,
                                 o=psA[:, 512 * g:512 * (g + 1)],
                                 rr=rtmp[0:1, 512 * g:512 * (g + 1)]:
                                 E.matmul(o, onesr_t[:], rr, start=True,
                                          stop=True),
                                 waits=[dn if g == 0 else None, exp_b[g]],
                                 inc="tensor")
                rc_p = None
                for g in range(4):
                    rc_g = V("vector", lambda E,
                             o=SCR[:, 512 * g:512 * (g + 1)],
                             a=psA[:, 512 * g:512 * (g + 1)]:
                             E.reciprocal(o, a),
                             waits=[bcm_b[g + 1] if g < 3 else bcm_b[3],
                                    rc_p if g == 3 else None,
                                    scr_free if g == 0 else None],
                             inc="vector")
                    if g == 2:
                        rc_p = rc_g
                    exp_b[g] = rc_g
                    t = 2 * j + (1 if g >= 2 else 0)
                    rh = 64 * (g % 2)
                    ao_t = V("vector", lambda E,
                             o=aot[rh:rh + 64, 512 * t:512 * t + 512],
                             a=psB[0:64, 512 * g:512 * (g + 1)],
                             b=SCR[0:64, 512 * g:512 * (g + 1)]:
                             E.tensor_tensor(o, a, b, OP.mult),
                             waits=[rc_g], inc="vector")
                    psb_b[g] = ao_t
                psA_free = exp_b[3]
                psB_free = ao_t
                scr_free = ao_t
            msk_free = ao_t
            st_done = None
            pend_wo = None

            def emit_wo(oc, ws, pw, mmw_tok, d2, cover, d2n):
                nonlocal st_done, x1_store_t
                xr = xqf[:, ws * 512:(ws + 1) * 512]
                xo = x1t[:, ws * 512:(ws + 1) * 512]
                ad = V("vector", lambda E, o=xo, a=pw, b=xr:
                       E.tensor_tensor(o, a, b, OP.add),
                       waits=[mmw_tok, cover, d2, d2n, x1_free[ws]],
                       inc="vector")
                xq_free[ws] = ad
                pw_free[ws] = ad
                s_t = sc.dma("gpsimd",
                             ap["x1d"][128 * oc:128 * (oc + 1), q0:q0 + 512],
                             xo, waits=[ad])
                x1_store_t = s_t
                cb = V("vector", lambda E, o=x1b[:], a=xo: E.tensor_copy(o, a),
                       waits=[st_done, ad], inc="vector")
                x1_free[ws] = s_t
                sq2 = V("scalar", lambda E, o=x1s[:], a=x1b[:]:
                        E.activation(o, a, AF.Square), waits=[cb],
                        inc="scalar")
                V("tensor", lambda E, o=psB[0:1, 1024:1536], rr=x1b[:],
                  st_=(oc == 0): E.matmul(o, onesc_t[:], rr, start=st_,
                                          stop=(oc == 15)),
                  waits=[sq2])
                st_done = V("tensor", lambda E, o=psB[0:1, 1536:2048],
                            rr=x1s[:], st_=(oc == 0):
                            E.matmul(o, onesc_t[:], rr, start=st_,
                                     stop=(oc == 15)),
                            inc=("tensor" if oc == 15 else None)) or st_done

            pend_ad = None
            for oc in range(NCH):
                ws = oc % 2
                d_t = sc.dma("sync", wpf[:],
                             ap["wo"][128 * oc:128 * (oc + 1), :],
                             waits=[wpb_free[ws], wpf_free])
                cv = V("scalar", lambda E, o=wpb[:, ws * 2048:(ws + 1) * 2048],
                       a=wpf[:]: E.copy(o, a), waits=[d_t], inc="scalar")
                wpf_free = cv
                pw = psA[:, 512 * ws:512 * (ws + 1)]
                mmw = None
                for t in range(16):
                    mmw = V("tensor", lambda E, o=pw,
                            l=wpb[:, ws * 2048 + 128 * t:ws * 2048 + 128 * (t + 1)],
                            r=aot[:, 512 * t:512 * (t + 1)], st_=(t == 0):
                            E.matmul(o, l, r, start=st_, stop=(t == 15)),
                            waits=([cv, ao_t,
                                    psA_free if oc < 2 else pw_free[ws]]
                                   if t == 0 else []),
                            inc=("tensor" if t == 15 else None)) or mmw
                wpb_free[ws] = mmw
                d2 = sc.dma("sync", xqf[:, ws * 512:(ws + 1) * 512],
                            ap["xqT"][128 * oc:128 * (oc + 1), q0:q0 + 512],
                            waits=[xq_free[ws]])
                if pend_wo is not None:
                    pend_wo(mmw, d2)
                pend_wo = (lambda cover, d2n, oc=oc, ws=ws, pw=pw, mmw=mmw,
                           d2=d2: emit_wo(oc, ws, pw, mmw, d2, cover, d2n))
            dumW = V("tensor", lambda E: E.matmul(
                psA[0:1, 1024:1536], onesr_t[0:1, 0:1], rtmp[0:1, 0:512],
                start=True, stop=True),
                waits=[wpb_free[1]], inc="tensor")
            dumWb = V("tensor", lambda E: E.matmul(
                psA[0:1, 1024:1536], onesr_t[0:1, 0:1], rtmp[0:1, 0:512],
                start=True, stop=True),
                waits=[dumW], inc="tensor")
            d2echo = sc.dma("sync", dscr4[:], ap["xqT"][0:128, 0:16])
            pend_wo(dumWb, d2echo)
            pend_wo = None
            dum3 = V("tensor", lambda E: E.matmul(
                psA[0:1, 1024:1536], onesr_t[0:1, 0:1], rtmp[0:1, 0:512],
                start=True, stop=True),
                waits=[st_done, psA_free], inc="tensor")
            dum3b = V("tensor", lambda E: E.matmul(
                psA[0:1, 1024:1536], onesr_t[0:1, 0:1], rtmp[0:1, 0:512],
                start=True, stop=True),
                waits=[dum3], inc="tensor")
            e1 = V("scalar", lambda E, o=ROWS[0:1, q0:q0 + 512],
                   a=psB[0:1, 1024:1536]: E.copy(o, a),
                   waits=[st_done, dum3b])
            e2 = V("scalar", lambda E, o=ROWS[32:33, q0:q0 + 512],
                   a=psB[0:1, 1536:2048]: E.copy(o, a), inc="scalar")
            psB_free = e2
            psA_free = e2
        x1_stores = x1_store_t
        sc.flush()
    att.close()
    if upto <= 4:
        top.close(); return nc

    ffn = ExitStack()
    RB2 = sb("RB2", [128, TOWN], BF16, ffn)
    MB2 = sb("MB2", [128, TOWN], BF16, ffn)
    with ExitStack() as ph:
        brow2 = sb("brow2", [1, 1024], F32, ph)
        tmpf2 = sb("tmpf2", [128, 2048], F32, ph)
        psBC2 = ph.enter_context(nc.psum_tensor("psBC2", [128, 1536], F32))
        bsl3b = [psBC2[:, 0:512], psBC2[:, 512:1024], psBC2[:, 1024:1536]]
        b2 = ln_bcast(0, TOWN, RB2, MB2, psA_free, bsl3b, brow2, tmpf2,
                      LNEPS)
        sc.flush()
    ln2bc = b2
    if upto <= 5:
        ffn.close(); top.close(); return nc

    out_done = None
    with ExitStack() as ph:
        xn2 = sb("xn2", [128, 16 * HB], BF16, ph)
        hbuf = sb("hbuf", [128, NIC * HB], BF16, ph)
        wstu = sb("wstu", [128, NIC * 256], BF16, ph)
        wst = wstu
        wdst = wstu
        byt = sb("byt", [128, 4 * 256], U16, ph)
        sgb = sb("sgb", [128, 4 * 256], F16, ph)
        dqA = sb("dqA", [128, 256], U16, ph)
        dqB = sb("dqB", [128, 256], U16, ph)
        dqU = sb("dqU", [128, 2 * 256], U16, ph)
        sg = sb("sg", [128, 2 * HB], BF16, ph)
        ysq = sb("ysq", [128, 512], BF16, ph)
        rb3 = sb("rb3", [128, HB], F32, ph)
        xff = sb("xff", [128, 2 * HB], F32, ph)
        oscr = sb("oscr", [128, HB], F32, ph)
        rtmp = sb("rtmp3", [1, HB], F32, ph)
        brow3 = sb("brow3", [1, 1024], F32, ph)
        dscr = sb("dscr", [128, 16], F32, ph)
        psG = ph.enter_context(nc.psum_tensor("psG", [128, 2 * 512], F32))
        psD = ph.enter_context(nc.psum_tensor("psD", [128, 4 * 512], F32))
        psR = ph.enter_context(nc.psum_tensor("psR", [1, 1024], F32))

        rb3_ev = None
        byt_free = [None] * 4
        dq_free = None
        down_prev = None
        xf_free = [None, None]
        os_free = [None, None]
        y_free = None
        hb_free = None

        dqp = {"u_lo": None, "u_hi": None}

        def dequant(bt_ap, scb_ap, out_lo, out_hi, w8, wfree):
            a1 = V("vector", lambda E, b=bt_ap: E.tensor_scalar(
                dqA[:], b, 0x07, 9, OP.bitwise_and, OP.logical_shift_left),
                waits=list(w8) + [dq_free, wfree], inc="vector")
            a2 = V("vector", lambda E, b=bt_ap: E.tensor_scalar(
                dqB[:], b, 0x08, 12, OP.bitwise_and, OP.logical_shift_left),
                inc="vector")
            v3 = V("vector", lambda E: E.tensor_tensor(
                dqU[:, 0:256], dqA[:], dqB[:], OP.bitwise_or),
                waits=[a1, a2, dqp["u_lo"]], inc="vector")
            a4 = V("vector", lambda E, b=bt_ap: E.tensor_scalar(
                dqA[:], b, 0x70, 5, OP.bitwise_and, OP.logical_shift_left),
                waits=[v3], inc="vector")
            a5 = V("vector", lambda E, b=bt_ap: E.tensor_scalar(
                dqB[:], b, 0x80, 8, OP.bitwise_and, OP.logical_shift_left),
                inc="vector")
            t = V("vector", lambda E: E.tensor_tensor(
                dqU[:, 256:512], dqA[:], dqB[:], OP.bitwise_or),
                waits=[a4, a5, dqp["u_hi"]], inc="vector")
            v7 = V("vector", lambda E, o=out_lo, s_=scb_ap: E.tensor_tensor(
                o, dqU[:, 0:256].bitcast(F16), s_, OP.mult),
                waits=[v3], inc="vector")
            t2 = V("vector", lambda E, o=out_hi, s_=scb_ap: E.tensor_tensor(
                o, dqU[:, 256:512].bitcast(F16), s_, OP.mult),
                waits=[t], inc="vector")
            dqp["u_lo"] = v7
            dqp["u_hi"] = t2
            return t, t2

        for hf in range(2 if fparts >= 19 else 1):
            t0 = HB * hf
            lastx = None
            d_tok = [None] * (NCH + 1)
            d_tok[0] = sc.dma("sync", xff[:, 0:HB],
                              ap["x1d"][0:128, t0:t0 + HB],
                              waits=[xf_free[0], x1_stores,
                                     ln2bc if hf == 0 else None,
                                     y_free if hf == 1 else None])
            for ch in range(NCH):
                xr = xff[:, (ch % 2) * HB:(ch % 2 + 1) * HB]
                osl = oscr[:, 0:HB]
                if ch + 1 < NCH:
                    d_tok[ch + 1] = sc.dma(
                        "sync", xff[:, ((ch + 1) % 2) * HB:((ch + 1) % 2 + 1) * HB],
                        ap["x1d"][128 * (ch + 1):128 * (ch + 2), t0:t0 + HB],
                        waits=[xf_free[(ch + 1) % 2]])
                else:
                    d_tok[ch + 1] = sc.dma("sync", dscr[:],
                                           ap["x1d"][0:128, 0:16])
                tt1 = V("vector", lambda E, o=osl, a=xr, b=MB2[:, t0:t0 + HB]:
                        E.tensor_tensor(o, a, b, OP.add),
                        waits=[d_tok[ch], d_tok[ch + 1],
                               os_free[0] if ch == 0 else None,
                               lastx],
                        inc="vector")
                xf_free[ch % 2] = tt1
                lastx = V("vector", lambda E, o=xn2[:, HB * ch:HB * (ch + 1)],
                          a=osl, b=RB2[:, t0:t0 + HB]:
                          E.tensor_tensor(o, a, b, OP.mult),
                          waits=[y_free if (hf == 1 and ch == 0) else None,
                                 tt1],
                          inc="vector")
            os_free[0] = lastx
            if fparts <= 1:
                break
            sg_t = [None, None]
            hb_t = None
            psG_free = [rb3_ev, rb3_ev]
            wst_free = [down_prev, down_prev]
            hb_free = down_prev
            pend_ev = None

            def emit_sg(oc, tt_, gsl, pg, grp, nxt):
                sg_t[oc] = V("scalar", lambda E,
                             o=sg[:, HB * oc + 512 * tt_:
                                  HB * oc + 512 * (tt_ + 1)], a=pg:
                             E.activation(o, a, AF.Silu, scale=S14),
                             waits=[grp, nxt], inc="scalar")
                psG_free[gsl] = sg_t[oc]

            def emit_hb(obi, oc, tt_, gsl, pg, grp, nxt):
                nonlocal hb_t
                ic = 2 * obi + oc
                hb_t = V("vector", lambda E,
                         o=hbuf[:, HB * ic + 512 * tt_:
                               HB * ic + 512 * (tt_ + 1)], a=pg,
                         b=sg[:, HB * oc + 512 * tt_:
                              HB * oc + 512 * (tt_ + 1)]:
                         E.tensor_tensor(o, a, b, OP.mult),
                         waits=[grp, nxt, sg_t[oc], hb_free], inc="vector")
                psG_free[gsl] = hb_t
            for obi in range(NOB):
                for gu in range(2):
                    wsl = gu
                    wsp = wst[:, wsl * 4096:(wsl + 1) * 4096]
                    col0 = II * gu + 256 * obi
                    dq_t = None
                    dqj = []
                    for jj in range(8):
                        bsl = jj % 4
                        bt_ap = byt[:, 256 * bsl:256 * (bsl + 1)]
                        scb = sgb[:, 256 * bsl:256 * (bsl + 1)]
                        d1 = sc.dma("sync", bt_ap,
                                    ap["bgu"][128 * jj:128 * (jj + 1),
                                              col0:col0 + 256],
                                    waits=[byt_free[bsl]])
                        d2 = sc.dma("sync", scb[:, :],
                                    ap["sgu"][128 * jj:128 * (jj + 1),
                                              col0:col0 + 256])
                        _, dq_t = dequant(
                            bt_ap, scb,
                            wsp[:, 256 * (2 * jj):256 * (2 * jj + 1)],
                            wsp[:, 256 * (2 * jj + 1):256 * (2 * jj + 2)],
                            [d1, d2], wst_free[wsl] if jj == 0 else None)
                        byt_free[bsl] = dq_t
                        dqj.append(dq_t)
                    dq_free = dq_t
                    mmg = [None, None]
                    for oc in range(2):
                        for tt_ in range(2):
                            gsl = tt_
                            pg = psG[:, 512 * gsl:512 * (gsl + 1)]
                            for chh in range(16):
                                w0 = []
                                if chh == 0:
                                    w0 = [dqj[0], lastx, psG_free[gsl]]
                                elif chh % 2 == 0:
                                    w0 = [dqj[chh // 2]]
                                mmg[gsl] = V("tensor", lambda E, o=pg,
                                             l=wsp[:, 256 * chh + 128 * oc:
                                                   256 * chh + 128 * oc + 128],
                                             r=xn2[:, HB * chh + 512 * tt_:
                                                   HB * chh + 512 * (tt_ + 1)],
                                             st_=(chh == 0):
                                             E.matmul(o, l, r, start=st_,
                                                      stop=(chh == 15)),
                                             waits=w0,
                                             inc=("tensor" if chh == 15
                                                  else None)) or mmg[gsl]
                            if pend_ev is not None:
                                pend_ev(mmg[gsl])
                            grp = mmg[gsl]
                            if gu == 0:
                                pend_ev = (lambda nxt, oc=oc, tt_=tt_,
                                           gsl=gsl, pg=pg, grp=grp:
                                           emit_sg(oc, tt_, gsl, pg, grp,
                                                   nxt))
                            else:
                                pend_ev = (lambda nxt, obi=obi, oc=oc,
                                           tt_=tt_, gsl=gsl, pg=pg, grp=grp:
                                           emit_hb(obi, oc, tt_, gsl, pg,
                                                   grp, nxt))
                    wst_free[wsl] = mmg[0] if mmg[1] is None else mmg[1]
            dum = V("tensor", lambda E: E.matmul(
                psG[0:1, 0:512], onesr_t[0:1, 0:1], brow3[0:1, 0:512],
                start=True, stop=True),
                waits=[wst_free[1], psG_free[0]], inc="tensor")
            dum_b = V("tensor", lambda E: E.matmul(
                psG[0:1, 0:512], onesr_t[0:1, 0:1], brow3[0:1, 0:512],
                start=True, stop=True),
                waits=[dum], inc="tensor")
            pend_ev(dum_b)
            pend_ev = None
            if fparts <= 2:
                break
            n_hg = (fparts - 10) if 10 < fparts < 19 else (1 if fparts >= 30 else 8)
            psD_free = [None] * 4
            rms_t = None
            y_t = None
            wdst_free = [hb_t, hb_t]
            for hg in range(n_hg):
                hc0 = 256 * hg
                wd = wdst[:, 0:NIC * 256]
                dq_t = None
                dqj_dn = []
                for ip in range(NOB):
                    bsl = ip % 4
                    bt_ap = byt[:, 256 * bsl:256 * (bsl + 1)]
                    scb = sgb[:, 256 * bsl:256 * (bsl + 1)]
                    d1 = sc.dma("sync", bt_ap,
                           ap["bdn"][128 * ip:128 * (ip + 1), hc0:hc0 + 256],
                           waits=[byt_free[bsl]])
                    d2 = sc.dma("sync", scb[:, :],
                                ap["sdn"][128 * ip:128 * (ip + 1),
                                          hc0:hc0 + 256])
                    _, dq_t = dequant(
                        bt_ap, scb,
                        wd[:, 256 * (2 * ip):256 * (2 * ip + 1)],
                        wd[:, 256 * (2 * ip + 1):256 * (2 * ip + 2)],
                        [d1, d2], wdst_free[0] if ip == 0 else None)
                    byt_free[bsl] = dq_t
                    dqj_dn.append(dq_t)
                dq_free = dq_t
                mmd = None
                for ic in range(NIC):
                    for hhc in range(2):
                        for tt_ in range(2):
                            bidx = 2 * hhc + tt_
                            pd = psD[:, 512 * bidx:512 * (bidx + 1)]
                            w0 = []
                            if ic == 0:
                                w0 = [dqj_dn[0], hb_t]
                                if psD_free[bidx] is not None:
                                    w0 += psD_free[bidx]
                            elif ic % 2 == 0 and hhc == 0 and tt_ == 0:
                                w0 = [dqj_dn[ic // 2]]
                            mmd = V("tensor", lambda E, o=pd,
                                    l=wd[:, 256 * ic + 128 * hhc:
                                         256 * ic + 128 * hhc + 128],
                                    r=hbuf[:, HB * ic + 512 * tt_:
                                           HB * ic + 512 * (tt_ + 1)],
                                    st_=(ic == 0):
                                    E.matmul(o, l, r, start=st_,
                                             stop=(ic == NIC - 1)),
                                    waits=w0,
                                    inc=("tensor" if (ic == NIC - 1)
                                         else None)) or mmd
                wdst_free[0] = mmd
                for hhc in range(2):
                    hh = 2 * hg + hhc
                    for tt_ in range(2):
                        bidx = 2 * hhc + tt_
                        pd = psD[:, 512 * bidx:512 * (bidx + 1)]
                        y_t = V("vector", lambda E,
                                o=xn2[:, HB * hh + 512 * tt_:
                                      HB * hh + 512 * (tt_ + 1)],
                                a=pd: E.tensor_copy(o, a),
                                waits=[mmd, lastx if hg == 0 else None]
                                if (hhc == 0 and tt_ == 0) else [mmd],
                                inc="vector")
                        sqy = V("scalar", lambda E, o=ysq[:],
                                a=xn2[:, HB * hh + 512 * tt_:
                                      HB * hh + 512 * (tt_ + 1)]:
                                E.activation(o, a, AF.Square, scale=S28),
                                waits=[y_t, rms_t], inc="scalar")
                        rms_t = V("tensor", lambda E,
                                  o=psR[0:1, 512 * tt_:512 * (tt_ + 1)],
                                  st_=(hg == 0 and hhc == 0):
                                  E.matmul(o, onesc_t[:], ysq[:], start=st_,
                                           stop=(hg == n_hg - 1 and hhc == 1)),
                                  waits=[sqy], inc="tensor")
                        psD_free[bidx] = [y_t, y_t]
            if fparts <= 3:
                break
            dum2 = V("tensor", lambda E: E.matmul(
                psG[0:1, 0:512], onesr_t[0:1, 0:1], brow3[0:1, 0:512],
                start=True, stop=True),
                waits=[rms_t], inc="tensor")
            dum2b = V("tensor", lambda E: E.matmul(
                psG[0:1, 0:512], onesr_t[0:1, 0:1], brow3[0:1, 0:512],
                start=True, stop=True),
                waits=[dum2], inc="tensor")
            ev = V("scalar", lambda E: E.copy(rtmp[0:1, :], psR[0:1, :]),
                   waits=[rms_t, dum2b], inc="scalar")
            b4r = mm_bcast(lambda c0: rtmp[0:1, c0:c0 + 512], HB, oscr, ev,
                           [psG[:, 0:512], psG[:, 512:1024], psD[:, 0:512]],
                           brow3, free0=hb_t)
            b4 = None
            for h2 in range(2):
                tsq = V("scalar", lambda E, o=rb3[:, 512 * h2:512 * (h2 + 1)],
                        a=oscr[:, 512 * h2:512 * (h2 + 1)]:
                        E.activation(o, a, AF.Sqrt, scale=1.0 / D,
                                     bias=RMSEPS),
                        waits=[b4r, os_free[0] if h2 == 0 else None],
                        inc="scalar")
                trc = V("vector", lambda E, o=oscr[:, 512 * h2:512 * (h2 + 1)],
                        a=rb3[:, 512 * h2:512 * (h2 + 1)]:
                        E.reciprocal(o, a), waits=[tsq], inc="vector")
                b4 = V("vector", lambda E, o=rb3[:, 512 * h2:512 * (h2 + 1)],
                       a=oscr[:, 512 * h2:512 * (h2 + 1)]:
                       E.tensor_scalar(o, a, S28, None, OP.mult),
                       waits=[trc, b4], inc="vector")
            rb3_ev = b4
            fin = None
            d_tok = [None] * (NCH + 1)
            d_tok[0] = sc.dma("sync", xff[:, 0:HB],
                              ap["x1d"][0:128, t0:t0 + HB],
                              waits=[xf_free[0]])
            for ch in range(NCH):
                xr = xff[:, (ch % 2) * HB:(ch % 2 + 1) * HB]
                if ch + 1 < NCH:
                    d_tok[ch + 1] = sc.dma(
                        "sync", xff[:, ((ch + 1) % 2) * HB:((ch + 1) % 2 + 1) * HB],
                        ap["x1d"][128 * (ch + 1):128 * (ch + 2), t0:t0 + HB],
                        waits=[xf_free[(ch + 1) % 2]])
                else:
                    d_tok[ch + 1] = sc.dma("sync", dscr[:],
                                           ap["x1d"][0:128, 0:16])
                osl = oscr[:, 0:HB]
                m1 = V("vector", lambda E, o=osl,
                       a=xn2[:, HB * ch:HB * (ch + 1)], b=rb3[:]:
                       E.tensor_tensor(o, a, b, OP.mult),
                       waits=[b4, y_t if ch == 0 else None,
                              os_free[0]], inc="vector")
                a1 = V("vector", lambda E, o=osl, a=osl, b=xr:
                       E.tensor_tensor(o, a, b, OP.add),
                       waits=[d_tok[ch], d_tok[ch + 1], m1], inc="vector")
                xf_free[ch % 2] = a1
                s_t = sc.dma("gpsimd",
                             ap["outT"][128 * ch:128 * (ch + 1), t0:t0 + HB],
                             osl, waits=[a1])
                os_free[0] = s_t
                fin = s_t
            y_free = fin
            down_prev = wdst_free[0]
            out_done = fin
        if out_done is not None:
            V("gpsimd", lambda E: E.memset(dqA[0:1, 0:1], 0),
              waits=[out_done])
        sc.flush()
    ffn.close()
    top.close()
    return nc


_PROG = None


def _get_prog():
    global _PROG
    if _PROG is None:
        _PROG = build_program()
    return _PROG


def kernel(L0_attn_x, L0_attn_Wq, L0_attn_Wk, L0_attn_Wv, L0_attn_Wo,
           L0_ffn_gate_up_packed, L0_ffn_gate_up_scales,
           L0_ffn_down_packed, L0_ffn_down_scales):
    x = np.ascontiguousarray(np.asarray(L0_attn_x, dtype=np.float32))
    Wq = np.asarray(L0_attn_Wq, dtype=np.float32)
    Wk = np.asarray(L0_attn_Wk, dtype=np.float32)
    Wv = np.asarray(L0_attn_Wv, dtype=np.float32)
    Wo = np.asarray(L0_attn_Wo, dtype=np.float32)
    gup = np.asarray(L0_ffn_gate_up_packed)
    gus = np.asarray(L0_ffn_gate_up_scales)
    dnp_ = np.asarray(L0_ffn_down_packed)
    dns = np.asarray(L0_ffn_down_scales)

    dperm = _perm256(D)
    iperm = _perm256(II)
    qcp = _qcol_perm()
    kcp = _kcol_perm()
    worp = _wo_row_perm()

    def _panelize(w):
        K, N = w.shape
        npan = N // 128
        out = np.empty((npan * 128, K), dtype=w.dtype)
        for j in range(npan):
            out[128 * j:128 * (j + 1)] = (
                w[:, 128 * j:128 * (j + 1)].reshape(K // 128, 128, 128)
                .transpose(1, 0, 2).reshape(128, K))
        return np.ascontiguousarray(out)

    def _rowstage(w):
        out = np.empty((512, 2048), dtype=w.dtype)
        for rd in range(4):
            out[128 * rd:128 * (rd + 1)] = (
                w.reshape(16, 128, 512)[4 * rd:4 * rd + 4]
                .transpose(1, 0, 2).reshape(128, 2048))
        return np.ascontiguousarray(out)

    wq_h = _panelize(np.ascontiguousarray(Wq[dperm][:, qcp]))
    wk_h = _rowstage(np.ascontiguousarray(Wk[dperm][:, kcp]))
    wv_h = _rowstage(np.ascontiguousarray(Wv[dperm]))
    wo_h = _panelize(np.ascontiguousarray(Wo[worp][:, dperm]))
    gub = np.ascontiguousarray(
        gup.view(np.uint8).reshape(2 * II, 1024).T).astype(np.uint16)
    icolp = np.concatenate([iperm, II + iperm])
    gub = np.ascontiguousarray(gub[:, icolp])
    sgu_h = np.ascontiguousarray(
        np.repeat(np.asarray(gus, dtype=np.float16)[:, icolp], 64, axis=0))
    dnb = np.ascontiguousarray(
        dnp_.view(np.uint8).reshape(D, 2816).T).astype(np.uint16)
    dnb = np.ascontiguousarray(dnb[:, dperm])
    sdn_h = np.ascontiguousarray(
        np.repeat(np.asarray(dns, dtype=np.float16)[:, dperm], 64, axis=0))

    half = HD // 2
    invf = 1.0 / 10000.0 ** (np.arange(half, dtype=np.float64) / half)
    pos = np.arange(S, dtype=np.float64)
    ang = np.outer(invf, pos)
    cosf = np.cos(ang).astype(np.float32)
    sinf = np.sin(ang).astype(np.float32)
    cosT_h = np.ascontiguousarray(np.tile(cosf, (4, 1)).astype(bf16))
    sinT_h = np.ascontiguousarray(np.tile(sinf, (4, 1)).astype(bf16))

    identb_h = np.eye(128, dtype=bf16)
    onesr_h = np.ones((1, 128), dtype=np.float32)
    onesc_h = np.ones((128, 1), dtype=bf16)
    ind4_h = np.zeros((128, 4), dtype=np.float32)
    for a in range(4):
        ind4_h[32 * a:32 * (a + 1), a] = 1.0
    ind4_h = ind4_h.astype(bf16)

    in_maps = []
    blocksets = [(0, 2, 4, 6), (1, 3, 5, 7)]
    for core in range(8):
        b = core // 2
        blocks = blocksets[core % 2]
        ownpos = np.concatenate([np.arange(QS * blk, QS * (blk + 1))
                                 for blk in blocks])
        xTb = np.ascontiguousarray(x[b].T[dperm])
        xq = np.ascontiguousarray(xTb[:, ownpos])
        cosQ_h = np.ascontiguousarray(cosT_h[:, ownpos])
        sinQ_h = np.ascontiguousarray(sinT_h[:, ownpos])
        mk = np.zeros((NSLOT, NMASK, 128, QS), dtype=np.float32)
        for s_i, blk in enumerate(blocks):
            nkc = SPANS[s_i] // KCH
            qpos = np.arange(QS * blk, QS * (blk + 1))
            for mi in range(NMASK):
                c = nkc - NMASK + mi
                kpos = np.arange(KCH * c, KCH * (c + 1))
                mk[s_i, mi] = (kpos[:, None] <= qpos[None, :])
        maskd_h = np.ascontiguousarray(
            mk.transpose(0, 2, 1, 3).reshape(NSLOT * 128, NMASK * QS)
            .astype(bf16))
        in_maps.append({
            "xT": xTb, "xqT": xq, "wq": wq_h, "wk": wk_h, "wv": wv_h,
            "wo": wo_h, "bgu": gub, "bdn": dnb, "sgu": sgu_h, "sdn": sdn_h,
            "cosT": cosT_h, "sinT": sinT_h, "cosQ": cosQ_h, "sinQ": sinQ_h,
            "maskd": maskd_h, "identb": identb_h, "onesr": onesr_h,
            "onesc": onesc_h, "ind4": ind4_h,
        })

    nc = _get_prog()
    res = run_bass_kernel_spmd(nc, in_maps, core_ids=list(range(8)))

    out = np.empty((NB, S, D), dtype=np.float32)
    inv = np.empty_like(dperm)
    inv[dperm] = np.arange(D)
    for core in range(8):
        b = core // 2
        blocks = blocksets[core % 2]
        ownpos = np.concatenate([np.arange(QS * blk, QS * (blk + 1))
                                 for blk in blocks])
        oT = res.results[core]["outT"]
        out[b, ownpos, :] = oT[inv].T
    return out


# version 19
# speedup vs baseline: 1.2688x; 1.1321x over previous
import numpy as np
import ml_dtypes
from contextlib import ExitStack

import concourse.bass as bass
import concourse.mybir as mybir
from concourse.bass_utils import run_bass_kernel_spmd

dt = mybir.dt
OP = mybir.AluOpType
AF = mybir.ActivationFunctionType
AX = mybir.AxisListType
F32, BF16, F16, U16 = dt.float32, dt.bfloat16, dt.float16, dt.uint16
bf16 = ml_dtypes.bfloat16

D, S, NB = 2048, 4096, 4
NH, NKV, HD = 32, 8, 64
II = 5632
NCH = 16
TOWN = 2048
NSLOT, QS, KCH = 4, 512, 128
SPANS = (1024, 2048, 3072, 4096)
NMASK = 8
LNEPS, RMSEPS = 1e-5, 1e-6
S14, S28 = float(2.0 ** 14), float(2.0 ** 28)
NIC, NOB = 44, 22
VTW = 520
HB = 1024

ENGS = ("sync", "tensor", "vector", "scalar", "gpsimd")


class Sched:

    def __init__(self, nc, stack):
        self.nc = nc
        self.q = {e: [] for e in ENGS}
        self.sem = {}
        self.stack = stack
        for e in ENGS:
            h = stack.enter_context(nc.semaphore("sem_" + e))
            self.sem[e] = [h, 0]
        h = stack.enter_context(nc.semaphore("sem_dma"))
        self.sem["dma"] = [h, 0]
        h = stack.enter_context(nc.semaphore("sem_dmg"))
        self.sem["dmg"] = [h, 0]

    def step(self, eng, emit, waits=(), inc=None):
        wl = []
        for w in waits:
            if w is None:
                continue
            s, v = w
            if v > 0:
                wl.append((self.sem[s][0], v))
        tok = None
        ih = None
        if inc is not None:
            rec = self.sem[inc]
            rec[1] += 1
            ih, tok = rec[0], (inc, rec[1])
        self.q[eng].append((wl, emit, ih, 1))
        return tok

    def dma(self, eng, out, in_, waits=()):
        semn = "dmg" if eng == "gpsimd" else "dma"
        wl = []
        for w in waits:
            if w is None:
                continue
            s, v = w
            if v > 0:
                wl.append((self.sem[s][0], v))
        rec = self.sem[semn]
        rec[1] += 16
        self.q[eng].append(
            (wl, lambda E, o=out, i=in_: E.dma_start(out=o, in_=i), rec[0], 16))
        return (semn, rec[1])

    def flush(self):
        qs, self.q = self.q, {e: [] for e in ENGS}
        with self.nc.Block() as blk:
            for ename in ENGS:
                items = qs[ename]
                if not items:
                    continue

                def body(E, items=items):
                    for wl, emit, ih, iv in items:
                        for (h, v) in wl:
                            E.wait_ge(h, v)
                        inst = emit(E)
                        if ih is not None:
                            inst.then_inc(ih, iv)
                getattr(blk, ename)(body)


def _perm256(n):
    p = []
    for s0 in range(0, n, 256):
        p += list(range(s0, s0 + 256, 2)) + list(range(s0 + 1, s0 + 256, 2))
    return np.asarray(p, dtype=np.int64)


def _qcol_perm():
    e, o = [], []
    for j in range(8):
        for h in (j, j + 8, j + 16, j + 24):
            e += [64 * h + 2 * f for f in range(32)]
            o += [64 * h + 2 * f + 1 for f in range(32)]
    return np.asarray(e + o, dtype=np.int64)


def _kcol_perm():
    e, o = [], []
    for tile in ((0, 2, 4, 6), (1, 3, 5, 7)):
        for kv in tile:
            e += [64 * kv + 2 * f for f in range(32)]
            o += [64 * kv + 2 * f + 1 for f in range(32)]
    return np.asarray(e + o, dtype=np.int64)


def _wo_row_perm():
    rows = []
    for j in range(8):
        for h in (j, j + 8):
            rows += list(range(64 * h, 64 * h + 64))
        for h in (j + 16, j + 24):
            rows += list(range(64 * h, 64 * h + 64))
    return np.asarray(rows, dtype=np.int64)


def _kv_of(j, a):
    return (j // 4) + 2 * a


def build_program(upto=99, fparts=19):
    nc = bass.Bass("TRN2", target_bir_lowering=False)
    top = ExitStack()
    sc = Sched(nc, top)

    ap = {}
    for name, shape, dty, kind in [
        ("xT", [D, S], F32, "ExternalInput"),
        ("xqT", [D, TOWN], F32, "ExternalInput"),
        ("wq", [D, D], F32, "ExternalInput"),
        ("wk", [512, 2048], F32, "ExternalInput"),
        ("wv", [512, 2048], F32, "ExternalInput"),
        ("wo", [D, D], F32, "ExternalInput"),
        ("bgu", [1024, 2 * II], U16, "ExternalInput"),
        ("bdn", [2816, D], U16, "ExternalInput"),
        ("sgu", [1024, 2 * II], F16, "ExternalInput"),
        ("sdn", [2816, D], F16, "ExternalInput"),
        ("cosT", [128, S], BF16, "ExternalInput"),
        ("sinT", [128, S], BF16, "ExternalInput"),
        ("cosQ", [128, TOWN], BF16, "ExternalInput"),
        ("sinQ", [128, TOWN], BF16, "ExternalInput"),
        ("maskd", [NSLOT * 128, NMASK * QS], BF16, "ExternalInput"),
        ("identb", [128, 128], BF16, "ExternalInput"),
        ("onesr", [1, 128], F32, "ExternalInput"),
        ("onesc", [128, 1], BF16, "ExternalInput"),
        ("ind4", [128, 4], BF16, "ExternalInput"),
        ("outT", [D, TOWN], F32, "ExternalOutput"),
        ("x1d", [D, TOWN], F32, "Internal"),
    ]:
        ap[name] = nc.dram_tensor(name, shape, dty, kind=kind).ap()

    def sb(name, shape, dty, stk):
        return stk.enter_context(nc.sbuf_tensor(name, shape, dty))

    identb_t = sb("identb_t", [128, 128], BF16, top)
    epst = sb("epst", [128, 2], F32, top)
    onesr_t = sb("onesr_t", [1, 128], F32, top)
    onesc_t = sb("onesc_t", [128, 1], BF16, top)
    ind4_t = sb("ind4_t", [128, 4], BF16, top)
    negB = sb("negB", [128, 1], F32, top)
    kn2 = sb("kn2", [1, NKV], F32, top)
    ROWS = sb("ROWS", [128, S], F32, top)

    V = sc.step

    for nm, dst in (("identb", identb_t), ("onesr", onesr_t),
                    ("onesc", onesc_t), ("ind4", ind4_t)):
        tok = sc.dma("sync", dst[:], ap[nm][:])
    V("vector", lambda E: E.memset(kn2[:], 0.0), waits=[tok], inc="vector")
    V("vector", lambda E: E.memset(epst[:, 0:1], LNEPS))
    V("vector", lambda E: E.memset(epst[:, 1:2], RMSEPS))
    nc.const_aps.register(F32, LNEPS, epst[:, 0:1])
    nc.const_aps.register(F32, RMSEPS, epst[:, 1:2])
    sc.flush()

    with ExitStack() as ph:
        p1x = sb("p1x", [128, 2 * 2048], F32, ph)
        p1b = sb("p1b", [128, 2 * 2048], BF16, ph)
        p1s = sb("p1s", [128, 2 * 2048], BF16, ph)
        p1ps = ph.enter_context(nc.psum_tensor("p1ps", [1, 4096], F32))
        rounds = [(ap["xT"], 0, 0, 0), (ap["xT"], 2048, 0, 2048),
                  (ap["xqT"], 0, 64, 0)]
        w = 2048
        cp_t = [None] * 48
        sq_t = [None] * 48
        mm_t = [None] * 48
        ev_t = None
        i = 0
        for r, (src, c0, rbase, rcol) in enumerate(rounds):
            for ch in range(NCH):
                sl = i % 2
                xr = p1x[:, sl * 2048:(sl + 1) * 2048]
                xb = p1b[:, sl * 2048:(sl + 1) * 2048]
                xq2 = p1s[:, sl * 2048:(sl + 1) * 2048]
                d_t = sc.dma("sync", xr, src[128 * ch:128 * (ch + 1), c0:c0 + w],
                             waits=[cp_t[i - 2] if i >= 2 else None])
                cp_t[i] = V("vector", lambda E, o=xb, a=xr: E.tensor_copy(o, a),
                            waits=[d_t, sq_t[i - 2] if i >= 2 else None,
                                   mm_t[i - 2] if i >= 2 else None],
                            inc="vector")
                sq_t[i] = V("scalar", lambda E, o=xq2, a=xb:
                            E.activation(o, a, AF.Square),
                            waits=[cp_t[i], mm_t[i - 2] if i >= 2 else None],
                            inc="scalar")
                for g in range(4):
                    V("tensor", lambda E, o=p1ps[0:1, 1024 * g:1024 * g + 512],
                      rr=xb[:, 512 * g:512 * (g + 1)], st_=(ch == 0):
                      E.matmul(o, onesc_t[:], rr, start=st_, stop=(ch == 15)),
                      waits=([sq_t[i], ev_t] if g == 0 else []))
                    lastg = (g == 3)
                    mm_t[i] = V("tensor",
                                lambda E, o=p1ps[0:1, 1024 * g + 512:1024 * (g + 1)],
                                rr=xq2[:, 512 * g:512 * (g + 1)], st_=(ch == 0):
                                E.matmul(o, onesc_t[:], rr, start=st_,
                                         stop=(ch == 15)),
                                inc=("tensor" if lastg else None)) or mm_t[i]
                i += 1
            for g in range(4):
                V("scalar", lambda E,
                  o=ROWS[rbase:rbase + 1, rcol + 512 * g:rcol + 512 * (g + 1)],
                  a=p1ps[0:1, 1024 * g:1024 * g + 512]:
                  E.copy(o, a), waits=([mm_t[i - 1]] if g == 0 else []))
                ev_t = V("scalar", lambda E,
                         o=ROWS[rbase + 32:rbase + 33,
                                rcol + 512 * g:rcol + 512 * (g + 1)],
                         a=p1ps[0:1, 1024 * g + 512:1024 * (g + 1)]:
                         E.copy(o, a), inc="scalar")
        sc.flush()
    if upto <= 1:
        top.close(); return nc

    att = ExitStack()
    RBq = sb("RBq", [128, TOWN], BF16, att)
    MBq = sb("MBq", [128, TOWN], BF16, att)
    Kt = sb("Kt", [128, 4 * S], BF16, att)
    VT = sb("VT", [128, (S // KCH) * VTW], BF16, att)
    ln1 = ExitStack()
    RB1 = sb("RB1", [128, S], BF16, ln1)
    MB1 = sb("MB1", [128, S], BF16, ln1)

    def ln_rowmath(base, w, tmp, eps, w0=None):
        last = None
        for c0 in range(0, w, 2048):
            cw = min(2048, w - c0)
            su = ROWS[base:base + 1, c0:c0 + cw]
            sq = ROWS[base + 32:base + 33, c0:c0 + cw]
            A = tmp[0:1, 0:cw]
            Bv = tmp[0:1, 2048:2048 + cw]
            C = tmp[0:1, 4096:4096 + cw]
            tk = V("vector", lambda E, a=A, s_=sq: E.tensor_copy(a, s_),
                   waits=[last, w0 if last is None else None], inc="vector")
            tk = V("vector", lambda E, b=Bv, s_=su: E.tensor_copy(b, s_),
                   waits=[tk], inc="vector")
            tk = V("vector", lambda E, a=A: E.tensor_scalar(a, a, 1.0 / D, None,
                                                            OP.mult),
                   waits=[tk], inc="vector")
            tk = V("vector", lambda E, b=Bv: E.tensor_scalar(b, b, 1.0 / D, None,
                                                             OP.mult),
                   waits=[tk], inc="vector")
            tk = V("vector", lambda E, c=C, b=Bv: E.tensor_tensor(c, b, b,
                                                                  OP.mult),
                   waits=[tk], inc="vector")
            t1 = V("vector", lambda E, a=A, c=C:
                   E.tensor_tensor(a, a, c, OP.subtract), waits=[tk],
                   inc="vector")
            t2 = V("scalar", lambda E, c=C, a=A:
                   E.activation(c, a, AF.Sqrt, bias=eps), waits=[t1],
                   inc="scalar")
            tk = V("vector", lambda E, a=A, c=C: E.reciprocal(a, c),
                   waits=[t2], inc="vector")
            tk = V("vector", lambda E, s_=su, a=A: E.tensor_copy(s_, a),
                   waits=[tk], inc="vector")
            tk = V("vector", lambda E, c=C, b=Bv: E.tensor_scalar(c, b, -1.0,
                                                                  None,
                                                                  OP.mult),
                   waits=[tk], inc="vector")
            last = V("vector", lambda E, s_=sq, c=C: E.tensor_copy(s_, c),
                     waits=[tk], inc="vector")
        return last

    def ln_bcast(base, w, RB, MB, wt, slots, brow, tmp, eps, free0=None):
        n = w // 512
        mm = [None] * (2 * n + 1)
        ev3 = [free0] * 3
        last = None
        tA, tM, tC, tB = (tmp[:, 0:512], tmp[:, 512:1024],
                          tmp[:, 1024:1536], tmp[:, 1536:2048])
        for i in range(2 * n + 1):
            sl = i % 3
            bsl = i % 2
            if i < 2 * n:
                row = base if (i % 2 == 0) else base + 32
                c0 = 512 * (i // 2)
                cp = V("vector", lambda E,
                       o=brow[0:1, 512 * bsl:512 * (bsl + 1)],
                       a=ROWS[row:row + 1, c0:c0 + 512]: E.tensor_copy(o, a),
                       waits=[wt if i == 0 else None,
                              free0 if i == 0 else None,
                              mm[i - 2] if i >= 2 else None], inc="vector")
            else:
                cp = None
            mm[i] = V("tensor", lambda E, o=slots[sl],
                      r=brow[0:1, 512 * bsl:512 * (bsl + 1)]:
                      E.matmul(o, onesr_t[:], r, start=True, stop=True),
                      waits=[cp, ev3[sl]], inc="tensor")
            if i >= 2 and i % 2 == 0:
                c = (i - 2) // 2
                c0 = 512 * c
                s_sl = (2 * c) % 3
                q_sl = (2 * c + 1) % 3
                t1 = V("vector", lambda E, o=tM, a=slots[s_sl]:
                       E.tensor_scalar(o, a, 1.0 / D, None, OP.mult),
                       waits=[mm[i], last], inc="vector")
                t2 = V("vector", lambda E, o=MB[:, c0:c0 + 512], a=slots[s_sl]:
                       E.tensor_scalar(o, a, -1.0 / D, None, OP.mult),
                       waits=[t1], inc="vector")
                ev3[s_sl] = t2
                t3 = V("vector", lambda E, o=tA, a=slots[q_sl]:
                       E.tensor_scalar(o, a, 1.0 / D, None, OP.mult),
                       waits=[t2], inc="vector")
                ev3[q_sl] = t3
                t4 = V("vector", lambda E, o=tC, a=tM, b=tM:
                       E.tensor_tensor(o, a, b, OP.mult), waits=[t3],
                       inc="vector")
                t5 = V("vector", lambda E, o=tA, a=tA, b=tC:
                       E.tensor_tensor(o, a, b, OP.subtract), waits=[t4],
                       inc="vector")
                t6 = V("scalar", lambda E, o=tB, a=tA:
                       E.activation(o, a, AF.Sqrt, bias=eps), waits=[t5],
                       inc="scalar")
                t7 = V("vector", lambda E, o=tM, a=tB:
                       E.reciprocal(o, a), waits=[t6], inc="vector")
                last = V("vector", lambda E, o=RB[:, c0:c0 + 512], a=tM:
                         E.tensor_copy(o, a), waits=[t7], inc="vector")
        return last

    def mm_bcast(rsl, w, dest, wt, slots, brow, free0=None):
        n = w // 512
        ev = [free0, free0, free0]
        mm = [None] * (n + 2)
        last = None
        for i in range(n + 2):
            sl = i % 3
            bsl = i % 2
            if i < n:
                cp = V("vector", lambda E,
                       o=brow[0:1, 512 * bsl:512 * (bsl + 1)],
                       a=rsl(512 * i): E.tensor_copy(o, a),
                       waits=[wt if i == 0 else None,
                              free0 if i == 0 else None,
                              mm[i - 2] if i >= 2 else None, last],
                       inc="vector")
            else:
                cp = None
            mm[i] = V("tensor", lambda E, o=slots[sl],
                      r=brow[0:1, 512 * bsl:512 * (bsl + 1)]:
                      E.matmul(o, onesr_t[:], r, start=True, stop=True),
                      waits=[cp, ev[sl]], inc="tensor")
            if i >= 2:
                j = i - 2
                jsl = j % 3
                last = V("vector", lambda E, o=dest[:, 512 * j:512 * (j + 1)],
                         a=slots[jsl]:
                         E.tensor_copy(o, a), waits=[mm[i], last],
                         inc="vector")
                ev[jsl] = last
        return last

    bc_t = []
    with ExitStack() as ph:
        brow = sb("brow", [1, 1024], F32, ph)
        tmpf = sb("tmpf", [128, 2048], F32, ph)
        psBC = ph.enter_context(nc.psum_tensor("psBC", [128, 1536], F32))
        bsl3 = [psBC[:, 0:512], psBC[:, 512:1024], psBC[:, 1024:1536]]
        b = ln_bcast(0, S, RB1, MB1, ev_t, bsl3, brow, tmpf, LNEPS)
        bc_t.append(b)
        bc_t.append(b)
        b = ln_bcast(64, TOWN, RBq, MBq, ev_t, bsl3, brow, tmpf, LNEPS,
                     free0=b)
        bc_t.append(b)
        bc_t.append(b)
        sc.flush()
    if upto <= 2:
        ln1.close(); att.close(); top.close(); return nc

    with ExitStack() as ph:
        cosK = sb("cosK", [128, S], BF16, ph)
        sinK = sb("sinK", [128, S], BF16, ph)
        wkv_f = sb("wkv_f", [128, 4 * 512], F32, ph)
        wk_b = sb("wk_b", [128, 16 * 512], BF16, ph)
        wv_b = sb("wv_b", [128, 16 * 512], BF16, ph)
        xs_f = sb("xs_f", [128, 2 * 512], F32, ph)
        xns = sb("xns", [128, 2 * 16 * 512], BF16, ph)
        rsc = sb("rsc", [128, 2 * 512], F32, ph)
        vsb = sb("vsb", [128, 512], BF16, ph)
        ksq = sb("ksq", [128, 2 * 512], BF16, ph)
        psE = ph.enter_context(nc.psum_tensor("psE", [128, 512], F32))
        psO = ph.enter_context(nc.psum_tensor("psO", [128, 512], F32))
        psV = ph.enter_context(nc.psum_tensor("psV", [128, 512], F32))
        psT = ph.enter_context(nc.psum_tensor("psT", [128, 128], BF16))
        psN2 = ph.enter_context(nc.psum_tensor("psN2", [1, 2048], F32))

        vt0 = V("vector", lambda E: E.memset(VT[:], 1.0), inc="vector")
        sc.dma("sync", cosK[:], ap["cosT"][:])
        sc.dma("sync", sinK[:], ap["sinT"][:])
        cw2 = None
        for wsrc, wdst_b in ((ap["wk"], wk_b), (ap["wv"], wv_b)):
            for rd in range(4):
                d_t = sc.dma("sync", wkv_f[:],
                             wsrc[128 * rd:128 * (rd + 1), :], waits=[cw2])
                cw2 = V("scalar", lambda E,
                        o=wdst_b[:, 2048 * rd:2048 * (rd + 1)], a=wkv_f[:]:
                        E.copy(o, a), waits=[d_t], inc="scalar")

        xn_cons = [None, None]
        rp_prev = None
        tr_last = None
        mmK_t = None
        rope_t = [None, None]
        vtr_t = None
        dma_x = [None] * (8 * NCH)
        for slp in range(8):
            sl = slp
            xsl = slp % 2
            xn = xns[:, xsl * 8192:(xsl + 1) * 8192]
            c0 = 512 * sl
            last_apply = None
            for ch in range(NCH):
                xr = xs_f[:, (ch % 2) * 512:(ch % 2 + 1) * 512]
                ii = slp * NCH + ch
                d_t = sc.dma("sync", xr,
                             ap["xT"][128 * ch:128 * (ch + 1), c0:c0 + 512],
                             waits=[dma_x[ii - 2] if ii >= 2 else bc_t[1]])
                tt1 = V("vector", lambda E, o=rsc[:, 0:512], a=xr,
                        b=MB1[:, c0:c0 + 512]: E.tensor_tensor(o, a, b, OP.add),
                        waits=[d_t, xn_cons[xsl] if ch == 0 else None,
                               bc_t[1] if slp == 0 and ch == 0 else None,
                               rp_prev if ch == 0 else None,
                               last_apply],
                        inc="vector")
                dma_x[ii] = tt1
                last_apply = V("vector",
                               lambda E, o=xn[:, 512 * ch:512 * (ch + 1)],
                               a=rsc[:, 0:512], b=RB1[:, c0:c0 + 512]:
                               E.tensor_tensor(o, a, b, OP.mult),
                               waits=[tt1], inc="vector")
            rp = None
            for pair in range(2):
                for part, ps in ((0, psE), (1, psO)):
                    for ch in range(NCH):
                        co = 512 * ch + 128 * (pair + 2 * part)
                        tok = V("tensor", lambda E, o=ps[:],
                                l=wk_b[:, co:co + 128],
                                r=xn[:, 512 * ch:512 * (ch + 1)], st_=(ch == 0):
                                E.matmul(o, l, r, start=st_, stop=(ch == 15)),
                                waits=([last_apply, cw2, rp] if ch == 0 else []),
                                inc=("tensor" if ch == 15 else None))
                        if ch == 15:
                            mmK_t = tok
                csl = cosK[:, c0:c0 + 512]
                ssl = sinK[:, c0:c0 + 512]
                ke = Kt[:, pair * S + c0:pair * S + c0 + 512]
                ko = Kt[:, (pair + 2) * S + c0:(pair + 2) * S + c0 + 512]
                k1 = V("vector", lambda E, o=rsc[:, 0:512], a=psE[:], b=csl:
                       E.tensor_tensor(o, a, b, OP.mult), waits=[mmK_t, last_apply],
                       inc="vector")
                k2 = V("vector", lambda E, o=rsc[:, 512:1024], a=psO[:], b=ssl:
                       E.tensor_tensor(o, a, b, OP.mult), inc="vector")
                k3 = V("vector", lambda E, o=ke, a=rsc[:, 0:512], b=rsc[:, 512:1024]:
                       E.tensor_tensor(o, a, b, OP.subtract), waits=[k1, k2],
                       inc="vector")
                k4 = V("vector", lambda E, o=rsc[:, 0:512], a=psE[:], b=ssl:
                       E.tensor_tensor(o, a, b, OP.mult), waits=[k3], inc="vector")
                k5 = V("vector", lambda E, o=rsc[:, 512:1024], a=psO[:], b=csl:
                       E.tensor_tensor(o, a, b, OP.mult), inc="vector")
                rp = V("vector", lambda E, o=ko, a=rsc[:, 0:512],
                       b=rsc[:, 512:1024]: E.tensor_tensor(o, a, b, OP.add),
                       waits=[k4, k5], inc="vector")
            rope_t[xsl] = rp
            rp_prev = rp
            for vt in range(4):
                vslot = vt % 2
                pv = psV[:, 0:512]
                for ch in range(NCH):
                    co = 512 * ch + 128 * vt
                    tok = V("tensor", lambda E, o=pv, l=wv_b[:, co:co + 128],
                            r=xn[:, 512 * ch:512 * (ch + 1)], st_=(ch == 0):
                            E.matmul(o, l, r, start=st_, stop=(ch == 15)),
                            waits=([cw2, rp, vtr_t] if ch == 0 else []),
                            inc=("tensor" if ch == 15 else None))
                    if ch == 15:
                        mmV_t = tok
                        xn_cons[xsl] = tok
                vs = vsb[:, 0:512]
                ev = V("scalar", lambda E, o=vs, a=pv: E.copy(o, a),
                       waits=[mmV_t, tr_last], inc="scalar")
                for kc in range(4):
                    cglob = 4 * sl + kc
                    tr = V("tensor", lambda E, o=psT[:],
                           a=vs[:, 128 * kc:128 * (kc + 1)]:
                           E.transpose(o, a, identb_t[:]),
                           waits=[ev, vt0, vtr_t], inc="tensor")
                    tr_last = tr
                    for half in range(2):
                        kv = 2 * vt + half
                        vtr_t = V("scalar", lambda E,
                                  o=VT[:, cglob * VTW + kv * 65:
                                       cglob * VTW + kv * 65 + 64],
                                  a=psT[:, 64 * half:64 * (half + 1)]:
                                  E.copy(o, a), waits=[tr],
                                  inc="scalar")
        kq_t = None
        for pair in range(2):
            sq_ts = []
            mmn = None
            for g in range(8):
                s1 = V("scalar", lambda E, o=ksq[:, 0:512],
                       a=Kt[:, pair * S + 512 * g:pair * S + 512 * (g + 1)]:
                       E.activation(o, a, AF.Square),
                       waits=[rope_t[0], rope_t[1], mmn] if g == 0 else [mmn],
                       inc="scalar")
                s2 = V("scalar", lambda E, o=ksq[:, 512:1024],
                       a=Kt[:, (pair + 2) * S + 512 * g:(pair + 2) * S + 512 * (g + 1)]:
                       E.activation(o, a, AF.Square), inc="scalar")
                for a_ in range(4):
                    V("tensor", lambda E, o=psN2[0:1, 512 * a_:512 * (a_ + 1)],
                      rr=ksq[:, 0:512], ia=a_, st_=(g == 0):
                      E.matmul(o, ind4_t[:, ia:ia + 1], rr, start=st_,
                               stop=False),
                      waits=([s2, kq_t] if (g == 0 and a_ == 0) else []))
                    mmn = V("tensor", lambda E, o=psN2[0:1, 512 * a_:512 * (a_ + 1)],
                            rr=ksq[:, 512:1024], ia=a_:
                            E.matmul(o, ind4_t[:, ia:ia + 1], rr, start=False,
                                     stop=(g == 7)),
                            inc=("tensor" if (g == 7 and a_ == 3) else None)) or mmn
            for a_ in range(4):
                kv = pair + 2 * a_
                kq_t = V("vector", lambda E, o=kn2[0:1, kv:kv + 1],
                         a=psN2[0:1, 512 * a_:512 * (a_ + 1)]:
                         E.reduce_max(o, a, AX.X),
                         waits=([mmn] if a_ == 0 else []),
                         inc=("vector" if a_ == 3 else None)) or kq_t
        sc.flush()
    ln1.close()
    if upto <= 3:
        att.close(); top.close(); return nc

    Qt = sb("Qt", [128, 16 * QS], BF16, att)
    aot = sb("aot", [128, 16 * QS], BF16, att)
    cosQ_t = sb("cosQ_t", [128, TOWN], BF16, att)
    sinQ_t = sb("sinQ_t", [128, TOWN], BF16, att)
    sc.dma("sync", cosQ_t[:], ap["cosQ"][:])
    sc.dma("sync", sinQ_t[:], ap["sinQ"][:])

    with ExitStack() as ph:
        dscr4 = sb("dscr4", [128, 16], F32, ph)
        xnq = sb("xnq", [128, 16 * 512], BF16, ph)
        wpf = sb("wpf", [128, 16 * 128], F32, ph)
        wpb = sb("wpb", [128, 2 * 2048], BF16, ph)
        xqf = sb("xqf", [128, 2 * 512], F32, ph)
        qsq = sb("qsq", [128, 2 * 512], BF16, ph)
        PT = sb("PT", [128, 2 * 2048], BF16, ph)
        SCR = sb("SCR", [128, 2048], F32, ph)
        mskt = sb("mskt", [128, NMASK * 512], BF16, ph)
        x1t = sb("x1t", [128, 2 * 512], F32, ph)
        x1b = sb("x1b", [128, 512], BF16, ph)
        x1s = sb("x1s", [128, 512], BF16, ph)
        rtmp = sb("rtmp", [1, 2048], F32, ph)
        psA = ph.enter_context(nc.psum_tensor("psA", [128, 2048], F32))
        psB = ph.enter_context(nc.psum_tensor("psB", [128, 2048], F32))

        kq_wait = kq_t
        psA_free = None
        psB_free = None
        wpb_free = [None, None]
        wpf_free = None
        xq_free = [None, None]
        scr_free = None
        x1_free = [None, None]
        pw_free = [None, None]
        msk_free = None
        x1_store_t = None
        for slot in range(NSLOT):
            q0 = QS * slot
            nkc = SPANS[slot] // KCH
            lastq = None
            dq_tok = [None] * (NCH + 1)
            dq_tok[0] = sc.dma("sync", xqf[:, 0:512],
                               ap["xqT"][0:128, q0:q0 + 512],
                               waits=[xq_free[0], bc_t[3]])
            for ch in range(NCH):
                xr = xqf[:, (ch % 2) * 512:(ch % 2 + 1) * 512]
                if ch + 1 < NCH:
                    dq_tok[ch + 1] = sc.dma(
                        "sync",
                        xqf[:, ((ch + 1) % 2) * 512:((ch + 1) % 2 + 1) * 512],
                        ap["xqT"][128 * (ch + 1):128 * (ch + 2), q0:q0 + 512],
                        waits=[xq_free[(ch + 1) % 2]])
                else:
                    dq_tok[ch + 1] = sc.dma("sync", dscr4[:],
                                            ap["xqT"][0:128, 0:16])
                tt1 = V("vector", lambda E, o=SCR[:, 0:512], a=xr,
                        b=MBq[:, q0:q0 + 512]: E.tensor_tensor(o, a, b, OP.add),
                        waits=[dq_tok[ch], dq_tok[ch + 1],
                               scr_free if ch == 0 else None, lastq],
                        inc="vector")
                xq_free[ch % 2] = tt1
                lastq = V("vector", lambda E, o=xnq[:, 512 * ch:512 * (ch + 1)],
                          a=SCR[:, 0:512], b=RBq[:, q0:q0 + 512]:
                          E.tensor_tensor(o, a, b, OP.mult), waits=[tt1],
                          inc="vector")
            rope_done = None
            for j in range(8):
                ws = j % 2
                mmq = [None, None]
                for part, coff, ps in ((0, 128 * j, psA), (1, 1024 + 128 * j, psB)):
                    pid = coff // 128
                    d_t = sc.dma("sync", wpf[:],
                                 ap["wq"][128 * pid:128 * (pid + 1), :],
                                 waits=[wpb_free[ws], wpf_free])
                    cv = V("scalar", lambda E,
                           o=wpb[:, ws * 2048:(ws + 1) * 2048], a=wpf[:]:
                           E.copy(o, a), waits=[d_t], inc="scalar")
                    wpf_free = cv
                    for ch in range(NCH):
                        mmq[part] = V("tensor", lambda E, o=ps[:, 0:512],
                                      l=wpb[:, ws * 2048 + 128 * ch:
                                            ws * 2048 + 128 * (ch + 1)],
                                      r=xnq[:, 512 * ch:512 * (ch + 1)],
                                      st_=(ch == 0):
                                      E.matmul(o, l, r, start=st_, stop=(ch == 15)),
                                      waits=([cv, lastq,
                                              psA_free if part == 0 else psB_free]
                                             if ch == 0 else []),
                                      inc=("tensor" if ch == 15 else None)) \
                            or mmq[part]
                    wpb_free[ws] = mmq[part]
                csl = cosQ_t[:, q0:q0 + 512]
                ssl = sinQ_t[:, q0:q0 + 512]
                qe = Qt[:, 512 * j:512 * (j + 1)]
                qo = Qt[:, 512 * (j + 8):512 * (j + 9)]
                q1 = V("vector", lambda E, o=SCR[:, 0:512], a=psA[:, 0:512], b=csl:
                       E.tensor_tensor(o, a, b, OP.mult), waits=[mmq[1], lastq],
                       inc="vector")
                q2 = V("vector", lambda E, o=SCR[:, 512:1024], a=psB[:, 0:512], b=ssl:
                       E.tensor_tensor(o, a, b, OP.mult), inc="vector")
                q3 = V("vector", lambda E, o=qe, a=SCR[:, 0:512], b=SCR[:, 512:1024]:
                       E.tensor_tensor(o, a, b, OP.subtract), waits=[q1, q2],
                       inc="vector")
                q4 = V("vector", lambda E, o=SCR[:, 0:512], a=psA[:, 0:512], b=ssl:
                       E.tensor_tensor(o, a, b, OP.mult), waits=[q3], inc="vector")
                q5 = V("vector", lambda E, o=SCR[:, 512:1024], a=psB[:, 0:512], b=csl:
                       E.tensor_tensor(o, a, b, OP.mult), inc="vector")
                rope_done = V("vector", lambda E, o=qo, a=SCR[:, 0:512],
                              b=SCR[:, 512:1024]:
                              E.tensor_tensor(o, a, b, OP.add), waits=[q4, q5],
                              inc="vector")
                psA_free = rope_done
                psB_free = rope_done
            scr_free = rope_done
            qmx = None
            mmn = None
            for j in range(8):
                s1 = V("scalar", lambda E, o=qsq[:, 0:512],
                       a=Qt[:, 512 * j:512 * (j + 1)]:
                       E.activation(o, a, AF.Square),
                       waits=[rope_done, mmn] if j == 0 else [mmn],
                       inc="scalar")
                s2 = V("scalar", lambda E, o=qsq[:, 512:1024],
                       a=Qt[:, 512 * (j + 8):512 * (j + 9)]:
                       E.activation(o, a, AF.Square), inc="scalar")
                for a_ in range(4):
                    V("tensor", lambda E, o=psB[0:1, 512 * a_:512 * (a_ + 1)],
                      rr=qsq[:, 0:512], ia=a_:
                      E.matmul(o, ind4_t[:, ia:ia + 1], rr, start=True,
                               stop=False),
                      waits=([s2, psB_free] if a_ == 0 else []))
                    mmn = V("tensor", lambda E,
                            o=psB[0:1, 512 * a_:512 * (a_ + 1)],
                            rr=qsq[:, 512:1024], ia=a_:
                            E.matmul(o, ind4_t[:, ia:ia + 1], rr, start=False,
                                     stop=True),
                            inc=("tensor" if a_ == 3 else None)) or mmn
                for a_ in range(4):
                    h = j + 8 * a_
                    qmx = V("vector", lambda E, o=rtmp[0:1, 32 + h:33 + h],
                            a=psB[0:1, 512 * a_:512 * (a_ + 1)]:
                            E.reduce_max(o, a, AX.X),
                            waits=([mmn] if a_ == 0 else []),
                            inc=("vector" if a_ == 3 else None)) or qmx
                psB_free = qmx
            tr1 = V("vector", lambda E: E.reduce_max(rtmp[0:1, 64:65],
                                                     rtmp[0:1, 32:64], AX.X),
                    waits=[qmx, kq_wait], inc="vector")
            tr2 = V("vector", lambda E: E.reduce_max(rtmp[0:1, 65:66],
                                                     kn2[0:1, :], AX.X),
                    inc="vector")
            t1 = V("vector", lambda E: E.tensor_tensor(
                rtmp[0:1, 66:67], rtmp[0:1, 64:65], rtmp[0:1, 65:66], OP.mult),
                waits=[tr1, tr2], inc="vector")
            t2 = V("scalar", lambda E: E.activation(rtmp[0:1, 67:68],
                                                    rtmp[0:1, 66:67], AF.Sqrt),
                   waits=[t1], inc="scalar")
            t3 = V("scalar", lambda E: E.mul(rtmp[0:1, 68:69], rtmp[0:1, 67:68],
                                             -0.125), waits=[t2], inc="scalar")
            t4 = V("tensor", lambda E: E.matmul(psB[:, 0:1], onesr_t[:],
                                                rtmp[0:1, 68:69], start=True,
                                                stop=True),
                   waits=[t3, psB_free], inc="tensor")
            t5 = V("tensor", lambda E: E.matmul(psB[0:1, 512:1024],
                                                onesr_t[0:1, 0:1],
                                                rtmp[0:1, 0:512], start=True,
                                                stop=True),
                   waits=[t4], inc="tensor")
            t5b = V("tensor", lambda E: E.matmul(psB[0:1, 512:1024],
                                                 onesr_t[0:1, 0:1],
                                                 rtmp[0:1, 0:512], start=True,
                                                 stop=True),
                    waits=[t5], inc="tensor")
            nb_t = V("vector", lambda E: E.tensor_copy(negB[:], psB[:, 0:1]),
                     waits=[t5b], inc="vector")
            psB_free = nb_t
            mt = sc.dma("sync", mskt[:],
                        ap["maskd"][slot * 128:(slot + 1) * 128, :],
                        waits=[msk_free])
            ao_t = None
            exp_b = [psA_free] * 4
            pv_b = [[None] * 4 for _ in range(2)]
            pdep_b = [None] * 4
            psb_b = [nb_t] * 4
            for j in range(8):
                et = j // 4
                pvmm = None
                pend_pv = None
                for c in range(nkc):
                    pslot = c % 2
                    pt = PT[:, pslot * 2048:(pslot + 1) * 2048]
                    qk_b = [None] * 4
                    for a_ in range(4):
                        V("tensor", lambda E,
                          l=Kt[32 * a_:32 * (a_ + 1),
                               et * S + KCH * c:et * S + KCH * (c + 1)],
                          r=Qt[32 * a_:32 * (a_ + 1), 512 * j:512 * (j + 1)],
                          o=psA[:, 512 * a_:512 * (a_ + 1)], tp=(32 * a_, 0):
                          E.matmul(o, l, r, start=True, stop=False,
                                   tile_position=tp),
                          waits=[exp_b[a_]])
                        qk_b[a_] = V("tensor", lambda E,
                                     l=Kt[32 * a_:32 * (a_ + 1),
                                          (et + 2) * S + KCH * c:
                                          (et + 2) * S + KCH * (c + 1)],
                                     r=Qt[32 * a_:32 * (a_ + 1),
                                          512 * (j + 8):512 * (j + 9)],
                                     o=psA[:, 512 * a_:512 * (a_ + 1)],
                                     tp=(32 * a_, 0):
                                     E.matmul(o, l, r, start=False, stop=True,
                                              tile_position=tp), inc="tensor")
                    for a_ in range(4):
                        exp_b[a_] = V("scalar", lambda E,
                                      o=pt[:, 512 * a_:512 * (a_ + 1)],
                                      a=psA[:, 512 * a_:512 * (a_ + 1)]:
                                      E.activation(o, a, AF.Exp, bias=negB[:],
                                                   scale=0.125),
                                      waits=[qk_b[a_],
                                             nb_t if (c == 0 and a_ == 0)
                                             else None,
                                             pv_b[pslot][a_]], inc="scalar")
                    if c >= nkc - NMASK:
                        mi = c - (nkc - NMASK)
                        for a_ in range(4):
                            pdep_b[a_] = V("vector", lambda E,
                                           o=pt[:, 512 * a_:512 * (a_ + 1)],
                                           m=mskt[:, 512 * mi:512 * (mi + 1)]:
                                           E.tensor_tensor(o, o, m, OP.mult),
                                           waits=[exp_b[a_],
                                                  mt if (mi == 0 and a_ == 0)
                                                  else None],
                                           inc="vector")
                    else:
                        for a_ in range(4):
                            pdep_b[a_] = exp_b[a_]
                    def emit_pv(c, pt, pdep, c0=False, cl=False):
                        nonlocal pvmm
                        for a_ in range(4):
                            kv = _kv_of(j, a_)
                            pvmm = V("tensor", lambda E,
                                     l=VT[:, c * VTW + kv * 65:
                                          c * VTW + kv * 65 + 65],
                                     r=pt[:, 512 * a_:512 * (a_ + 1)],
                                     o=psB[0:65, 512 * a_:512 * (a_ + 1)]:
                                     E.matmul(o, l, r, start=c0, stop=cl),
                                     waits=[pdep[a_],
                                            psb_b[a_] if c0 else None],
                                     inc="tensor")
                            pv_b[c % 2][a_] = pvmm
                    if pend_pv is not None:
                        pend_pv()
                    pend_pv = (lambda c=c, pt=pt, pdep=list(pdep_b):
                               emit_pv(c, pt, pdep, c0=(c == 0),
                                       cl=(c == nkc - 1)))
                    if c == nkc - 1:
                        pend_pv()
                        pend_pv = None
                dn = V("scalar", lambda E: E.copy(rtmp[0:1, :],
                                              psB[64:65, :]),
                       waits=[pvmm], inc="scalar")
                bcm_b = [None] * 4
                for g in range(4):
                    bcm_b[g] = V("tensor", lambda E,
                                 o=psA[:, 512 * g:512 * (g + 1)],
                                 rr=rtmp[0:1, 512 * g:512 * (g + 1)]:
                                 E.matmul(o, onesr_t[:], rr, start=True,
                                          stop=True),
                                 waits=[dn if g == 0 else None, exp_b[g]],
                                 inc="tensor")
                rc_p = None
                for g in range(4):
                    rc_g = V("vector", lambda E,
                             o=SCR[:, 512 * g:512 * (g + 1)],
                             a=psA[:, 512 * g:512 * (g + 1)]:
                             E.reciprocal(o, a),
                             waits=[bcm_b[g + 1] if g < 3 else bcm_b[3],
                                    rc_p if g == 3 else None,
                                    scr_free if g == 0 else None],
                             inc="vector")
                    if g == 2:
                        rc_p = rc_g
                    exp_b[g] = rc_g
                    t = 2 * j + (1 if g >= 2 else 0)
                    rh = 64 * (g % 2)
                    ao_t = V("vector", lambda E,
                             o=aot[rh:rh + 64, 512 * t:512 * t + 512],
                             a=psB[0:64, 512 * g:512 * (g + 1)],
                             b=SCR[0:64, 512 * g:512 * (g + 1)]:
                             E.tensor_tensor(o, a, b, OP.mult),
                             waits=[rc_g], inc="vector")
                    psb_b[g] = ao_t
                psA_free = exp_b[3]
                psB_free = ao_t
                scr_free = ao_t
            msk_free = ao_t
            st_done = None
            pend_wo = None

            def emit_wo(oc, ws, pw, mmw_tok, d2, cover, d2n):
                nonlocal st_done, x1_store_t
                xr = xqf[:, ws * 512:(ws + 1) * 512]
                xo = x1t[:, ws * 512:(ws + 1) * 512]
                ad = V("vector", lambda E, o=xo, a=pw, b=xr:
                       E.tensor_tensor(o, a, b, OP.add),
                       waits=[mmw_tok, cover, d2, d2n, x1_free[ws]],
                       inc="vector")
                xq_free[ws] = ad
                pw_free[ws] = ad
                s_t = sc.dma("gpsimd",
                             ap["x1d"][128 * oc:128 * (oc + 1), q0:q0 + 512],
                             xo, waits=[ad])
                x1_store_t = s_t
                cb = V("vector", lambda E, o=x1b[:], a=xo: E.tensor_copy(o, a),
                       waits=[st_done, ad], inc="vector")
                x1_free[ws] = s_t
                sq2 = V("scalar", lambda E, o=x1s[:], a=x1b[:]:
                        E.activation(o, a, AF.Square), waits=[cb],
                        inc="scalar")
                V("tensor", lambda E, o=psB[0:1, 1024:1536], rr=x1b[:],
                  st_=(oc == 0): E.matmul(o, onesc_t[:], rr, start=st_,
                                          stop=(oc == 15)),
                  waits=[sq2])
                st_done = V("tensor", lambda E, o=psB[0:1, 1536:2048],
                            rr=x1s[:], st_=(oc == 0):
                            E.matmul(o, onesc_t[:], rr, start=st_,
                                     stop=(oc == 15)),
                            inc=("tensor" if oc == 15 else None)) or st_done

            pend_ad = None
            for oc in range(NCH):
                ws = oc % 2
                d_t = sc.dma("sync", wpf[:],
                             ap["wo"][128 * oc:128 * (oc + 1), :],
                             waits=[wpb_free[ws], wpf_free])
                cv = V("scalar", lambda E, o=wpb[:, ws * 2048:(ws + 1) * 2048],
                       a=wpf[:]: E.copy(o, a), waits=[d_t], inc="scalar")
                wpf_free = cv
                pw = psA[:, 512 * ws:512 * (ws + 1)]
                mmw = None
                for t in range(16):
                    mmw = V("tensor", lambda E, o=pw,
                            l=wpb[:, ws * 2048 + 128 * t:ws * 2048 + 128 * (t + 1)],
                            r=aot[:, 512 * t:512 * (t + 1)], st_=(t == 0):
                            E.matmul(o, l, r, start=st_, stop=(t == 15)),
                            waits=([cv, ao_t,
                                    psA_free if oc < 2 else pw_free[ws]]
                                   if t == 0 else []),
                            inc=("tensor" if t == 15 else None)) or mmw
                wpb_free[ws] = mmw
                d2 = sc.dma("sync", xqf[:, ws * 512:(ws + 1) * 512],
                            ap["xqT"][128 * oc:128 * (oc + 1), q0:q0 + 512],
                            waits=[xq_free[ws]])
                if pend_wo is not None:
                    pend_wo(mmw, d2)
                pend_wo = (lambda cover, d2n, oc=oc, ws=ws, pw=pw, mmw=mmw,
                           d2=d2: emit_wo(oc, ws, pw, mmw, d2, cover, d2n))
            dumW = V("tensor", lambda E: E.matmul(
                psA[0:1, 1024:1536], onesr_t[0:1, 0:1], rtmp[0:1, 0:512],
                start=True, stop=True),
                waits=[wpb_free[1]], inc="tensor")
            dumWb = V("tensor", lambda E: E.matmul(
                psA[0:1, 1024:1536], onesr_t[0:1, 0:1], rtmp[0:1, 0:512],
                start=True, stop=True),
                waits=[dumW], inc="tensor")
            d2echo = sc.dma("sync", dscr4[:], ap["xqT"][0:128, 0:16])
            pend_wo(dumWb, d2echo)
            pend_wo = None
            dum3 = V("tensor", lambda E: E.matmul(
                psA[0:1, 1024:1536], onesr_t[0:1, 0:1], rtmp[0:1, 0:512],
                start=True, stop=True),
                waits=[st_done, psA_free], inc="tensor")
            dum3b = V("tensor", lambda E: E.matmul(
                psA[0:1, 1024:1536], onesr_t[0:1, 0:1], rtmp[0:1, 0:512],
                start=True, stop=True),
                waits=[dum3], inc="tensor")
            e1 = V("scalar", lambda E, o=ROWS[0:1, q0:q0 + 512],
                   a=psB[0:1, 1024:1536]: E.copy(o, a),
                   waits=[st_done, dum3b])
            e2 = V("scalar", lambda E, o=ROWS[32:33, q0:q0 + 512],
                   a=psB[0:1, 1536:2048]: E.copy(o, a), inc="scalar")
            psB_free = e2
            psA_free = e2
        x1_stores = x1_store_t
        sc.flush()
    att.close()
    if upto <= 4:
        top.close(); return nc

    ffn = ExitStack()
    RB2 = sb("RB2", [128, TOWN], BF16, ffn)
    MB2 = sb("MB2", [128, TOWN], BF16, ffn)
    with ExitStack() as ph:
        brow2 = sb("brow2", [1, 1024], F32, ph)
        tmpf2 = sb("tmpf2", [128, 2048], F32, ph)
        psBC2 = ph.enter_context(nc.psum_tensor("psBC2", [128, 1536], F32))
        bsl3b = [psBC2[:, 0:512], psBC2[:, 512:1024], psBC2[:, 1024:1536]]
        b2 = ln_bcast(0, TOWN, RB2, MB2, psA_free, bsl3b, brow2, tmpf2,
                      LNEPS)
        sc.flush()
    ln2bc = b2
    if upto <= 5:
        ffn.close(); top.close(); return nc

    out_done = None
    with ExitStack() as ph:
        xn2 = sb("xn2", [128, 16 * HB], BF16, ph)
        hbuf = sb("hbuf", [128, NIC * HB], BF16, ph)
        wstu = sb("wstu", [128, NIC * 256], BF16, ph)
        wst = wstu
        wdst = wstu
        byt = sb("byt", [128, 4 * 256], U16, ph)
        sgb = sb("sgb", [128, 4 * 256], F16, ph)
        dqA = sb("dqA", [128, 256], U16, ph)
        dqB = sb("dqB", [128, 256], U16, ph)
        dqU = sb("dqU", [128, 2 * 256], U16, ph)
        sg = sb("sg", [128, 2 * HB], BF16, ph)
        ysq = sb("ysq", [128, 512], BF16, ph)
        rb3 = sb("rb3", [128, HB], F32, ph)
        xff = sb("xff", [128, 2 * HB], F32, ph)
        oscr = sb("oscr", [128, HB], F32, ph)
        rtmp = sb("rtmp3", [1, HB], F32, ph)
        brow3 = sb("brow3", [1, 1024], F32, ph)
        dscr = sb("dscr", [128, 16], F32, ph)
        psG = ph.enter_context(nc.psum_tensor("psG", [128, 2 * 512], F32))
        psD = ph.enter_context(nc.psum_tensor("psD", [128, 4 * 512], F32))
        psR = ph.enter_context(nc.psum_tensor("psR", [1, 1024], F32))

        rb3_ev = None
        byt_free = [None] * 4
        dq_free = None
        down_prev = None
        xf_free = [None, None]
        os_free = [None, None]
        y_free = None
        hb_free = None

        dqp = {"u_lo": None, "u_hi": None}

        def dequant(bt_ap, scb_ap, out_lo, out_hi, w8, wfree):
            a1 = V("vector", lambda E, b=bt_ap: E.tensor_scalar(
                dqA[:], b, 0x07, 9, OP.bitwise_and, OP.logical_shift_left),
                waits=list(w8) + [dq_free, wfree], inc="vector")
            a2 = V("vector", lambda E, b=bt_ap: E.tensor_scalar(
                dqB[:], b, 0x08, 12, OP.bitwise_and, OP.logical_shift_left),
                inc="vector")
            v3 = V("vector", lambda E: E.tensor_tensor(
                dqU[:, 0:256], dqA[:], dqB[:], OP.bitwise_or),
                waits=[a1, a2, dqp["u_lo"]], inc="vector")
            a4 = V("vector", lambda E, b=bt_ap: E.tensor_scalar(
                dqA[:], b, 0x70, 5, OP.bitwise_and, OP.logical_shift_left),
                waits=[v3], inc="vector")
            a5 = V("vector", lambda E, b=bt_ap: E.tensor_scalar(
                dqB[:], b, 0x80, 8, OP.bitwise_and, OP.logical_shift_left),
                inc="vector")
            t = V("vector", lambda E: E.tensor_tensor(
                dqU[:, 256:512], dqA[:], dqB[:], OP.bitwise_or),
                waits=[a4, a5, dqp["u_hi"]], inc="vector")
            v7 = V("vector", lambda E, o=out_lo, s_=scb_ap: E.tensor_tensor(
                o, dqU[:, 0:256].bitcast(F16), s_, OP.mult),
                waits=[v3], inc="vector")
            t2 = V("vector", lambda E, o=out_hi, s_=scb_ap: E.tensor_tensor(
                o, dqU[:, 256:512].bitcast(F16), s_, OP.mult),
                waits=[t], inc="vector")
            dqp["u_lo"] = v7
            dqp["u_hi"] = t2
            return t, t2

        for hf in range(2 if fparts >= 19 else 1):
            t0 = HB * hf
            lastx = None
            d_tok = [None] * (NCH + 1)
            d_tok[0] = sc.dma("sync", xff[:, 0:HB],
                              ap["x1d"][0:128, t0:t0 + HB],
                              waits=[xf_free[0], x1_stores,
                                     ln2bc if hf == 0 else None,
                                     y_free if hf == 1 else None])
            for ch in range(NCH):
                xr = xff[:, (ch % 2) * HB:(ch % 2 + 1) * HB]
                osl = oscr[:, 0:HB]
                if ch + 1 < NCH:
                    d_tok[ch + 1] = sc.dma(
                        "sync", xff[:, ((ch + 1) % 2) * HB:((ch + 1) % 2 + 1) * HB],
                        ap["x1d"][128 * (ch + 1):128 * (ch + 2), t0:t0 + HB],
                        waits=[xf_free[(ch + 1) % 2]])
                else:
                    d_tok[ch + 1] = sc.dma("sync", dscr[:],
                                           ap["x1d"][0:128, 0:16])
                tt1 = V("vector", lambda E, o=osl, a=xr, b=MB2[:, t0:t0 + HB]:
                        E.tensor_tensor(o, a, b, OP.add),
                        waits=[d_tok[ch], d_tok[ch + 1],
                               os_free[0] if ch == 0 else None,
                               lastx],
                        inc="vector")
                xf_free[ch % 2] = tt1
                lastx = V("vector", lambda E, o=xn2[:, HB * ch:HB * (ch + 1)],
                          a=osl, b=RB2[:, t0:t0 + HB]:
                          E.tensor_tensor(o, a, b, OP.mult),
                          waits=[y_free if (hf == 1 and ch == 0) else None,
                                 tt1],
                          inc="vector")
            os_free[0] = lastx
            if fparts <= 1:
                break
            sg_t = [None, None]
            hb_t = None
            psG_free = [rb3_ev, rb3_ev]
            wst_free = [down_prev, down_prev]
            hb_free = down_prev
            pend_ev = None

            def emit_sg(oc, tt_, gsl, pg, grp, nxt):
                sg_t[oc] = V("scalar", lambda E,
                             o=sg[:, HB * oc + 512 * tt_:
                                  HB * oc + 512 * (tt_ + 1)], a=pg:
                             E.activation(o, a, AF.Silu, scale=S14),
                             waits=[grp, nxt], inc="scalar")
                psG_free[gsl] = sg_t[oc]

            def emit_hb(obi, oc, tt_, gsl, pg, grp, nxt):
                nonlocal hb_t
                ic = 2 * obi + oc
                hb_t = V("vector", lambda E,
                         o=hbuf[:, HB * ic + 512 * tt_:
                               HB * ic + 512 * (tt_ + 1)], a=pg,
                         b=sg[:, HB * oc + 512 * tt_:
                              HB * oc + 512 * (tt_ + 1)]:
                         E.tensor_tensor(o, a, b, OP.mult),
                         waits=[grp, nxt, sg_t[oc], hb_free], inc="vector")
                psG_free[gsl] = hb_t
            for obi in range(NOB):
                for gu in range(2):
                    wsl = gu
                    wsp = wst[:, wsl * 4096:(wsl + 1) * 4096]
                    col0 = II * gu + 256 * obi
                    dq_t = None
                    dqj = []
                    for jj in range(8):
                        bsl = jj % 4
                        bt_ap = byt[:, 256 * bsl:256 * (bsl + 1)]
                        scb = sgb[:, 256 * bsl:256 * (bsl + 1)]
                        d1 = sc.dma("sync", bt_ap,
                                    ap["bgu"][128 * jj:128 * (jj + 1),
                                              col0:col0 + 256],
                                    waits=[byt_free[bsl]])
                        d2 = sc.dma("sync", scb[:, :],
                                    ap["sgu"][128 * jj:128 * (jj + 1),
                                              col0:col0 + 256])
                        _, dq_t = dequant(
                            bt_ap, scb,
                            wsp[:, 256 * (2 * jj):256 * (2 * jj + 1)],
                            wsp[:, 256 * (2 * jj + 1):256 * (2 * jj + 2)],
                            [d1, d2], wst_free[wsl] if jj == 0 else None)
                        byt_free[bsl] = dq_t
                        dqj.append(dq_t)
                    dq_free = dq_t
                    mmg = [None, None]
                    for oc in range(2):
                        for tt_ in range(2):
                            gsl = tt_
                            pg = psG[:, 512 * gsl:512 * (gsl + 1)]
                            for chh in range(16):
                                w0 = []
                                if chh == 0:
                                    w0 = [dqj[0], lastx, psG_free[gsl]]
                                elif chh % 2 == 0:
                                    w0 = [dqj[chh // 2]]
                                mmg[gsl] = V("tensor", lambda E, o=pg,
                                             l=wsp[:, 256 * chh + 128 * oc:
                                                   256 * chh + 128 * oc + 128],
                                             r=xn2[:, HB * chh + 512 * tt_:
                                                   HB * chh + 512 * (tt_ + 1)],
                                             st_=(chh == 0):
                                             E.matmul(o, l, r, start=st_,
                                                      stop=(chh == 15)),
                                             waits=w0,
                                             inc=("tensor" if chh == 15
                                                  else None)) or mmg[gsl]
                            if pend_ev is not None:
                                pend_ev(mmg[gsl])
                            grp = mmg[gsl]
                            if gu == 0:
                                pend_ev = (lambda nxt, oc=oc, tt_=tt_,
                                           gsl=gsl, pg=pg, grp=grp:
                                           emit_sg(oc, tt_, gsl, pg, grp,
                                                   nxt))
                            else:
                                pend_ev = (lambda nxt, obi=obi, oc=oc,
                                           tt_=tt_, gsl=gsl, pg=pg, grp=grp:
                                           emit_hb(obi, oc, tt_, gsl, pg,
                                                   grp, nxt))
                    wst_free[wsl] = mmg[0] if mmg[1] is None else mmg[1]
            dum = V("tensor", lambda E: E.matmul(
                psG[0:1, 0:512], onesr_t[0:1, 0:1], brow3[0:1, 0:512],
                start=True, stop=True),
                waits=[wst_free[1], psG_free[0]], inc="tensor")
            dum_b = V("tensor", lambda E: E.matmul(
                psG[0:1, 0:512], onesr_t[0:1, 0:1], brow3[0:1, 0:512],
                start=True, stop=True),
                waits=[dum], inc="tensor")
            pend_ev(dum_b)
            pend_ev = None
            if fparts <= 2:
                break
            n_hg = (fparts - 10) if 10 < fparts < 19 else (1 if fparts >= 30 else 8)
            psD_free = [None] * 4
            rms_t = None
            y_t = None
            wdst_free = [hb_t, hb_t]
            for hg in range(n_hg):
                hc0 = 256 * hg
                wd = wdst[:, 0:NIC * 256]
                dq_t = None
                dqj_dn = []
                for ip in range(NOB):
                    bsl = ip % 4
                    bt_ap = byt[:, 256 * bsl:256 * (bsl + 1)]
                    scb = sgb[:, 256 * bsl:256 * (bsl + 1)]
                    d1 = sc.dma("sync", bt_ap,
                           ap["bdn"][128 * ip:128 * (ip + 1), hc0:hc0 + 256],
                           waits=[byt_free[bsl]])
                    d2 = sc.dma("sync", scb[:, :],
                                ap["sdn"][128 * ip:128 * (ip + 1),
                                          hc0:hc0 + 256])
                    _, dq_t = dequant(
                        bt_ap, scb,
                        wd[:, 256 * (2 * ip):256 * (2 * ip + 1)],
                        wd[:, 256 * (2 * ip + 1):256 * (2 * ip + 2)],
                        [d1, d2], wdst_free[0] if ip == 0 else None)
                    byt_free[bsl] = dq_t
                    dqj_dn.append(dq_t)
                dq_free = dq_t
                mmd = None
                for ic in range(NIC):
                    for hhc in range(2):
                        for tt_ in range(2):
                            bidx = 2 * hhc + tt_
                            pd = psD[:, 512 * bidx:512 * (bidx + 1)]
                            w0 = []
                            if ic == 0:
                                w0 = [dqj_dn[0], hb_t]
                                if psD_free[bidx] is not None:
                                    w0 += psD_free[bidx]
                            elif ic % 2 == 0 and hhc == 0 and tt_ == 0:
                                w0 = [dqj_dn[ic // 2]]
                            mmd = V("tensor", lambda E, o=pd,
                                    l=wd[:, 256 * ic + 128 * hhc:
                                         256 * ic + 128 * hhc + 128],
                                    r=hbuf[:, HB * ic + 512 * tt_:
                                           HB * ic + 512 * (tt_ + 1)],
                                    st_=(ic == 0):
                                    E.matmul(o, l, r, start=st_,
                                             stop=(ic == NIC - 1)),
                                    waits=w0,
                                    inc=("tensor" if (ic == NIC - 1)
                                         else None)) or mmd
                wdst_free[0] = mmd
                for hhc in range(2):
                    hh = 2 * hg + hhc
                    for tt_ in range(2):
                        bidx = 2 * hhc + tt_
                        pd = psD[:, 512 * bidx:512 * (bidx + 1)]
                        y_t = V("vector", lambda E,
                                o=xn2[:, HB * hh + 512 * tt_:
                                      HB * hh + 512 * (tt_ + 1)],
                                a=pd: E.tensor_copy(o, a),
                                waits=[mmd, lastx if hg == 0 else None]
                                if (hhc == 0 and tt_ == 0) else [mmd],
                                inc="vector")
                        sqy = V("scalar", lambda E, o=ysq[:],
                                a=xn2[:, HB * hh + 512 * tt_:
                                      HB * hh + 512 * (tt_ + 1)]:
                                E.activation(o, a, AF.Square, scale=S28),
                                waits=[y_t, rms_t], inc="scalar")
                        rms_t = V("tensor", lambda E,
                                  o=psR[0:1, 512 * tt_:512 * (tt_ + 1)],
                                  st_=(hg == 0 and hhc == 0):
                                  E.matmul(o, onesc_t[:], ysq[:], start=st_,
                                           stop=(hg == n_hg - 1 and hhc == 1)),
                                  waits=[sqy], inc="tensor")
                        psD_free[bidx] = [y_t, y_t]
            if fparts <= 3:
                break
            dum2 = V("tensor", lambda E: E.matmul(
                psG[0:1, 0:512], onesr_t[0:1, 0:1], brow3[0:1, 0:512],
                start=True, stop=True),
                waits=[rms_t], inc="tensor")
            dum2b = V("tensor", lambda E: E.matmul(
                psG[0:1, 0:512], onesr_t[0:1, 0:1], brow3[0:1, 0:512],
                start=True, stop=True),
                waits=[dum2], inc="tensor")
            ev = V("scalar", lambda E: E.copy(rtmp[0:1, :], psR[0:1, :]),
                   waits=[rms_t, dum2b], inc="scalar")
            b4r = mm_bcast(lambda c0: rtmp[0:1, c0:c0 + 512], HB, oscr, ev,
                           [psG[:, 0:512], psG[:, 512:1024], psD[:, 0:512]],
                           brow3, free0=hb_t)
            b4 = None
            for h2 in range(2):
                tsq = V("scalar", lambda E, o=rb3[:, 512 * h2:512 * (h2 + 1)],
                        a=oscr[:, 512 * h2:512 * (h2 + 1)]:
                        E.activation(o, a, AF.Sqrt, scale=1.0 / D,
                                     bias=RMSEPS),
                        waits=[b4r, os_free[0] if h2 == 0 else None],
                        inc="scalar")
                trc = V("vector", lambda E, o=oscr[:, 512 * h2:512 * (h2 + 1)],
                        a=rb3[:, 512 * h2:512 * (h2 + 1)]:
                        E.reciprocal(o, a), waits=[tsq], inc="vector")
                b4 = V("vector", lambda E, o=rb3[:, 512 * h2:512 * (h2 + 1)],
                       a=oscr[:, 512 * h2:512 * (h2 + 1)]:
                       E.tensor_scalar(o, a, S28, None, OP.mult),
                       waits=[trc, b4], inc="vector")
            rb3_ev = b4
            fin = None
            d_tok = [None] * (NCH + 1)
            d_tok[0] = sc.dma("sync", xff[:, 0:HB],
                              ap["x1d"][0:128, t0:t0 + HB],
                              waits=[xf_free[0]])
            for ch in range(NCH):
                xr = xff[:, (ch % 2) * HB:(ch % 2 + 1) * HB]
                if ch + 1 < NCH:
                    d_tok[ch + 1] = sc.dma(
                        "sync", xff[:, ((ch + 1) % 2) * HB:((ch + 1) % 2 + 1) * HB],
                        ap["x1d"][128 * (ch + 1):128 * (ch + 2), t0:t0 + HB],
                        waits=[xf_free[(ch + 1) % 2]])
                else:
                    d_tok[ch + 1] = sc.dma("sync", dscr[:],
                                           ap["x1d"][0:128, 0:16])
                osl = oscr[:, 0:HB]
                m1 = V("vector", lambda E, o=osl,
                       a=xn2[:, HB * ch:HB * (ch + 1)], b=rb3[:]:
                       E.tensor_tensor(o, a, b, OP.mult),
                       waits=[b4, y_t if ch == 0 else None,
                              os_free[0]], inc="vector")
                a1 = V("vector", lambda E, o=osl, a=osl, b=xr:
                       E.tensor_tensor(o, a, b, OP.add),
                       waits=[d_tok[ch], d_tok[ch + 1], m1], inc="vector")
                xf_free[ch % 2] = a1
                s_t = sc.dma("gpsimd",
                             ap["outT"][128 * ch:128 * (ch + 1), t0:t0 + HB],
                             osl, waits=[a1])
                os_free[0] = s_t
                fin = s_t
            y_free = fin
            down_prev = wdst_free[0]
            out_done = fin
        if out_done is not None:
            V("gpsimd", lambda E: E.memset(dqA[0:1, 0:1], 0),
              waits=[out_done])
        sc.flush()
    ffn.close()
    top.close()
    return nc


_PROG = None


def _get_prog():
    global _PROG
    if _PROG is None:
        _PROG = build_program()
    return _PROG


def kernel(L0_attn_x, L0_attn_Wq, L0_attn_Wk, L0_attn_Wv, L0_attn_Wo,
           L0_ffn_gate_up_packed, L0_ffn_gate_up_scales,
           L0_ffn_down_packed, L0_ffn_down_scales):
    x = np.ascontiguousarray(np.asarray(L0_attn_x, dtype=np.float32))
    Wq = np.asarray(L0_attn_Wq, dtype=np.float32)
    Wk = np.asarray(L0_attn_Wk, dtype=np.float32)
    Wv = np.asarray(L0_attn_Wv, dtype=np.float32)
    Wo = np.asarray(L0_attn_Wo, dtype=np.float32)
    gup = np.asarray(L0_ffn_gate_up_packed)
    gus = np.asarray(L0_ffn_gate_up_scales)
    dnp_ = np.asarray(L0_ffn_down_packed)
    dns = np.asarray(L0_ffn_down_scales)

    dperm = _perm256(D)
    iperm = _perm256(II)
    qcp = _qcol_perm()
    kcp = _kcol_perm()
    worp = _wo_row_perm()

    def _panelize(w):
        K, N = w.shape
        npan = N // 128
        out = np.empty((npan * 128, K), dtype=w.dtype)
        for j in range(npan):
            out[128 * j:128 * (j + 1)] = (
                w[:, 128 * j:128 * (j + 1)].reshape(K // 128, 128, 128)
                .transpose(1, 0, 2).reshape(128, K))
        return np.ascontiguousarray(out)

    def _rowstage(w):
        out = np.empty((512, 2048), dtype=w.dtype)
        for rd in range(4):
            out[128 * rd:128 * (rd + 1)] = (
                w.reshape(16, 128, 512)[4 * rd:4 * rd + 4]
                .transpose(1, 0, 2).reshape(128, 2048))
        return np.ascontiguousarray(out)

    wq_h = _panelize(np.ascontiguousarray(Wq[dperm][:, qcp]))
    wk_h = _rowstage(np.ascontiguousarray(Wk[dperm][:, kcp]))
    wv_h = _rowstage(np.ascontiguousarray(Wv[dperm]))
    wo_h = _panelize(np.ascontiguousarray(Wo[worp][:, dperm]))
    gub = np.ascontiguousarray(
        gup.view(np.uint8).reshape(2 * II, 1024).T).astype(np.uint16)
    icolp = np.concatenate([iperm, II + iperm])
    gub = np.ascontiguousarray(gub[:, icolp])
    sgu_h = np.ascontiguousarray(
        np.repeat(np.asarray(gus, dtype=np.float16)[:, icolp], 64, axis=0))
    dnb = np.ascontiguousarray(
        dnp_.view(np.uint8).reshape(D, 2816).T).astype(np.uint16)
    dnb = np.ascontiguousarray(dnb[:, dperm])
    sdn_h = np.ascontiguousarray(
        np.repeat(np.asarray(dns, dtype=np.float16)[:, dperm], 64, axis=0))

    half = HD // 2
    invf = 1.0 / 10000.0 ** (np.arange(half, dtype=np.float64) / half)
    pos = np.arange(S, dtype=np.float64)
    ang = np.outer(invf, pos)
    cosf = np.cos(ang).astype(np.float32)
    sinf = np.sin(ang).astype(np.float32)
    cosT_h = np.ascontiguousarray(np.tile(cosf, (4, 1)).astype(bf16))
    sinT_h = np.ascontiguousarray(np.tile(sinf, (4, 1)).astype(bf16))

    identb_h = np.eye(128, dtype=bf16)
    onesr_h = np.ones((1, 128), dtype=np.float32)
    onesc_h = np.ones((128, 1), dtype=bf16)
    ind4_h = np.zeros((128, 4), dtype=np.float32)
    for a in range(4):
        ind4_h[32 * a:32 * (a + 1), a] = 1.0
    ind4_h = ind4_h.astype(bf16)

    in_maps = []
    blocksets = [(0, 2, 4, 6), (1, 3, 5, 7)]
    for core in range(8):
        b = core // 2
        blocks = blocksets[core % 2]
        ownpos = np.concatenate([np.arange(QS * blk, QS * (blk + 1))
                                 for blk in blocks])
        xTb = np.ascontiguousarray(x[b].T[dperm])
        xq = np.ascontiguousarray(xTb[:, ownpos])
        cosQ_h = np.ascontiguousarray(cosT_h[:, ownpos])
        sinQ_h = np.ascontiguousarray(sinT_h[:, ownpos])
        mk = np.zeros((NSLOT, NMASK, 128, QS), dtype=np.float32)
        for s_i, blk in enumerate(blocks):
            nkc = SPANS[s_i] // KCH
            qpos = np.arange(QS * blk, QS * (blk + 1))
            for mi in range(NMASK):
                c = nkc - NMASK + mi
                kpos = np.arange(KCH * c, KCH * (c + 1))
                mk[s_i, mi] = (kpos[:, None] <= qpos[None, :])
        maskd_h = np.ascontiguousarray(
            mk.transpose(0, 2, 1, 3).reshape(NSLOT * 128, NMASK * QS)
            .astype(bf16))
        in_maps.append({
            "xT": xTb, "xqT": xq, "wq": wq_h, "wk": wk_h, "wv": wv_h,
            "wo": wo_h, "bgu": gub, "bdn": dnb, "sgu": sgu_h, "sdn": sdn_h,
            "cosT": cosT_h, "sinT": sinT_h, "cosQ": cosQ_h, "sinQ": sinQ_h,
            "maskd": maskd_h, "identb": identb_h, "onesr": onesr_h,
            "onesc": onesc_h, "ind4": ind4_h,
        })

    nc = _get_prog()
    res = run_bass_kernel_spmd(nc, in_maps, core_ids=list(range(8)))

    out = np.empty((NB, S, D), dtype=np.float32)
    inv = np.empty_like(dperm)
    inv[dperm] = np.arange(D)
    for core in range(8):
        b = core // 2
        blocks = blocksets[core % 2]
        ownpos = np.concatenate([np.arange(QS * blk, QS * (blk + 1))
                                 for blk in blocks])
        oT = res.results[core]["outT"]
        out[b, ownpos, :] = oT[inv].T
    return out
